# version 22
# baseline (speedup 1.0000x reference)
import bisect
from contextlib import ExitStack

import numpy as np
import ml_dtypes

import concourse.bass as bass
import concourse.mybir as mybir
from concourse.bass_utils import run_bass_kernel_spmd

F32 = mybir.dt.float32
BF16 = mybir.dt.bfloat16
AF = mybir.ActivationFunctionType
ALU = mybir.AluOpType
AX = mybir.AxisListType

N_CORES = 8
SEQ = 2048
NH = 32
EPS = 1e-5
SCALE = float(1.0 / np.sqrt(np.float32(128.0)))

SAME_ENGINE_SYNC = True

_BLOCK_ATTR = {"pe": "tensor", "act": "scalar", "dve": "vector",
               "pool": "gpsimd", "sp": "sync"}


class TBuf:
    __slots__ = ("h", "w", "r", "name")

    def __init__(self, h, name=""):
        self.h = h
        self.name = name
        self.w = {}
        self.r = {}


class Chan:
    __slots__ = ("name", "sem", "count")

    def __init__(self, name, sem):
        self.name = name
        self.sem = sem
        self.count = 0


class Prog:
    ENGINES = ("pe", "act", "dve", "pool", "sp")

    def __init__(self, nc, ctx):
        self.nc = nc
        self.ops = {e: [] for e in self.ENGINES}
        self.n = {e: 0 for e in self.ENGINES}
        self.incs = {e: [] for e in self.ENGINES}
        self.inc_flags = {e: set() for e in self.ENGINES}
        self.wm = {e: {} for e in self.ENGINES}
        self.sems = {e: ctx.enter_context(nc.semaphore("s_" + e))
                     for e in self.ENGINES}
        self.chans = {}
        self._ctx = ctx

    def chan(self, name):
        if name not in self.chans:
            sem = self._ctx.enter_context(self.nc.semaphore("c_" + name))
            self.chans[name] = Chan(name, sem)
        return self.chans[name]

    def _inc_value(self, ename, pos):
        lst = self.incs[ename]
        idx = bisect.bisect_left(lst, pos)
        if idx < len(lst):
            return idx + 1
        assert pos <= self.n[ename], (ename, pos, self.n[ename])
        lst.append(pos)
        self.inc_flags[ename].add(pos)
        return len(lst)

    def _need(self, consumer, key, pos):
        if key[0] == "e":
            ename = key[1]
            if ename == consumer and (ename == "pe" or not SAME_ENGINE_SYNC):
                return
            val = self._inc_value(ename, pos)
        else:
            val = 16 * self.chans[key[1]].count
        if self.wm[consumer].get(key, 0) >= val:
            return
        self.ops[consumer].append(("w", key, val))
        self.wm[consumer][key] = val

    def emit(self, ename, build, reads=(), writes=(), chan=None):
        deps = {}
        for b in reads:
            for k, p in b.w.items():
                if deps.get(k, 0) < p:
                    deps[k] = p
        for b in writes:
            for k, p in b.w.items():
                if deps.get(k, 0) < p:
                    deps[k] = p
            for k, p in b.r.items():
                if deps.get(k, 0) < p:
                    deps[k] = p
        for k in sorted(deps, key=repr):
            self._need(ename, k, deps[k])
        if chan is None:
            self.n[ename] += 1
            pos = self.n[ename]
            self.ops[ename].append(("i", build, pos))
            tok_key, tok_pos = ("e", ename), pos
        else:
            c = self.chan(chan)
            c.count += 1
            pos = c.count
            self.ops[ename].append(("d", build, chan))
            tok_key, tok_pos = ("c", chan), pos
        for b in reads:
            if b.r.get(tok_key, 0) < tok_pos:
                b.r[tok_key] = tok_pos
        for b in writes:
            if b.w.get(tok_key, 0) < tok_pos:
                b.w[tok_key] = tok_pos
            b.r = {}
        return (tok_key, tok_pos)

    def barrier(self):
        snap = {("e", e): self.n[e] for e in self.ENGINES if self.n[e] > 0}
        for cname, ch in self.chans.items():
            if ch.count > 0:
                snap[("c", cname)] = ch.count
        for e in self.ENGINES:
            for k in sorted(snap, key=repr):
                if k == ("e", e):
                    continue
                self._need(e, k, snap[k])

    def replay(self):
        nc = self.nc
        with nc.Block() as block:
            for ename in self.ENGINES:
                ops = self.ops[ename]
                if not ops:
                    continue

                def runner(eng, ops=ops, ename=ename):
                    flags = self.inc_flags[ename]
                    sems = self.sems
                    chans = self.chans
                    for op in ops:
                        if op[0] == "w":
                            key = op[1]
                            sem = (sems[key[1]] if key[0] == "e"
                                   else chans[key[1]].sem)
                            eng.wait_ge(sem, op[2])
                        elif op[0] == "i":
                            inst = op[1](eng)
                            if op[2] in flags:
                                inst.then_inc(sems[ename], 1)
                        else:
                            inst = op[1](eng)
                            inst.then_inc(chans[op[2]].sem, 16)

                getattr(block, _BLOCK_ATTR[ename])(runner)


def build_program(seq=SEQ, nh=NH, debug=False):
    ne = nh
    dm = 128 * nh
    sl_w = min(512, seq)
    nsl = seq // sl_w
    st_w = min(512, seq)
    nst = seq // st_w
    SLB = min(256, seq)
    nslab = seq // SLB
    G = 4
    ngrp = SLB // G

    nc = bass.Bass("TRN2", target_bir_lowering=False)

    xt = nc.dram_tensor("xt", [dm, seq], F32, kind="ExternalInput")
    wqh = nc.dram_tensor("wqh", [nh, 128, ne, 128], F32, kind="ExternalInput")
    wkh = nc.dram_tensor("wkh", [nh, 128, ne, 128], F32, kind="ExternalInput")
    wvh = nc.dram_tensor("wvh", [nh, 128, ne, 128], F32, kind="ExternalInput")
    wsh = nc.dram_tensor("wsh", [ne, 128, ne, 128], F32, kind="ExternalInput")
    gwh = nc.dram_tensor("gwh", [128, ne * 8], F32, kind="ExternalInput")
    bqh = nc.dram_tensor("bqh", [128, nh], F32, kind="ExternalInput")
    bkh = nc.dram_tensor("bkh", [128, nh], F32, kind="ExternalInput")
    bvh = nc.dram_tensor("bvh", [128, nh], F32, kind="ExternalInput")
    bfh = nc.dram_tensor("bfh", [128, ne], F32, kind="ExternalInput")
    ics = nc.dram_tensor("ics", [8, dm], F32, kind="ExternalInput")
    ocs = nc.dram_tensor("ocs", [8, dm], F32, kind="ExternalInput")

    nslab_ = seq // min(256, seq)
    SLB_ = min(256, seq)
    q_scr = nc.dram_tensor("q_scr", [nslab_, nh, 128, SLB_], BF16, kind="Internal")
    k_scr = nc.dram_tensor("k_scr", [nslab_, nh, 128, SLB_], BF16, kind="Internal")
    v_scr = nc.dram_tensor("v_scr", [nslab_, nh, 128, SLB_], BF16, kind="Internal")
    c_scr = nc.dram_tensor("c_scr", [nh, 128, seq], BF16, kind="Internal")
    z_scr = nc.dram_tensor("z_scr", [ne, 128, seq], BF16, kind="Internal")
    y1_scr = nc.dram_tensor("y1_scr", [ne, 128, seq], BF16, kind="Internal")
    ws_scr = nc.dram_tensor("ws_scr", [ne, 128, dm], BF16, kind="Internal")

    out = nc.dram_tensor("out", [dm, seq], F32, kind="ExternalOutput")
    _p4 = np.kron(np.eye(4, dtype=np.float32),
                  np.ones((1, nh), dtype=np.float32)) * 32.0
    p4a_dram = nc.inline_tensor(_p4.astype(ml_dtypes.bfloat16), name="p4a_c")
    p4b_dram = nc.inline_tensor(
        np.concatenate([_p4, _p4], axis=1).astype(ml_dtypes.bfloat16),
        name="p4b_c")
    if debug:
        dbg_qkv = nc.dram_tensor("dbg_qkv", [3, nslab_, nh, 128, SLB_], BF16,
                                 kind="ExternalOutput")
        dbg_c = nc.dram_tensor("dbg_c", [nh, 128, seq], BF16,
                               kind="ExternalOutput")
        dbg_y1 = nc.dram_tensor("dbg_y1", [ne, 128, seq], BF16,
                                kind="ExternalOutput")
        dbg_rt = nc.dram_tensor("dbg_rt", [8, seq], BF16,
                                kind="ExternalOutput")

    ctx = ExitStack()
    with ctx:
        P = Prog(nc, ctx)
        E = ctx.enter_context

        ident_bf = TBuf(E(nc.sbuf_tensor("ident_bf", [128, 128], BF16)))
        ones_bf = TBuf(E(nc.sbuf_tensor("ones_bf", [128, 1], BF16)))
        onesrow_f = TBuf(E(nc.sbuf_tensor("onesrow_f", [1, 128], F32)))
        onesrow_bf = TBuf(E(nc.sbuf_tensor("onesrow_bf", [1, 128], BF16)))
        rowpool = TBuf(E(nc.sbuf_tensor("rowpool", [1, 2 * seq], F32)))
        BQ = TBuf(E(nc.sbuf_tensor("BQ", [128, nh], F32)))
        BK = TBuf(E(nc.sbuf_tensor("BK", [128, nh], F32)))
        BV = TBuf(E(nc.sbuf_tensor("BV", [128, nh], F32)))
        BFN = TBuf(E(nc.sbuf_tensor("BFN", [128, ne], F32)))
        GW = TBuf(E(nc.sbuf_tensor("GW", [128, ne * 8], BF16)))
        routing_t = TBuf(E(nc.sbuf_tensor("routing_t", [8, seq], BF16)))
        Bm = TBuf(E(nc.sbuf_tensor("Bm", [128, seq], BF16)))
        Br = TBuf(E(nc.sbuf_tensor("Br", [128, seq], BF16)))

        P.emit("pool", lambda g: g.memset(ident_bf.h[:, :], 0.0),
               writes=[ident_bf])
        P.emit("pool", lambda g: g.affine_select(
            out=ident_bf.h[:, :], in_=ident_bf.h[:, :],
            compare_op=ALU.not_equal, fill=1.0, base=0,
            pattern=[[-1, 128]], channel_multiplier=1),
            reads=[ident_bf], writes=[ident_bf])
        P.emit("pool", lambda g: g.memset(ones_bf.h[:, :], 1.0),
               writes=[ones_bf])
        P.emit("pool", lambda g: g.memset(onesrow_f.h[:, :], 1.0),
               writes=[onesrow_f])
        P.emit("pool", lambda g: g.memset(onesrow_bf.h[:, :], 1.0),
               writes=[onesrow_bf])

        for mi, (buf, srct) in enumerate(
                ((BQ, bqh), (BK, bkh), (BV, bvh), (BFN, bfh))):
            P.emit("sp", lambda s, b=buf, t=srct: s.dma_start(b.h[:, :], t.ap()),
                   writes=[buf], chan=f"msc{mi}")
        P.emit("pool", lambda g: g.dma_start(GW.h[:, :], gwh.ap()),
               writes=[GW], chan="mgw")

        zfill_tb = TBuf(None, "zfill")
        ctxA0 = ExitStack()
        with ctxA0:
            EA = ctxA0.enter_context
            X = [TBuf(EA(nc.sbuf_tensor(f"X{c}", [128, seq], BF16)), f"X{c}")
                 for c in range(ne)]
            for c in range(ne):
                P.emit("pool", lambda g, c=c: g.dma_start(
                    X[c].h[:, :], xt.ap()[c * 128:(c + 1) * 128, :]),
                    writes=[X[c]], chan=f"x{c}")
            for c in range(ne):
                P.emit("sp", lambda s, c=c: s.dma_start(
                    z_scr.ap()[c], X[c].h[:, :]),
                    reads=[X[c]], writes=[zfill_tb], chan="zfill")

            wbufs = [TBuf(EA(nc.sbuf_tensor(f"W{i}", [128, dm], BF16)))
                     for i in range(2)]
            stage = [TBuf(EA(nc.sbuf_tensor(f"pst{i}", [128, seq], BF16)))
                     for i in range(3)]
            wsbufs = [TBuf(EA(nc.sbuf_tensor(f"wsbuf{i}", [128, dm // 2], BF16)))
                      for i in range(2)]
            proj_ps = [TBuf(EA(nc.psum_tensor(f"proj_ps{i}", [128, 512], F32)))
                       for i in range(2)]

            ws_tasks = [(h, hf) for h in range(ne) for hf in range(2)]

            def emit_ws_load(k):
                h, hf = ws_tasks[k]
                b = wsbufs[k % 2]
                P.emit("pool", lambda g, h=h, hf=hf, b=b: g.dma_start(
                    b.h[:, :],
                    wsh.ap()[h][:, hf * (ne // 2):(hf + 1) * (ne // 2), :]),
                    writes=[b], chan="ws_in")

            emit_ws_load(0)
            emit_ws_load(1)
            ws_k = 0

            wsrcs = (wqh, wkh, wvh)
            wdsts = (q_scr, k_scr, v_scr)
            wbias = (BQ, BK, BV)
            wload = [(h, wi) for h in range(nh) for wi in range(3)]

            def emit_wload(j):
                h, wi = wload[j]
                buf = wbufs[j % 2]
                P.emit("pool", lambda g, h=h, wi=wi, b=buf: g.dma_start(
                    b.h[:, :], wsrcs[wi].ap()[h]),
                    writes=[buf], chan=f"wl{j % 2}")

            emit_wload(0)
            if len(wload) > 1:
                emit_wload(1)
            next_load = 2

            for j, (h, wi) in enumerate(wload):
                wb = wbufs[j % 2]
                st = stage[j % 3]
                for s4 in range(nsl):
                    pp = proj_ps[(j * nsl + s4) % 2]
                    for e in range(ne):
                        P.emit("pe", lambda t, wb=wb, e=e, s4=s4, pp=pp: t.matmul(
                            pp.h[:, 0:sl_w],
                            wb.h[:, e * 128:(e + 1) * 128],
                            X[e].h[:, s4 * sl_w:(s4 + 1) * sl_w],
                            start=(e == 0), stop=(e == ne - 1)),
                            reads=[wb, X[e]], writes=[pp])
                    P.emit("act", lambda sc, st=st, s4=s4, wi=wi, h=h, pp=pp:
                           sc.activation(
                               st.h[:, s4 * sl_w:(s4 + 1) * sl_w],
                               pp.h[:, 0:sl_w], AF.Identity,
                               bias=wbias[wi].h[:, h:h + 1]),
                           reads=[pp, wbias[wi]], writes=[st])
                P.emit("sp", lambda s, st=st, wi=wi, h=h: s.dma_start(
                    wdsts[wi].ap()[:, h, :, :].transpose([1, 0, 2]),
                    st.h[:, :].rearrange("d (sl s) -> d sl s", s=SLB)),
                    reads=[st], chan="qkv_out")
                if next_load < len(wload):
                    emit_wload(next_load)
                    next_load += 1
                if wi == 0 and h < ne:
                    for _ in range(2):
                        wh, whf = ws_tasks[ws_k]
                        b = wsbufs[ws_k % 2]
                        P.emit("act", lambda sc, b=b: sc.sign(
                            b.h[:, :], b.h[:, :]),
                            reads=[b], writes=[b])
                        P.emit("sp", lambda s, wh=wh, whf=whf, b=b: s.dma_start(
                            ws_scr.ap()[wh][:, whf * (dm // 2):(whf + 1) * (dm // 2)],
                            b.h[:, :]),
                            reads=[b], chan="ws_out")
                        if ws_k + 2 < len(ws_tasks):
                            emit_ws_load(ws_k + 2)
                        ws_k += 1
            if debug:
                P.barrier()
                for wi in range(3):
                    P.emit("sp", lambda s, wi=wi: s.dma_start(
                        dbg_qkv.ap()[wi], wdsts[wi].ap()), chan="dbg")

        P.barrier()
        ctxA1 = ExitStack()
        with ctxA1:
            EA = ctxA1.enter_context
            Qs = [TBuf(EA(nc.sbuf_tensor(f"Qs{i}", [128, ne * SLB], BF16)))
                  for i in range(2)]
            Ks = [TBuf(EA(nc.sbuf_tensor(f"Ks{i}", [128, ne * SLB], BF16)))
                  for i in range(2)]
            Vs = [TBuf(EA(nc.sbuf_tensor(f"Vs{i}", [128, ne * SLB], BF16)))
                  for i in range(2)]
            SRW = [TBuf(EA(nc.sbuf_tensor(f"SRW{p}", [128, ne * SLB], BF16)))
                   for p in range(2)]
            gdim = G * ne
            CTXF_ts = [EA(nc.sbuf_tensor(f"CTXF{i}", [128, ne * SLB], BF16))
                       for i in range(2)]
            CTXFbs = [[TBuf(CTXF_ts[i], f"ctxf{i}A"), TBuf(CTXF_ts[i], f"ctxf{i}B")]
                      for i in range(2)]
            U4 = [TBuf(EA(nc.sbuf_tensor(f"U4_{i}", [gdim, 2 * gdim], BF16)))
                  for i in range(3)]
            Vts4 = [TBuf(EA(nc.sbuf_tensor(f"Vts4_{i}", [gdim, 256], BF16)))
                    for i in range(3)]
            ctxn = [TBuf(EA(nc.sbuf_tensor(f"ctxn{i}", [gdim, 256], BF16)))
                    for i in range(3)]
            rden = [TBuf(EA(nc.sbuf_tensor(f"rden{i}", [gdim, 2], F32)))
                    for i in range(2)]
            negrow = TBuf(EA(nc.sbuf_tensor("negrow", [1, gdim], BF16)))
            ones2 = TBuf(EA(nc.sbuf_tensor("ones2", [1, 2 * gdim], BF16)))
            p4a = TBuf(EA(nc.sbuf_tensor("p4a", [4, gdim], BF16)))
            p4b = TBuf(EA(nc.sbuf_tensor("p4b", [4, 2 * gdim], BF16)))

            gram_ps = [TBuf(EA(nc.psum_tensor(f"gram{i}", [gdim, 2 * gdim], F32)))
                       for i in range(2)]
            vts_ps = TBuf(EA(nc.psum_tensor("vtsp0", [gdim, 256], BF16)))
            ctxT_ps = [TBuf(EA(nc.psum_tensor(f"ctxT{i}", [gdim, 256], F32)))
                       for i in range(2)]
            den_ps = TBuf(EA(nc.psum_tensor("denp", [gdim, 2], F32)))
            ctf_ps = [TBuf(EA(nc.psum_tensor(f"ctf{i}", [128, 2 * gdim], BF16)))
                      for i in range(2)]
            P.emit("pool", lambda g: g.memset(negrow.h[:, :], -1024.0),
                   writes=[negrow])
            P.emit("pool", lambda g: g.memset(ones2.h[:, :], 1.0),
                   writes=[ones2])
            P.emit("sp", lambda s: s.dma_start(p4a.h[:, :], p4a_dram.ap()),
                   writes=[p4a], chan="mmask")
            P.emit("sp", lambda s: s.dma_start(p4b.h[:, :], p4b_dram.ap()),
                   writes=[p4b], chan="mmask")
            def load_one(sl, which):
                buf = SRW[0] if which in ("q", "v") else SRW[1]
                scr = {"q": q_scr, "k": k_scr, "v": v_scr}[which]
                P.emit("sp", lambda s, b=buf, scr=scr, sl=sl: s.dma_start(
                    b.h[:, :].rearrange("d (t s) -> d t s", s=SLB),
                    scr.ap()[sl].transpose([1, 0, 2])),
                    writes=[buf], chan=which + "sl" + str(sl % 2))

            def rearr_one(sl, which):
                stg = SRW[0] if which in ("q", "v") else SRW[1]
                dst = {"q": Qs, "k": Ks, "v": Vs}[which][sl % 2]
                P.emit("dve", lambda v, b=dst, stg=stg: v.tensor_copy(
                    b.h[:, :].rearrange("d (s t) -> d s t", t=ne),
                    stg.h[:, :].rearrange("d (t s) -> d s t", s=SLB)),
                    reads=[stg], writes=[dst])

            for w in ("q", "k"):
                load_one(0, w)
            for w in ("q", "k"):
                rearr_one(0, w)
            load_one(0, "v")
            rearr_one(0, "v")
            for sl in range(nslab):
                s0 = sl * SLB
                qb_, kb_, vb_ = Qs[sl % 2], Ks[sl % 2], Vs[sl % 2]
                CTXF_t = CTXF_ts[sl % 2]
                CTXFb = CTXFbs[sl % 2]

                def gview(b, g):
                    return b.h[:, g * G * ne:(g + 1) * G * ne]

                def rview(b, g):
                    return (b.h[:, :].rearrange("d (t s) -> d s t", s=SLB)
                            [:, g * G:(g + 1) * G, :])

                nmac = ngrp // 2

                def stage1(m):
                    gp = gram_ps[m % 2]
                    P.emit("pe", lambda t, gp=gp: t.matmul(
                        gp.h[:, :], negrow.h[:, :], ones2.h[:, :],
                        start=True, stop=False),
                        reads=[negrow, ones2], writes=[gp])
                    P.emit("pe", lambda t, gp=gp: t.matmul(
                        gp.h[:, :], p4a.h[:, :], p4b.h[:, :],
                        start=False, stop=False),
                        reads=[p4a, p4b], writes=[gp])
                    for k in (0, 1):
                        g = 2 * m + k
                        P.emit("pe", lambda t, g=g, k=k, vb_=vb_: t.transpose(
                            vts_ps.h[:, k * 128:(k + 1) * 128],
                            gview(vb_, g), ident_bf.h[:, :]),
                            reads=[vb_, ident_bf], writes=[vts_ps])
                        P.emit("pe", lambda t, g=g, k=k, gp=gp, kb_=kb_,
                               qb_=qb_: t.matmul(
                                   gp.h[:, k * gdim:(k + 1) * gdim],
                                   gview(kb_, g), gview(qb_, g),
                                   start=False, stop=(k == 1)),
                               reads=[kb_, qb_], writes=[gp])
                    ub = U4[m % 3]
                    P.emit("act", lambda sc, ub=ub, gp=gp: sc.activation(
                        ub.h[:, :], gp.h[:, :], AF.Exp, scale=SCALE),
                        reads=[gp], writes=[ub])
                    vb4 = Vts4[m % 3]
                    P.emit("dve", lambda v, vb4=vb4: v.tensor_copy(
                        vb4.h[:, :], vts_ps.h[:, :]),
                        reads=[vts_ps], writes=[vb4])

                def stage2(p):
                    um, vb4 = U4[p % 3], Vts4[p % 3]
                    ctp = ctxT_ps[p % 2]
                    for k in (0, 1):
                        P.emit("pe", lambda t, k=k, um=um, vb4=vb4, ctp=ctp:
                               t.matmul(
                                   ctp.h[:, k * 128:(k + 1) * 128],
                                   um.h[:, k * gdim:(k + 1) * gdim],
                                   vb4.h[:, k * 128:(k + 1) * 128],
                                   start=True, stop=True),
                               reads=[um, vb4], writes=[ctp])
                        P.emit("pe", lambda t, k=k, um=um: t.matmul(
                            den_ps.h[:, k:k + 1],
                            um.h[:, k * gdim:(k + 1) * gdim],
                            ones_bf.h[0:gdim, 0:1],
                            start=True, stop=True),
                            reads=[um, ones_bf], writes=[den_ps])
                    rd = rden[p % 2]
                    P.emit("dve", lambda v, rd=rd: v.reciprocal(
                        rd.h[:, :], den_ps.h[:, :]),
                        reads=[den_ps], writes=[rd])
                    cn_ = ctxn[p % 3]
                    for k in (0, 1):
                        P.emit("dve", lambda v, k=k, cn_=cn_, ctp=ctp, rd=rd:
                               v.tensor_scalar_mul(
                                   cn_.h[:, k * 128:(k + 1) * 128],
                                   ctp.h[:, k * 128:(k + 1) * 128],
                                   rd.h[:, k:k + 1]),
                               reads=[ctp, rd], writes=[cn_])

                def stage3(p):
                    cn_ = ctxn[p % 3]
                    cf = ctf_ps[p % 2]
                    for k in (0, 1):
                        P.emit("pe", lambda t, k=k, cn_=cn_, cf=cf: t.transpose(
                            cf.h[:, k * gdim:(k + 1) * gdim],
                            cn_.h[:, k * 128:(k + 1) * 128],
                            ident_bf.h[0:gdim, 0:gdim]),
                            reads=[cn_, ident_bf], writes=[cf])
                    ctb = CTXFb[p % 2]
                    P.emit("act", lambda sc, p=p, cf=cf, CTXF_t=CTXF_t:
                        sc.activation(
                        CTXF_t[:, :].rearrange("d (h s) -> d s h", s=SLB)
                        [:, p * 2 * G:(p + 1) * 2 * G, :],
                        cf.h[:, :].rearrange("d (s h) -> d s h", h=ne),
                        AF.Copy),
                        reads=[cf], writes=[ctb])

                for m in range(nmac + 4):
                    if m < nmac:
                        stage1(m)
                    if 2 <= m < nmac + 2:
                        stage2(m - 2)
                    if 4 <= m:
                        stage3(m - 4)
                    if sl + 1 < nslab:
                        if m == 0:
                            load_one(sl + 1, "q")
                            load_one(sl + 1, "k")
                        if m == min(8, nmac - 4):
                            rearr_one(sl + 1, "q")
                            rearr_one(sl + 1, "k")
                        if m == min(9, nmac - 3):
                            load_one(sl + 1, "v")
                        if m == min(16, nmac - 2):
                            rearr_one(sl + 1, "v")

                for h in range(ne):
                    P.emit("pool", lambda g, h=h, s0=s0, CTXF_t=CTXF_t:
                        g.dma_start(
                            z_scr.ap()[h][:, s0:s0 + SLB],
                            CTXF_t[:, h * SLB:(h + 1) * SLB],
                            accum_op=ALU.add),
                        reads=[CTXFb[0], CTXFb[1], zfill_tb], chan="c_out")

        P.barrier()
        ctxB = ExitStack()
        with ctxB:
            EB = ctxB.enter_context
            Z = [TBuf(EB(nc.sbuf_tensor(f"Z{c}", [128, seq], BF16)))
                 for c in range(ne)]
            sqb = [TBuf(EB(nc.sbuf_tensor(f"sqb{i}", [128, seq], BF16)))
                   for i in range(2)]
            tnb = [TBuf(EB(nc.sbuf_tensor(f"tnb{i}", [128, seq], BF16)))
                   for i in range(2)]

            ctxB1 = ExitStack()
            with ctxB1:
                sum1 = TBuf(ctxB1.enter_context(
                    nc.psum_tensor("sum1", [1, seq], F32)))
                sum2 = TBuf(ctxB1.enter_context(
                    nc.psum_tensor("sum2", [1, seq], F32)))
                for c in range(ne):
                    P.emit("sp", lambda s, c=c: s.dma_start(
                        Z[c].h[:, :], z_scr.ap()[c]),
                        writes=[Z[c]], chan=f"zin{c % 4}")
                    sq = sqb[c % 2]
                    P.emit("act", lambda sc, c=c, sq=sq: sc.activation(
                        sq.h[:, :], Z[c].h[:, :], AF.Square),
                        reads=[Z[c]], writes=[sq])
                    for s4 in range(nsl):
                        P.emit("pe", lambda t, c=c, s4=s4: t.matmul(
                            sum1.h[0:1, s4 * sl_w:(s4 + 1) * sl_w],
                            ones_bf.h[:, :],
                            Z[c].h[:, s4 * sl_w:(s4 + 1) * sl_w],
                            start=(c == 0), stop=(c == ne - 1)),
                            reads=[ones_bf, Z[c]], writes=[sum1])
                        P.emit("pe", lambda t, sq=sq, s4=s4, c=c: t.matmul(
                            sum2.h[0:1, s4 * sl_w:(s4 + 1) * sl_w],
                            ones_bf.h[:, :],
                            sq.h[:, s4 * sl_w:(s4 + 1) * sl_w],
                            start=(c == 0), stop=(c == ne - 1)),
                            reads=[ones_bf, sq], writes=[sum2])
                rA = rowpool.h[0:1, 0:seq]
                rB = rowpool.h[0:1, seq:2 * seq]
                P.emit("dve", lambda v, a=rA: v.tensor_scalar_mul(
                    a, sum1.h[0:1, :], 1.0 / dm),
                    reads=[sum1], writes=[rowpool])
                P.emit("dve", lambda v, a=rA, b=rB: v.tensor_mul(b, a, a),
                       reads=[rowpool], writes=[rowpool])
                P.emit("dve", lambda v, b=rB: v.tensor_scalar_add(b, b, -EPS),
                       reads=[rowpool], writes=[rowpool])
                P.emit("dve", lambda v, b=rB: v.scalar_tensor_tensor(
                    b, sum2.h[0:1, :], 1.0 / dm, b, ALU.mult, ALU.subtract),
                    reads=[sum2, rowpool], writes=[rowpool])
                P.emit("act", lambda sc, b=rB: sc.activation(b, b, AF.Ln),
                       reads=[rowpool], writes=[rowpool])
                P.emit("act", lambda sc, b=rB: sc.activation(
                    b, b, AF.Exp, scale=-0.5),
                    reads=[rowpool], writes=[rowpool])
            P.barrier()

            ctxB2 = ExitStack()
            with ctxB2:
                ctxB2p = ExitStack()
                bcB_ps = TBuf(ctxB2p.enter_context(
                    nc.psum_tensor("bcB_ps", [128, seq], F32)))
                lg_ps = TBuf(ctxB2p.enter_context(
                    nc.psum_tensor("lg_ps", [8, seq], F32)))
                for s4 in range(nsl):
                    P.emit("pe", lambda t, s4=s4: t.matmul(
                        bcB_ps.h[:, s4 * sl_w:(s4 + 1) * sl_w],
                        onesrow_f.h[:, :],
                        rowpool.h[0:1, seq + s4 * sl_w:seq + (s4 + 1) * sl_w],
                        start=True, stop=True),
                        reads=[onesrow_f, rowpool], writes=[bcB_ps])
                P.emit("act", lambda sc: sc.activation(
                    Br.h[:, :], bcB_ps.h[:, :], AF.Copy),
                    reads=[bcB_ps], writes=[Br])
                P.emit("dve", lambda v: v.tensor_mul(
                    rowpool.h[0:1, 0:seq], rowpool.h[0:1, 0:seq],
                    rowpool.h[0:1, seq:2 * seq]),
                    reads=[rowpool], writes=[rowpool])
                for s4 in range(nsl):
                    P.emit("pe", lambda t, s4=s4: t.matmul(
                        bcB_ps.h[:, s4 * sl_w:(s4 + 1) * sl_w],
                        onesrow_f.h[:, :],
                        rowpool.h[0:1, s4 * sl_w:(s4 + 1) * sl_w],
                        start=True, stop=True),
                        reads=[onesrow_f, rowpool], writes=[bcB_ps])
                P.emit("act", lambda sc: sc.activation(
                    Bm.h[:, :], bcB_ps.h[:, :], AF.Copy),
                    reads=[bcB_ps], writes=[Bm])
                for c in range(ne):
                    tn = tnb[c % 2]
                    P.emit("dve", lambda v, c=c, tn=tn: v.tensor_mul(
                        tn.h[:, :], Z[c].h[:, :], Br.h[:, :]),
                        reads=[Z[c], Br], writes=[tn])
                    P.emit("dve", lambda v, c=c, tn=tn: v.tensor_sub(
                        Z[c].h[:, :], tn.h[:, :], Bm.h[:, :]),
                        reads=[tn, Bm], writes=[Z[c]])
                    P.emit("sp", lambda s, c=c: s.dma_start(
                        y1_scr.ap()[c], Z[c].h[:, :]),
                        reads=[Z[c]], chan="y1_out")
                    for s4 in range(nsl):
                        P.emit("pe", lambda t, c=c, s4=s4: t.matmul(
                            lg_ps.h[0:8, s4 * sl_w:(s4 + 1) * sl_w],
                            GW.h[:, c * 8:(c + 1) * 8],
                            Z[c].h[:, s4 * sl_w:(s4 + 1) * sl_w],
                            start=(c == 0), stop=(c == ne - 1)),
                            reads=[GW, Z[c]], writes=[lg_ps])
                lg_sb = TBuf(ctxB2.enter_context(
                    nc.sbuf_tensor("lg_sb", [8, seq], BF16)))
                P.emit("act", lambda sc: sc.activation(
                    lg_sb.h[:, :], lg_ps.h[:, :], AF.Copy),
                    reads=[lg_ps], writes=[lg_sb])
                nt = seq // 128
                ltm = TBuf(ctxB2.enter_context(
                    nc.sbuf_tensor("ltm", [128, nt * 8], BF16)))
                prob = TBuf(ctxB2.enter_context(
                    nc.sbuf_tensor("prob", [128, nt * 8], F32)))
                mx = TBuf(ctxB2.enter_context(
                    nc.sbuf_tensor("mx", [128, nt], F32)))
                den = TBuf(ctxB2.enter_context(
                    nc.sbuf_tensor("den", [128, nt], F32)))
                rtm = TBuf(ctxB2.enter_context(
                    nc.sbuf_tensor("rtm", [128, nt * 8], BF16)))
                P.barrier()
                ctxB2p.close()
                ctxB3 = ExitStack()
                with ctxB3:
                    lt_ps = TBuf(ctxB3.enter_context(
                        nc.psum_tensor("lt_ps", [128, nt * 8], BF16)))
                    rt_ps = TBuf(ctxB3.enter_context(
                        nc.psum_tensor("rt_ps", [8, seq], BF16)))
                    for tt in range(nt):
                        P.emit("pe", lambda t, tt=tt: t.transpose(
                            lt_ps.h[:, tt * 8:(tt + 1) * 8],
                            lg_sb.h[0:8, tt * 128:(tt + 1) * 128],
                            ident_bf.h[0:8, 0:8]),
                            reads=[lg_sb, ident_bf], writes=[lt_ps])
                    P.emit("dve", lambda v: v.tensor_copy(
                        ltm.h[:, :], lt_ps.h[:, :]),
                        reads=[lt_ps], writes=[ltm])
                    P.emit("dve", lambda v: v.reduce_max(
                        mx.h[:, :],
                        ltm.h[:, :].rearrange("p (g x) -> p g x", x=8),
                        axis=AX.X),
                        reads=[ltm], writes=[mx])
                    P.emit("dve", lambda v: v.tensor_scalar_mul(
                        mx.h[:, :], mx.h[:, :], -1.0),
                        reads=[mx], writes=[mx])
                    for tt in range(nt):
                        P.emit("act", lambda sc, tt=tt: sc.activation(
                            prob.h[:, tt * 8:(tt + 1) * 8],
                            ltm.h[:, tt * 8:(tt + 1) * 8], AF.Exp,
                            bias=mx.h[:, tt:tt + 1],
                            accum_out=den.h[:, tt:tt + 1]),
                            reads=[ltm, mx], writes=[prob, den])
                    P.emit("dve", lambda v: v.reciprocal(
                        den.h[:, :], den.h[:, :]),
                        reads=[den], writes=[den])
                    for tt in range(nt):
                        P.emit("dve", lambda v, tt=tt: v.tensor_scalar_mul(
                            rtm.h[:, tt * 8:(tt + 1) * 8],
                            prob.h[:, tt * 8:(tt + 1) * 8],
                            den.h[:, tt:tt + 1]),
                            reads=[prob, den], writes=[rtm])
                    for tt in range(nt):
                        P.emit("pe", lambda t, tt=tt: t.transpose(
                            rt_ps.h[0:8, tt * 128:(tt + 1) * 128],
                            rtm.h[:, tt * 8:(tt + 1) * 8],
                            ident_bf.h[:, :]),
                            reads=[rtm, ident_bf], writes=[rt_ps])
                    P.emit("dve", lambda v: v.tensor_copy(
                        routing_t.h[:, :], rt_ps.h[:, :]),
                        reads=[rt_ps], writes=[routing_t])
                    if debug:
                        P.emit("sp", lambda s: s.dma_start(
                            dbg_rt.ap(), routing_t.h[:, :]),
                            reads=[routing_t], chan="dbg")
                        P.emit("sp", lambda s: s.dma_start(
                            dbg_y1.ap(), y1_scr.ap()), chan="dbg")
                        P.emit("sp", lambda s: s.dma_start(
                            dbg_c.ap(), c_scr.ap()), chan="dbg")
                    P.barrier()

        ctxC = ExitStack()
        with ctxC:
            EC = ctxC.enter_context
            ICS = TBuf(EC(nc.sbuf_tensor("ICS", [8, dm], BF16)))
            OCS = TBuf(EC(nc.sbuf_tensor("OCS", [8, dm], BF16)))
            P.emit("pool", lambda g: g.dma_start(ICS.h[:, :], ics.ap()),
                   writes=[ICS], chan="mics")
            P.emit("pool", lambda g: g.dma_start(OCS.h[:, :], ocs.ap()),
                   writes=[OCS], chan="mocs")
            xs = [[TBuf(EC(nc.sbuf_tensor(f"xs{p}_{e}", [128, st_w], BF16)))
                   for e in range(ne)] for p in range(2)]
            z2 = [TBuf(EC(nc.sbuf_tensor(f"z2{d}", [128, st_w], BF16)))
                  for d in range(ne)]
            WS = [TBuf(EC(nc.sbuf_tensor(f"WS{i}", [128, dm], BF16)))
                  for i in range(3)]
            y1a = [TBuf(EC(nc.sbuf_tensor(f"y1a{i}", [128, st_w], BF16)))
                   for i in range(2)]
            y1r = [TBuf(EC(nc.sbuf_tensor(f"y1r{i}", [128, st_w], BF16)))
                   for i in range(2)]
            os_sb = [TBuf(EC(nc.sbuf_tensor(f"os{i}", [128, st_w], BF16)))
                     for i in range(2)]
            tf = [TBuf(EC(nc.sbuf_tensor(f"tf{i}", [128, st_w], BF16)))
                  for i in range(2)]
            fo = [TBuf(EC(nc.sbuf_tensor(f"fo{i}", [128, st_w], BF16)))
                  for i in range(2)]
            sqc = [TBuf(EC(nc.sbuf_tensor(f"sqc{i}", [128, st_w], BF16)))
                   for i in range(3)]
            tn2 = [TBuf(EC(nc.sbuf_tensor(f"tn2{i}", [128, st_w], BF16)))
                   for i in range(2)]
            y2s = [TBuf(EC(nc.sbuf_tensor(f"y2s{i}", [128, st_w], F32)))
                   for i in range(4)]

            out_ps = [TBuf(EC(nc.psum_tensor(f"out_ps{i}", [128, st_w], F32)))
                      for i in range(2)]
            is_ps = [TBuf(EC(nc.psum_tensor(f"is_ps{i}", [128, st_w], F32)))
                     for i in range(2)]
            os_ps = TBuf(EC(nc.psum_tensor("os_ps", [128, st_w], F32)))
            l2s1 = TBuf(EC(nc.psum_tensor("l2s1", [1, st_w], F32)))
            l2s2 = TBuf(EC(nc.psum_tensor("l2s2", [1, st_w], F32)))
            bc2_ps = TBuf(EC(nc.psum_tensor("bc2_ps", [128, st_w], F32)))

            def emit_xs(sti):
                s0x = sti * st_w
                par = sti % 2
                for e in range(ne):
                    ya = y1a[e % 2]
                    P.emit("sp", lambda s, e=e, ya=ya, s0x=s0x: s.dma_start(
                        ya.h[:, :], y1_scr.ap()[e][:, s0x:s0x + st_w]),
                        writes=[ya], chan=f"y1a{e % 2}")
                    ip = is_ps[e % 2]
                    P.emit("pe", lambda t, e=e, ip=ip, s0x=s0x: t.matmul(
                        ip.h[:, :], ICS.h[:, e * 128:(e + 1) * 128],
                        routing_t.h[:, s0x:s0x + st_w], start=True, stop=True),
                        reads=[ICS, routing_t], writes=[ip])
                    P.emit("dve", lambda v, e=e, ya=ya, ip=ip, par=par:
                           v.tensor_mul(
                               xs[par][e].h[:, :], ya.h[:, :], ip.h[:, :]),
                           reads=[ya, ip], writes=[xs[par][e]])

            def emit_norm(stp, d, tail=False):
                s0p = stp * st_w
                eng = "dve" if (tail and d % 2 == 1) else "pool"
                tn = tn2[d % 2]
                ys = y2s[d % 4]
                if eng == "pool":
                    P.emit("pool", lambda g, d=d, tn=tn: g.tensor_mul(
                        tn.h[:, :], z2[d].h[:, :], Br.h[:, 0:st_w]),
                        reads=[z2[d], Br], writes=[tn])
                    P.emit("pool", lambda g, tn=tn, ys=ys: g.tensor_sub(
                        ys.h[:, :], tn.h[:, :], Bm.h[:, 0:st_w]),
                        reads=[tn, Bm], writes=[ys])
                else:
                    P.emit("dve", lambda v, d=d, tn=tn: v.tensor_mul(
                        tn.h[:, :], z2[d].h[:, :], Br.h[:, 0:st_w]),
                        reads=[z2[d], Br], writes=[tn])
                    P.emit("dve", lambda v, tn=tn, ys=ys: v.tensor_sub(
                        ys.h[:, :], tn.h[:, :], Bm.h[:, 0:st_w]),
                        reads=[tn, Bm], writes=[ys])
                P.emit("pool", lambda g, d=d, ys=ys, s0p=s0p: g.dma_start(
                    out.ap()[d * 128:(d + 1) * 128, s0p:s0p + st_w],
                    ys.h[:, :]),
                    reads=[ys], chan="out")

            emit_xs(0)

            def emit_stats(d, zd, sq):
                P.emit("pe", lambda t, zd=zd, d=d: t.matmul(
                    l2s1.h[0:1, :], ones_bf.h[:, :], zd.h[:, :],
                    start=(d == 0), stop=(d == ne - 1)),
                    reads=[ones_bf, zd], writes=[l2s1])
                P.emit("pe", lambda t, sq=sq, d=d: t.matmul(
                    l2s2.h[0:1, :], ones_bf.h[:, :], sq.h[:, :],
                    start=(d == 0), stop=(d == ne - 1)),
                    reads=[ones_bf, sq], writes=[l2s2])

            for sti in range(nst):
                s0 = sti * st_w
                xcur = xs[sti % 2]
                pending_stats = []
                for d in range(ne):
                    wsb = WS[d % 3]
                    P.emit("sp", lambda s, d=d, wsb=wsb: s.dma_start(
                        wsb.h[:, :], ws_scr.ap()[d]),
                        writes=[wsb], chan=f"ws_ffn{d % 3}")
                    P.emit("pe", lambda t, d=d, s0=s0: t.matmul(
                        os_ps.h[:, :], OCS.h[:, d * 128:(d + 1) * 128],
                        routing_t.h[:, s0:s0 + st_w], start=True, stop=True),
                        reads=[OCS, routing_t], writes=[os_ps])
                    ob = os_sb[d % 2]
                    P.emit("act", lambda sc, ob=ob: sc.activation(
                        ob.h[:, :], os_ps.h[:, :], AF.Copy),
                        reads=[os_ps], writes=[ob])
                    op = out_ps[d % 2]
                    for e in range(ne):
                        P.emit("pe", lambda t, e=e, wsb=wsb, op=op, xcur=xcur:
                               t.matmul(
                                   op.h[:, :],
                                   wsb.h[:, e * 128:(e + 1) * 128],
                                   xcur[e].h[:, :],
                                   start=(e == 0), stop=(e == ne - 1)),
                               reads=[wsb, xcur[e]], writes=[op])
                    if pending_stats:
                        emit_stats(*pending_stats.pop())
                    yr = y1r[d % 2]
                    P.emit("sp", lambda s, d=d, yr=yr, s0=s0: s.dma_start(
                        yr.h[:, :], y1_scr.ap()[d][:, s0:s0 + st_w]),
                        writes=[yr], chan=f"y1r{d % 2}")
                    fb = fo[d % 2]
                    P.emit("act", lambda sc, fb=fb, op=op: sc.activation(
                        fb.h[:, :], op.h[:, :], AF.Copy),
                        reads=[op], writes=[fb])
                    tb = tf[d % 2]
                    P.emit("dve", lambda v, tb=tb, fb=fb, ob=ob: v.tensor_mul(
                        tb.h[:, :], fb.h[:, :], ob.h[:, :]),
                        reads=[fb, ob], writes=[tb])
                    P.emit("dve", lambda v, d=d, tb=tb, yr=yr:
                           v.scalar_tensor_tensor(
                               z2[d].h[:, :], tb.h[:, :], BFN.h[:, d:d + 1],
                               yr.h[:, :], ALU.add, ALU.add),
                           reads=[tb, BFN, yr], writes=[z2[d]])
                    sq = sqc[d % 3]
                    P.emit("act", lambda sc, d=d, sq=sq: sc.activation(
                        sq.h[:, :], z2[d].h[:, :], AF.Square),
                        reads=[z2[d]], writes=[sq])
                    pending_stats.append((d, z2[d], sq))
                    if d == 8 and sti + 1 < nst:
                        emit_xs(sti + 1)
                if pending_stats:
                    emit_stats(*pending_stats.pop())
                rA2 = rowpool.h[0:1, 0:st_w]
                rB2 = rowpool.h[0:1, seq:seq + st_w]
                P.emit("dve", lambda v, a=rA2: v.tensor_scalar_mul(
                    a, l2s1.h[0:1, :], 1.0 / dm),
                    reads=[l2s1], writes=[rowpool])
                P.emit("dve", lambda v, a=rA2, b=rB2: v.tensor_mul(b, a, a),
                       reads=[rowpool], writes=[rowpool])
                P.emit("dve", lambda v, b=rB2: v.tensor_scalar_add(b, b, -EPS),
                       reads=[rowpool], writes=[rowpool])
                P.emit("dve", lambda v, b=rB2: v.scalar_tensor_tensor(
                    b, l2s2.h[0:1, :], 1.0 / dm, b, ALU.mult, ALU.subtract),
                    reads=[l2s2, rowpool], writes=[rowpool])
                P.emit("act", lambda sc, b=rB2: sc.activation(b, b, AF.Ln),
                       reads=[rowpool], writes=[rowpool])
                P.emit("act", lambda sc, b=rB2: sc.activation(
                    b, b, AF.Exp, scale=-0.5),
                    reads=[rowpool], writes=[rowpool])
                P.emit("pe", lambda t, b=rB2: t.matmul(
                    bc2_ps.h[:, :], onesrow_f.h[:, :], b,
                    start=True, stop=True),
                    reads=[onesrow_f, rowpool], writes=[bc2_ps])
                P.emit("act", lambda sc: sc.activation(
                    Br.h[:, 0:st_w], bc2_ps.h[:, :], AF.Copy),
                    reads=[bc2_ps], writes=[Br])
                P.emit("dve", lambda v, a=rA2, b=rB2: v.tensor_mul(a, a, b),
                       reads=[rowpool], writes=[rowpool])
                P.emit("pe", lambda t, a=rA2: t.matmul(
                    bc2_ps.h[:, :], onesrow_f.h[:, :], a,
                    start=True, stop=True),
                    reads=[onesrow_f, rowpool], writes=[bc2_ps])
                P.emit("act", lambda sc: sc.activation(
                    Bm.h[:, 0:st_w], bc2_ps.h[:, :], AF.Copy),
                    reads=[bc2_ps], writes=[Bm])
                for d in range(ne):
                    emit_norm(sti, d, tail=(sti == nst - 1))

            P.barrier()
        P.replay()
    return nc


_CACHE = {}


def _get_program():
    if "nc" not in _CACHE:
        _CACHE["nc"] = build_program()
    return _CACHE["nc"]


def kernel(L0_attn_hidden_states, L0_attn_wq, L0_attn_bq, L0_attn_wk,
           L0_attn_bk, L0_attn_wv, L0_attn_bv, L0_attn_wl, L0_attn_bl,
           L0_ffn_weight, L0_ffn_bias, L0_ffn_gate_w, L0_ffn_in_channel_scale,
           L0_ffn_out_channel_scale):
    x = np.asarray(L0_attn_hidden_states, dtype=np.float32)

    def tile_w(w):
        return np.ascontiguousarray(
            w.T.reshape(NH, 128, NH, 128).transpose(2, 1, 0, 3)).astype(
                np.float32, copy=False)

    wqh = tile_w(np.asarray(L0_attn_wq, np.float32))
    wkh = tile_w(np.asarray(L0_attn_wk, np.float32))
    wvh = tile_w(np.asarray(L0_attn_wv, np.float32))
    wsh = tile_w(np.asarray(L0_ffn_weight, np.float32))
    gwh = np.ascontiguousarray(
        np.asarray(L0_ffn_gate_w, np.float32).T.reshape(NH, 128, 8)
        .transpose(1, 0, 2)).reshape(128, NH * 8)
    bqh = np.ascontiguousarray(np.asarray(L0_attn_bq, np.float32).reshape(NH, 128).T)
    bkh = np.ascontiguousarray(np.asarray(L0_attn_bk, np.float32).reshape(NH, 128).T)
    bvh = np.ascontiguousarray(np.asarray(L0_attn_bv, np.float32).reshape(NH, 128).T)
    bfh = np.ascontiguousarray(np.asarray(L0_ffn_bias, np.float32).reshape(NH, 128).T)
    icsa = np.ascontiguousarray(np.asarray(L0_ffn_in_channel_scale, np.float32))
    ocsa = np.ascontiguousarray(np.asarray(L0_ffn_out_channel_scale, np.float32))

    nc = _get_program()
    in_maps = []
    for c in range(N_CORES):
        in_maps.append({
            "xt": np.ascontiguousarray(x[c].T),
            "wqh": wqh, "wkh": wkh, "wvh": wvh, "wsh": wsh, "gwh": gwh,
            "bqh": bqh, "bkh": bkh, "bvh": bvh, "bfh": bfh,
            "ics": icsa, "ocs": ocsa,
        })
    res = run_bass_kernel_spmd(nc, in_maps, core_ids=list(range(N_CORES)))
    outs = [np.ascontiguousarray(r["out"].T) for r in res.results]
    return np.stack(outs, axis=0).astype(np.float32, copy=False)


# version 23
# speedup vs baseline: 1.0256x; 1.0256x over previous
import bisect
from contextlib import ExitStack

import numpy as np
import ml_dtypes

import concourse.bass as bass
import concourse.mybir as mybir
from concourse.bass_utils import run_bass_kernel_spmd

F32 = mybir.dt.float32
BF16 = mybir.dt.bfloat16
AF = mybir.ActivationFunctionType
ALU = mybir.AluOpType
AX = mybir.AxisListType

N_CORES = 8
SEQ = 2048
NH = 32
EPS = 1e-5
SCALE = float(1.0 / np.sqrt(np.float32(128.0)))

SAME_ENGINE_SYNC = True

_BLOCK_ATTR = {"pe": "tensor", "act": "scalar", "dve": "vector",
               "pool": "gpsimd", "sp": "sync"}


class TBuf:
    __slots__ = ("h", "w", "r", "name")

    def __init__(self, h, name=""):
        self.h = h
        self.name = name
        self.w = {}
        self.r = {}


class Chan:
    __slots__ = ("name", "sem", "count")

    def __init__(self, name, sem):
        self.name = name
        self.sem = sem
        self.count = 0


class Prog:
    ENGINES = ("pe", "act", "dve", "pool", "sp")

    def __init__(self, nc, ctx):
        self.nc = nc
        self.ops = {e: [] for e in self.ENGINES}
        self.n = {e: 0 for e in self.ENGINES}
        self.incs = {e: [] for e in self.ENGINES}
        self.inc_flags = {e: set() for e in self.ENGINES}
        self.wm = {e: {} for e in self.ENGINES}
        self.sems = {e: ctx.enter_context(nc.semaphore("s_" + e))
                     for e in self.ENGINES}
        self.chans = {}
        self._ctx = ctx

    def chan(self, name):
        if name not in self.chans:
            sem = self._ctx.enter_context(self.nc.semaphore("c_" + name))
            self.chans[name] = Chan(name, sem)
        return self.chans[name]

    def _inc_value(self, ename, pos):
        lst = self.incs[ename]
        idx = bisect.bisect_left(lst, pos)
        if idx < len(lst):
            return idx + 1
        assert pos <= self.n[ename], (ename, pos, self.n[ename])
        lst.append(pos)
        self.inc_flags[ename].add(pos)
        return len(lst)

    def _need(self, consumer, key, pos):
        if key[0] == "e":
            ename = key[1]
            if ename == consumer and (ename == "pe" or not SAME_ENGINE_SYNC):
                return
            val = self._inc_value(ename, pos)
        else:
            val = 16 * self.chans[key[1]].count
        if self.wm[consumer].get(key, 0) >= val:
            return
        self.ops[consumer].append(("w", key, val))
        self.wm[consumer][key] = val

    def emit(self, ename, build, reads=(), writes=(), chan=None):
        deps = {}
        for b in reads:
            for k, p in b.w.items():
                if deps.get(k, 0) < p:
                    deps[k] = p
        for b in writes:
            for k, p in b.w.items():
                if deps.get(k, 0) < p:
                    deps[k] = p
            for k, p in b.r.items():
                if deps.get(k, 0) < p:
                    deps[k] = p
        for k in sorted(deps, key=repr):
            self._need(ename, k, deps[k])
        if chan is None:
            self.n[ename] += 1
            pos = self.n[ename]
            self.ops[ename].append(("i", build, pos))
            tok_key, tok_pos = ("e", ename), pos
        else:
            c = self.chan(chan)
            c.count += 1
            pos = c.count
            self.ops[ename].append(("d", build, chan))
            tok_key, tok_pos = ("c", chan), pos
        for b in reads:
            if b.r.get(tok_key, 0) < tok_pos:
                b.r[tok_key] = tok_pos
        for b in writes:
            if b.w.get(tok_key, 0) < tok_pos:
                b.w[tok_key] = tok_pos
            b.r = {}
        return (tok_key, tok_pos)

    def barrier(self):
        snap = {("e", e): self.n[e] for e in self.ENGINES if self.n[e] > 0}
        for cname, ch in self.chans.items():
            if ch.count > 0:
                snap[("c", cname)] = ch.count
        for e in self.ENGINES:
            for k in sorted(snap, key=repr):
                if k == ("e", e):
                    continue
                self._need(e, k, snap[k])

    def replay(self):
        nc = self.nc
        with nc.Block() as block:
            for ename in self.ENGINES:
                ops = self.ops[ename]
                if not ops:
                    continue

                def runner(eng, ops=ops, ename=ename):
                    flags = self.inc_flags[ename]
                    sems = self.sems
                    chans = self.chans
                    for op in ops:
                        if op[0] == "w":
                            key = op[1]
                            sem = (sems[key[1]] if key[0] == "e"
                                   else chans[key[1]].sem)
                            eng.wait_ge(sem, op[2])
                        elif op[0] == "i":
                            inst = op[1](eng)
                            if op[2] in flags:
                                inst.then_inc(sems[ename], 1)
                        else:
                            inst = op[1](eng)
                            inst.then_inc(chans[op[2]].sem, 16)

                getattr(block, _BLOCK_ATTR[ename])(runner)


def build_program(seq=SEQ, nh=NH, debug=False):
    ne = nh
    dm = 128 * nh
    sl_w = min(512, seq)
    nsl = seq // sl_w
    st_w = min(512, seq)
    nst = seq // st_w
    SLB = min(256, seq)
    nslab = seq // SLB
    G = 4
    ngrp = SLB // G

    nc = bass.Bass("TRN2", target_bir_lowering=False)

    xt = nc.dram_tensor("xt", [dm, seq], F32, kind="ExternalInput")
    wqh = nc.dram_tensor("wqh", [nh, 128, ne, 128], F32, kind="ExternalInput")
    wkh = nc.dram_tensor("wkh", [nh, 128, ne, 128], F32, kind="ExternalInput")
    wvh = nc.dram_tensor("wvh", [nh, 128, ne, 128], F32, kind="ExternalInput")
    wsh = nc.dram_tensor("wsh", [ne, 128, ne, 128], F32, kind="ExternalInput")
    gwh = nc.dram_tensor("gwh", [128, ne * 8], F32, kind="ExternalInput")
    bqh = nc.dram_tensor("bqh", [128, nh], F32, kind="ExternalInput")
    bkh = nc.dram_tensor("bkh", [128, nh], F32, kind="ExternalInput")
    bvh = nc.dram_tensor("bvh", [128, nh], F32, kind="ExternalInput")
    bfh = nc.dram_tensor("bfh", [128, ne], F32, kind="ExternalInput")
    ics = nc.dram_tensor("ics", [8, dm], F32, kind="ExternalInput")
    ocs = nc.dram_tensor("ocs", [8, dm], F32, kind="ExternalInput")

    nslab_ = seq // min(256, seq)
    SLB_ = min(256, seq)
    q_scr = nc.dram_tensor("q_scr", [nslab_, nh, 128, SLB_], BF16, kind="Internal")
    k_scr = nc.dram_tensor("k_scr", [nslab_, nh, 128, SLB_], BF16, kind="Internal")
    v_scr = nc.dram_tensor("v_scr", [nslab_, nh, 128, SLB_], BF16, kind="Internal")
    c_scr = nc.dram_tensor("c_scr", [nh, 128, seq], BF16, kind="Internal")
    z_scr = nc.dram_tensor("z_scr", [ne, 128, seq], BF16, kind="Internal")
    y1_scr = nc.dram_tensor("y1_scr", [ne, 128, seq], BF16, kind="Internal")
    ws_scr = nc.dram_tensor("ws_scr", [ne, 128, dm], BF16, kind="Internal")

    out = nc.dram_tensor("out", [dm, seq], F32, kind="ExternalOutput")
    _p4 = np.kron(np.eye(4, dtype=np.float32),
                  np.ones((1, nh), dtype=np.float32)) * 32.0
    p4a_dram = nc.inline_tensor(_p4.astype(ml_dtypes.bfloat16), name="p4a_c")
    p4b_dram = nc.inline_tensor(
        np.concatenate([_p4, _p4], axis=1).astype(ml_dtypes.bfloat16),
        name="p4b_c")
    if debug:
        dbg_qkv = nc.dram_tensor("dbg_qkv", [3, nslab_, nh, 128, SLB_], BF16,
                                 kind="ExternalOutput")
        dbg_c = nc.dram_tensor("dbg_c", [nh, 128, seq], BF16,
                               kind="ExternalOutput")
        dbg_y1 = nc.dram_tensor("dbg_y1", [ne, 128, seq], BF16,
                                kind="ExternalOutput")
        dbg_rt = nc.dram_tensor("dbg_rt", [8, seq], BF16,
                                kind="ExternalOutput")

    ctx = ExitStack()
    with ctx:
        P = Prog(nc, ctx)
        E = ctx.enter_context

        ident_bf = TBuf(E(nc.sbuf_tensor("ident_bf", [128, 128], BF16)))
        ones_bf = TBuf(E(nc.sbuf_tensor("ones_bf", [128, 1], BF16)))
        onesrow_f = TBuf(E(nc.sbuf_tensor("onesrow_f", [1, 128], F32)))
        onesrow_bf = TBuf(E(nc.sbuf_tensor("onesrow_bf", [1, 128], BF16)))
        rowpool = TBuf(E(nc.sbuf_tensor("rowpool", [1, 2 * seq], F32)))
        BQ = TBuf(E(nc.sbuf_tensor("BQ", [128, nh], F32)))
        BK = TBuf(E(nc.sbuf_tensor("BK", [128, nh], F32)))
        BV = TBuf(E(nc.sbuf_tensor("BV", [128, nh], F32)))
        BFN = TBuf(E(nc.sbuf_tensor("BFN", [128, ne], F32)))
        GW = TBuf(E(nc.sbuf_tensor("GW", [128, ne * 8], BF16)))
        routing_t = TBuf(E(nc.sbuf_tensor("routing_t", [8, seq], BF16)))
        Bm = TBuf(E(nc.sbuf_tensor("Bm", [128, seq], BF16)))
        Br = TBuf(E(nc.sbuf_tensor("Br", [128, seq], BF16)))

        P.emit("pool", lambda g: g.memset(ident_bf.h[:, :], 0.0),
               writes=[ident_bf])
        P.emit("pool", lambda g: g.affine_select(
            out=ident_bf.h[:, :], in_=ident_bf.h[:, :],
            compare_op=ALU.not_equal, fill=1.0, base=0,
            pattern=[[-1, 128]], channel_multiplier=1),
            reads=[ident_bf], writes=[ident_bf])
        P.emit("pool", lambda g: g.memset(ones_bf.h[:, :], 1.0),
               writes=[ones_bf])
        P.emit("pool", lambda g: g.memset(onesrow_f.h[:, :], 1.0),
               writes=[onesrow_f])
        P.emit("pool", lambda g: g.memset(onesrow_bf.h[:, :], 1.0),
               writes=[onesrow_bf])

        for mi, (buf, srct) in enumerate(
                ((BQ, bqh), (BK, bkh), (BV, bvh), (BFN, bfh))):
            P.emit("sp", lambda s, b=buf, t=srct: s.dma_start(b.h[:, :], t.ap()),
                   writes=[buf], chan=f"msc{mi}")
        P.emit("pool", lambda g: g.dma_start(GW.h[:, :], gwh.ap()),
               writes=[GW], chan="mgw")

        zfill_tb = TBuf(None, "zfill")
        ctxA0 = ExitStack()
        with ctxA0:
            EA = ctxA0.enter_context
            X = [TBuf(EA(nc.sbuf_tensor(f"X{c}", [128, seq], BF16)), f"X{c}")
                 for c in range(ne)]
            for c in range(ne):
                P.emit("pool", lambda g, c=c: g.dma_start(
                    X[c].h[:, :], xt.ap()[c * 128:(c + 1) * 128, :]),
                    writes=[X[c]], chan=f"x{c}")
            for c in range(ne):
                P.emit("sp", lambda s, c=c: s.dma_start(
                    z_scr.ap()[c], X[c].h[:, :]),
                    reads=[X[c]], writes=[zfill_tb], chan="zfill")

            wbufs = [TBuf(EA(nc.sbuf_tensor(f"W{i}", [128, dm], BF16)))
                     for i in range(2)]
            stage = [TBuf(EA(nc.sbuf_tensor(f"pst{i}", [128, seq], BF16)))
                     for i in range(3)]
            wsbufs = [TBuf(EA(nc.sbuf_tensor(f"wsbuf{i}", [128, dm // 2], BF16)))
                      for i in range(2)]
            proj_ps = [TBuf(EA(nc.psum_tensor(f"proj_ps{i}", [128, 512], F32)))
                       for i in range(2)]

            ws_tasks = [(h, hf) for h in range(ne) for hf in range(2)]

            def emit_ws_load(k):
                h, hf = ws_tasks[k]
                b = wsbufs[k % 2]
                P.emit("pool", lambda g, h=h, hf=hf, b=b: g.dma_start(
                    b.h[:, :],
                    wsh.ap()[h][:, hf * (ne // 2):(hf + 1) * (ne // 2), :]),
                    writes=[b], chan="ws_in")

            emit_ws_load(0)
            emit_ws_load(1)
            ws_k = 0

            wsrcs = (wqh, wkh, wvh)
            wdsts = (q_scr, k_scr, v_scr)
            wbias = (BQ, BK, BV)
            wload = [(h, wi) for h in range(nh) for wi in range(3)]

            def emit_wload(j):
                h, wi = wload[j]
                buf = wbufs[j % 2]
                P.emit("pool", lambda g, h=h, wi=wi, b=buf: g.dma_start(
                    b.h[:, :], wsrcs[wi].ap()[h]),
                    writes=[buf], chan=f"wl{j % 2}")

            emit_wload(0)
            if len(wload) > 1:
                emit_wload(1)
            next_load = 2

            for j, (h, wi) in enumerate(wload):
                wb = wbufs[j % 2]
                st = stage[j % 3]
                for s4 in range(nsl):
                    pp = proj_ps[(j * nsl + s4) % 2]
                    for e in range(ne):
                        P.emit("pe", lambda t, wb=wb, e=e, s4=s4, pp=pp: t.matmul(
                            pp.h[:, 0:sl_w],
                            wb.h[:, e * 128:(e + 1) * 128],
                            X[e].h[:, s4 * sl_w:(s4 + 1) * sl_w],
                            start=(e == 0), stop=(e == ne - 1)),
                            reads=[wb, X[e]], writes=[pp])
                    P.emit("act", lambda sc, st=st, s4=s4, wi=wi, h=h, pp=pp:
                           sc.activation(
                               st.h[:, s4 * sl_w:(s4 + 1) * sl_w],
                               pp.h[:, 0:sl_w], AF.Identity,
                               bias=wbias[wi].h[:, h:h + 1]),
                           reads=[pp, wbias[wi]], writes=[st])
                P.emit("sp", lambda s, st=st, wi=wi, h=h: s.dma_start(
                    wdsts[wi].ap()[:, h, :, :].transpose([1, 0, 2]),
                    st.h[:, :].rearrange("d (sl s) -> d sl s", s=SLB)),
                    reads=[st], chan="qkv_out")
                if next_load < len(wload):
                    emit_wload(next_load)
                    next_load += 1
                if wi == 0 and h < ne:
                    for _ in range(2):
                        wh, whf = ws_tasks[ws_k]
                        b = wsbufs[ws_k % 2]
                        P.emit("act", lambda sc, b=b: sc.sign(
                            b.h[:, :], b.h[:, :]),
                            reads=[b], writes=[b])
                        P.emit("sp", lambda s, wh=wh, whf=whf, b=b: s.dma_start(
                            ws_scr.ap()[wh][:, whf * (dm // 2):(whf + 1) * (dm // 2)],
                            b.h[:, :]),
                            reads=[b], chan="ws_out")
                        if ws_k + 2 < len(ws_tasks):
                            emit_ws_load(ws_k + 2)
                        ws_k += 1
            if debug:
                P.barrier()
                for wi in range(3):
                    P.emit("sp", lambda s, wi=wi: s.dma_start(
                        dbg_qkv.ap()[wi], wdsts[wi].ap()), chan="dbg")

        P.barrier()
        ctxA1 = ExitStack()
        with ctxA1:
            EA = ctxA1.enter_context
            Qs = [TBuf(EA(nc.sbuf_tensor(f"Qs{i}", [128, ne * SLB], BF16)))
                  for i in range(2)]
            Ks = [TBuf(EA(nc.sbuf_tensor(f"Ks{i}", [128, ne * SLB], BF16)))
                  for i in range(2)]
            Vs = [TBuf(EA(nc.sbuf_tensor(f"Vs{i}", [128, ne * SLB], BF16)))
                  for i in range(2)]
            SRW = [TBuf(EA(nc.sbuf_tensor(f"SRW{p}", [128, ne * SLB], BF16)))
                   for p in range(2)]
            gdim = G * ne
            CTXF_ts = [EA(nc.sbuf_tensor(f"CTXF{i}", [128, ne * SLB], BF16))
                       for i in range(2)]
            CTXFbs = [[TBuf(CTXF_ts[i], f"ctxf{i}A"), TBuf(CTXF_ts[i], f"ctxf{i}B")]
                      for i in range(2)]
            U4 = [TBuf(EA(nc.sbuf_tensor(f"U4_{i}", [gdim, 2 * gdim], BF16)))
                  for i in range(3)]
            Vts4 = [TBuf(EA(nc.sbuf_tensor(f"Vts4_{i}", [gdim, 256], BF16)))
                    for i in range(3)]
            ctxn = [TBuf(EA(nc.sbuf_tensor(f"ctxn{i}", [gdim, 256], BF16)))
                    for i in range(3)]
            rden = [TBuf(EA(nc.sbuf_tensor(f"rden{i}", [gdim, 2], F32)))
                    for i in range(2)]
            negrow = TBuf(EA(nc.sbuf_tensor("negrow", [1, gdim], BF16)))
            ones2 = TBuf(EA(nc.sbuf_tensor("ones2", [1, 2 * gdim], BF16)))
            p4a = TBuf(EA(nc.sbuf_tensor("p4a", [4, gdim], BF16)))
            p4b = TBuf(EA(nc.sbuf_tensor("p4b", [4, 2 * gdim], BF16)))

            gram_ps = [TBuf(EA(nc.psum_tensor(f"gram{i}", [gdim, 2 * gdim], F32)))
                       for i in range(2)]
            vts_ps = TBuf(EA(nc.psum_tensor("vtsp0", [gdim, 256], BF16)))
            ctxT_ps = [TBuf(EA(nc.psum_tensor(f"ctxT{i}", [gdim, 256], F32)))
                       for i in range(2)]
            den_ps = TBuf(EA(nc.psum_tensor("denp", [gdim, 2], F32)))
            ctf_ps = [TBuf(EA(nc.psum_tensor(f"ctf{i}", [128, 2 * gdim], BF16)))
                      for i in range(2)]
            P.emit("pool", lambda g: g.memset(negrow.h[:, :], -1024.0),
                   writes=[negrow])
            P.emit("pool", lambda g: g.memset(ones2.h[:, :], 1.0),
                   writes=[ones2])
            P.emit("sp", lambda s: s.dma_start(p4a.h[:, :], p4a_dram.ap()),
                   writes=[p4a], chan="mmask")
            P.emit("sp", lambda s: s.dma_start(p4b.h[:, :], p4b_dram.ap()),
                   writes=[p4b], chan="mmask")
            def load_one(sl, which):
                buf = SRW[0] if which in ("q", "v") else SRW[1]
                scr = {"q": q_scr, "k": k_scr, "v": v_scr}[which]
                P.emit("sp", lambda s, b=buf, scr=scr, sl=sl: s.dma_start(
                    b.h[:, :].rearrange("d (t s) -> d t s", s=SLB),
                    scr.ap()[sl].transpose([1, 0, 2])),
                    writes=[buf], chan=which + "sl" + str(sl % 2))

            def rearr_one(sl, which):
                stg = SRW[0] if which in ("q", "v") else SRW[1]
                dst = {"q": Qs, "k": Ks, "v": Vs}[which][sl % 2]
                P.emit("dve", lambda v, b=dst, stg=stg: v.tensor_copy(
                    b.h[:, :].rearrange("d (s t) -> d s t", t=ne),
                    stg.h[:, :].rearrange("d (t s) -> d s t", s=SLB)),
                    reads=[stg], writes=[dst])

            for w in ("q", "k"):
                load_one(0, w)
            for w in ("q", "k"):
                rearr_one(0, w)
            load_one(0, "v")
            rearr_one(0, "v")
            for sl in range(nslab):
                s0 = sl * SLB
                qb_, kb_, vb_ = Qs[sl % 2], Ks[sl % 2], Vs[sl % 2]
                CTXF_t = CTXF_ts[sl % 2]
                CTXFb = CTXFbs[sl % 2]

                def gview(b, g):
                    return b.h[:, g * G * ne:(g + 1) * G * ne]

                def rview(b, g):
                    return (b.h[:, :].rearrange("d (t s) -> d s t", s=SLB)
                            [:, g * G:(g + 1) * G, :])

                nmac = ngrp // 2

                def stage1(m):
                    gp = gram_ps[m % 2]
                    P.emit("pe", lambda t, gp=gp: t.matmul(
                        gp.h[:, :], negrow.h[:, :], ones2.h[:, :],
                        start=True, stop=False),
                        reads=[negrow, ones2], writes=[gp])
                    P.emit("pe", lambda t, gp=gp: t.matmul(
                        gp.h[:, :], p4a.h[:, :], p4b.h[:, :],
                        start=False, stop=False),
                        reads=[p4a, p4b], writes=[gp])
                    for k in (0, 1):
                        g = 2 * m + k
                        P.emit("pe", lambda t, g=g, k=k, vb_=vb_: t.transpose(
                            vts_ps.h[:, k * 128:(k + 1) * 128],
                            gview(vb_, g), ident_bf.h[:, :]),
                            reads=[vb_, ident_bf], writes=[vts_ps])
                        P.emit("pe", lambda t, g=g, k=k, gp=gp, kb_=kb_,
                               qb_=qb_: t.matmul(
                                   gp.h[:, k * gdim:(k + 1) * gdim],
                                   gview(kb_, g), gview(qb_, g),
                                   start=False, stop=(k == 1)),
                               reads=[kb_, qb_], writes=[gp])
                    ub = U4[m % 3]
                    P.emit("act", lambda sc, ub=ub, gp=gp: sc.activation(
                        ub.h[:, :], gp.h[:, :], AF.Exp, scale=SCALE),
                        reads=[gp], writes=[ub])
                    vb4 = Vts4[m % 3]
                    P.emit("dve", lambda v, vb4=vb4: v.tensor_copy(
                        vb4.h[:, :], vts_ps.h[:, :]),
                        reads=[vts_ps], writes=[vb4])

                def stage2(p):
                    um, vb4 = U4[p % 3], Vts4[p % 3]
                    ctp = ctxT_ps[p % 2]
                    for k in (0, 1):
                        P.emit("pe", lambda t, k=k, um=um, vb4=vb4, ctp=ctp:
                               t.matmul(
                                   ctp.h[:, k * 128:(k + 1) * 128],
                                   um.h[:, k * gdim:(k + 1) * gdim],
                                   vb4.h[:, k * 128:(k + 1) * 128],
                                   start=True, stop=True),
                               reads=[um, vb4], writes=[ctp])
                        P.emit("pe", lambda t, k=k, um=um: t.matmul(
                            den_ps.h[:, k:k + 1],
                            um.h[:, k * gdim:(k + 1) * gdim],
                            ones_bf.h[0:gdim, 0:1],
                            start=True, stop=True),
                            reads=[um, ones_bf], writes=[den_ps])
                    rd = rden[p % 2]
                    P.emit("dve", lambda v, rd=rd: v.reciprocal(
                        rd.h[:, :], den_ps.h[:, :]),
                        reads=[den_ps], writes=[rd])
                    cn_ = ctxn[p % 3]
                    for k in (0, 1):
                        P.emit("dve", lambda v, k=k, cn_=cn_, ctp=ctp, rd=rd:
                               v.tensor_scalar_mul(
                                   cn_.h[:, k * 128:(k + 1) * 128],
                                   ctp.h[:, k * 128:(k + 1) * 128],
                                   rd.h[:, k:k + 1]),
                               reads=[ctp, rd], writes=[cn_])

                def stage3(p):
                    cn_ = ctxn[p % 3]
                    cf = ctf_ps[p % 2]
                    for k in (0, 1):
                        P.emit("pe", lambda t, k=k, cn_=cn_, cf=cf: t.transpose(
                            cf.h[:, k * gdim:(k + 1) * gdim],
                            cn_.h[:, k * 128:(k + 1) * 128],
                            ident_bf.h[0:gdim, 0:gdim]),
                            reads=[cn_, ident_bf], writes=[cf])
                    ctb = CTXFb[p % 2]
                    P.emit("act", lambda sc, p=p, cf=cf, CTXF_t=CTXF_t:
                        sc.activation(
                        CTXF_t[:, :].rearrange("d (h s) -> d s h", s=SLB)
                        [:, p * 2 * G:(p + 1) * 2 * G, :],
                        cf.h[:, :].rearrange("d (s h) -> d s h", h=ne),
                        AF.Copy),
                        reads=[cf], writes=[ctb])

                for m in range(nmac + 4):
                    if m < nmac:
                        stage1(m)
                    if 2 <= m < nmac + 2:
                        stage2(m - 2)
                    if 4 <= m:
                        stage3(m - 4)
                    if sl + 1 < nslab:
                        if m == 0:
                            load_one(sl + 1, "q")
                            load_one(sl + 1, "k")
                        if m == min(8, nmac - 4):
                            rearr_one(sl + 1, "q")
                            rearr_one(sl + 1, "k")
                        if m == min(9, nmac - 3):
                            load_one(sl + 1, "v")
                        if m == min(16, nmac - 2):
                            rearr_one(sl + 1, "v")

                for h in range(ne):
                    P.emit("pool", lambda g, h=h, s0=s0, CTXF_t=CTXF_t:
                        g.dma_start(
                            z_scr.ap()[h][:, s0:s0 + SLB],
                            CTXF_t[:, h * SLB:(h + 1) * SLB],
                            accum_op=ALU.add),
                        reads=[CTXFb[0], CTXFb[1], zfill_tb],
                        chan=f"c_out{sl % 2}")

        P.barrier()
        ctxB = ExitStack()
        with ctxB:
            EB = ctxB.enter_context
            Z = [TBuf(EB(nc.sbuf_tensor(f"Z{c}", [128, seq], BF16)))
                 for c in range(ne)]
            sqb = [TBuf(EB(nc.sbuf_tensor(f"sqb{i}", [128, seq], BF16)))
                   for i in range(2)]
            tnb = [TBuf(EB(nc.sbuf_tensor(f"tnb{i}", [128, seq], BF16)))
                   for i in range(2)]

            ctxB1 = ExitStack()
            with ctxB1:
                sum1 = TBuf(ctxB1.enter_context(
                    nc.psum_tensor("sum1", [1, seq], F32)))
                sum2 = TBuf(ctxB1.enter_context(
                    nc.psum_tensor("sum2", [1, seq], F32)))
                for c in range(ne):
                    P.emit("sp", lambda s, c=c: s.dma_start(
                        Z[c].h[:, :], z_scr.ap()[c]),
                        writes=[Z[c]], chan=f"zin{c % 4}")
                    sq = sqb[c % 2]
                    P.emit("act", lambda sc, c=c, sq=sq: sc.activation(
                        sq.h[:, :], Z[c].h[:, :], AF.Square),
                        reads=[Z[c]], writes=[sq])
                    for s4 in range(nsl):
                        P.emit("pe", lambda t, c=c, s4=s4: t.matmul(
                            sum1.h[0:1, s4 * sl_w:(s4 + 1) * sl_w],
                            ones_bf.h[:, :],
                            Z[c].h[:, s4 * sl_w:(s4 + 1) * sl_w],
                            start=(c == 0), stop=(c == ne - 1)),
                            reads=[ones_bf, Z[c]], writes=[sum1])
                        P.emit("pe", lambda t, sq=sq, s4=s4, c=c: t.matmul(
                            sum2.h[0:1, s4 * sl_w:(s4 + 1) * sl_w],
                            ones_bf.h[:, :],
                            sq.h[:, s4 * sl_w:(s4 + 1) * sl_w],
                            start=(c == 0), stop=(c == ne - 1)),
                            reads=[ones_bf, sq], writes=[sum2])
                rA = rowpool.h[0:1, 0:seq]
                rB = rowpool.h[0:1, seq:2 * seq]
                P.emit("dve", lambda v, a=rA: v.tensor_scalar_mul(
                    a, sum1.h[0:1, :], 1.0 / dm),
                    reads=[sum1], writes=[rowpool])
                P.emit("dve", lambda v, a=rA, b=rB: v.tensor_mul(b, a, a),
                       reads=[rowpool], writes=[rowpool])
                P.emit("dve", lambda v, b=rB: v.tensor_scalar_add(b, b, -EPS),
                       reads=[rowpool], writes=[rowpool])
                P.emit("dve", lambda v, b=rB: v.scalar_tensor_tensor(
                    b, sum2.h[0:1, :], 1.0 / dm, b, ALU.mult, ALU.subtract),
                    reads=[sum2, rowpool], writes=[rowpool])
                P.emit("act", lambda sc, b=rB: sc.activation(b, b, AF.Ln),
                       reads=[rowpool], writes=[rowpool])
                P.emit("act", lambda sc, b=rB: sc.activation(
                    b, b, AF.Exp, scale=-0.5),
                    reads=[rowpool], writes=[rowpool])
            P.barrier()

            ctxB2 = ExitStack()
            with ctxB2:
                ctxB2p = ExitStack()
                bcB_ps = TBuf(ctxB2p.enter_context(
                    nc.psum_tensor("bcB_ps", [128, seq], F32)))
                lg_ps = TBuf(ctxB2p.enter_context(
                    nc.psum_tensor("lg_ps", [8, seq], F32)))
                for s4 in range(nsl):
                    P.emit("pe", lambda t, s4=s4: t.matmul(
                        bcB_ps.h[:, s4 * sl_w:(s4 + 1) * sl_w],
                        onesrow_f.h[:, :],
                        rowpool.h[0:1, seq + s4 * sl_w:seq + (s4 + 1) * sl_w],
                        start=True, stop=True),
                        reads=[onesrow_f, rowpool], writes=[bcB_ps])
                P.emit("act", lambda sc: sc.activation(
                    Br.h[:, :], bcB_ps.h[:, :], AF.Copy),
                    reads=[bcB_ps], writes=[Br])
                P.emit("dve", lambda v: v.tensor_mul(
                    rowpool.h[0:1, 0:seq], rowpool.h[0:1, 0:seq],
                    rowpool.h[0:1, seq:2 * seq]),
                    reads=[rowpool], writes=[rowpool])
                for s4 in range(nsl):
                    P.emit("pe", lambda t, s4=s4: t.matmul(
                        bcB_ps.h[:, s4 * sl_w:(s4 + 1) * sl_w],
                        onesrow_f.h[:, :],
                        rowpool.h[0:1, s4 * sl_w:(s4 + 1) * sl_w],
                        start=True, stop=True),
                        reads=[onesrow_f, rowpool], writes=[bcB_ps])
                P.emit("act", lambda sc: sc.activation(
                    Bm.h[:, :], bcB_ps.h[:, :], AF.Copy),
                    reads=[bcB_ps], writes=[Bm])
                for c in range(ne):
                    tn = tnb[c % 2]
                    P.emit("dve", lambda v, c=c, tn=tn: v.tensor_mul(
                        tn.h[:, :], Z[c].h[:, :], Br.h[:, :]),
                        reads=[Z[c], Br], writes=[tn])
                    P.emit("dve", lambda v, c=c, tn=tn: v.tensor_sub(
                        Z[c].h[:, :], tn.h[:, :], Bm.h[:, :]),
                        reads=[tn, Bm], writes=[Z[c]])
                    P.emit("sp", lambda s, c=c: s.dma_start(
                        y1_scr.ap()[c], Z[c].h[:, :]),
                        reads=[Z[c]], chan="y1_out")
                    for s4 in range(nsl):
                        P.emit("pe", lambda t, c=c, s4=s4: t.matmul(
                            lg_ps.h[0:8, s4 * sl_w:(s4 + 1) * sl_w],
                            GW.h[:, c * 8:(c + 1) * 8],
                            Z[c].h[:, s4 * sl_w:(s4 + 1) * sl_w],
                            start=(c == 0), stop=(c == ne - 1)),
                            reads=[GW, Z[c]], writes=[lg_ps])
                lg_sb = TBuf(ctxB2.enter_context(
                    nc.sbuf_tensor("lg_sb", [8, seq], BF16)))
                P.emit("act", lambda sc: sc.activation(
                    lg_sb.h[:, :], lg_ps.h[:, :], AF.Copy),
                    reads=[lg_ps], writes=[lg_sb])
                nt = seq // 128
                ltm = TBuf(ctxB2.enter_context(
                    nc.sbuf_tensor("ltm", [128, nt * 8], BF16)))
                prob = TBuf(ctxB2.enter_context(
                    nc.sbuf_tensor("prob", [128, nt * 8], F32)))
                mx = TBuf(ctxB2.enter_context(
                    nc.sbuf_tensor("mx", [128, nt], F32)))
                den = TBuf(ctxB2.enter_context(
                    nc.sbuf_tensor("den", [128, nt], F32)))
                rtm = TBuf(ctxB2.enter_context(
                    nc.sbuf_tensor("rtm", [128, nt * 8], BF16)))
                P.barrier()
                ctxB2p.close()
                ctxB3 = ExitStack()
                with ctxB3:
                    lt_ps = TBuf(ctxB3.enter_context(
                        nc.psum_tensor("lt_ps", [128, nt * 8], BF16)))
                    rt_ps = TBuf(ctxB3.enter_context(
                        nc.psum_tensor("rt_ps", [8, seq], BF16)))
                    for tt in range(nt):
                        P.emit("pe", lambda t, tt=tt: t.transpose(
                            lt_ps.h[:, tt * 8:(tt + 1) * 8],
                            lg_sb.h[0:8, tt * 128:(tt + 1) * 128],
                            ident_bf.h[0:8, 0:8]),
                            reads=[lg_sb, ident_bf], writes=[lt_ps])
                    P.emit("dve", lambda v: v.tensor_copy(
                        ltm.h[:, :], lt_ps.h[:, :]),
                        reads=[lt_ps], writes=[ltm])
                    P.emit("dve", lambda v: v.reduce_max(
                        mx.h[:, :],
                        ltm.h[:, :].rearrange("p (g x) -> p g x", x=8),
                        axis=AX.X),
                        reads=[ltm], writes=[mx])
                    P.emit("dve", lambda v: v.tensor_scalar_mul(
                        mx.h[:, :], mx.h[:, :], -1.0),
                        reads=[mx], writes=[mx])
                    for tt in range(nt):
                        P.emit("act", lambda sc, tt=tt: sc.activation(
                            prob.h[:, tt * 8:(tt + 1) * 8],
                            ltm.h[:, tt * 8:(tt + 1) * 8], AF.Exp,
                            bias=mx.h[:, tt:tt + 1],
                            accum_out=den.h[:, tt:tt + 1]),
                            reads=[ltm, mx], writes=[prob, den])
                    P.emit("dve", lambda v: v.reciprocal(
                        den.h[:, :], den.h[:, :]),
                        reads=[den], writes=[den])
                    for tt in range(nt):
                        P.emit("dve", lambda v, tt=tt: v.tensor_scalar_mul(
                            rtm.h[:, tt * 8:(tt + 1) * 8],
                            prob.h[:, tt * 8:(tt + 1) * 8],
                            den.h[:, tt:tt + 1]),
                            reads=[prob, den], writes=[rtm])
                    for tt in range(nt):
                        P.emit("pe", lambda t, tt=tt: t.transpose(
                            rt_ps.h[0:8, tt * 128:(tt + 1) * 128],
                            rtm.h[:, tt * 8:(tt + 1) * 8],
                            ident_bf.h[:, :]),
                            reads=[rtm, ident_bf], writes=[rt_ps])
                    P.emit("dve", lambda v: v.tensor_copy(
                        routing_t.h[:, :], rt_ps.h[:, :]),
                        reads=[rt_ps], writes=[routing_t])
                    if debug:
                        P.emit("sp", lambda s: s.dma_start(
                            dbg_rt.ap(), routing_t.h[:, :]),
                            reads=[routing_t], chan="dbg")
                        P.emit("sp", lambda s: s.dma_start(
                            dbg_y1.ap(), y1_scr.ap()), chan="dbg")
                        P.emit("sp", lambda s: s.dma_start(
                            dbg_c.ap(), c_scr.ap()), chan="dbg")
                    P.barrier()

        ctxC = ExitStack()
        with ctxC:
            EC = ctxC.enter_context
            ICS = TBuf(EC(nc.sbuf_tensor("ICS", [8, dm], BF16)))
            OCS = TBuf(EC(nc.sbuf_tensor("OCS", [8, dm], BF16)))
            P.emit("pool", lambda g: g.dma_start(ICS.h[:, :], ics.ap()),
                   writes=[ICS], chan="mics")
            P.emit("pool", lambda g: g.dma_start(OCS.h[:, :], ocs.ap()),
                   writes=[OCS], chan="mocs")
            xs = [[TBuf(EC(nc.sbuf_tensor(f"xs{p}_{e}", [128, st_w], BF16)))
                   for e in range(ne)] for p in range(2)]
            z2 = [TBuf(EC(nc.sbuf_tensor(f"z2{d}", [128, st_w], BF16)))
                  for d in range(ne)]
            WS = [TBuf(EC(nc.sbuf_tensor(f"WS{i}", [128, dm], BF16)))
                  for i in range(3)]
            y1a = [TBuf(EC(nc.sbuf_tensor(f"y1a{i}", [128, st_w], BF16)))
                   for i in range(2)]
            y1r = [TBuf(EC(nc.sbuf_tensor(f"y1r{i}", [128, st_w], BF16)))
                   for i in range(2)]
            os_sb = [TBuf(EC(nc.sbuf_tensor(f"os{i}", [128, st_w], BF16)))
                     for i in range(2)]
            tf = [TBuf(EC(nc.sbuf_tensor(f"tf{i}", [128, st_w], BF16)))
                  for i in range(2)]
            fo = [TBuf(EC(nc.sbuf_tensor(f"fo{i}", [128, st_w], BF16)))
                  for i in range(2)]
            sqc = [TBuf(EC(nc.sbuf_tensor(f"sqc{i}", [128, st_w], BF16)))
                   for i in range(3)]
            tn2 = [TBuf(EC(nc.sbuf_tensor(f"tn2{i}", [128, st_w], BF16)))
                   for i in range(2)]
            y2s = [TBuf(EC(nc.sbuf_tensor(f"y2s{i}", [128, st_w], F32)))
                   for i in range(4)]

            out_ps = [TBuf(EC(nc.psum_tensor(f"out_ps{i}", [128, st_w], F32)))
                      for i in range(2)]
            is_ps = [TBuf(EC(nc.psum_tensor(f"is_ps{i}", [128, st_w], F32)))
                     for i in range(2)]
            os_ps = TBuf(EC(nc.psum_tensor("os_ps", [128, st_w], F32)))
            l2s1 = TBuf(EC(nc.psum_tensor("l2s1", [1, st_w], F32)))
            l2s2 = TBuf(EC(nc.psum_tensor("l2s2", [1, st_w], F32)))
            bc2_ps = TBuf(EC(nc.psum_tensor("bc2_ps", [128, st_w], F32)))

            def emit_xs(sti):
                s0x = sti * st_w
                par = sti % 2
                for e in range(ne):
                    ya = y1a[e % 2]
                    P.emit("sp", lambda s, e=e, ya=ya, s0x=s0x: s.dma_start(
                        ya.h[:, :], y1_scr.ap()[e][:, s0x:s0x + st_w]),
                        writes=[ya], chan=f"y1a{e % 2}")
                    ip = is_ps[e % 2]
                    P.emit("pe", lambda t, e=e, ip=ip, s0x=s0x: t.matmul(
                        ip.h[:, :], ICS.h[:, e * 128:(e + 1) * 128],
                        routing_t.h[:, s0x:s0x + st_w], start=True, stop=True),
                        reads=[ICS, routing_t], writes=[ip])
                    P.emit("dve", lambda v, e=e, ya=ya, ip=ip, par=par:
                           v.tensor_mul(
                               xs[par][e].h[:, :], ya.h[:, :], ip.h[:, :]),
                           reads=[ya, ip], writes=[xs[par][e]])

            def emit_norm(stp, d, tail=False):
                s0p = stp * st_w
                eng = "dve" if (tail and d % 2 == 1) else "pool"
                tn = tn2[d % 2]
                ys = y2s[d % 4]
                if eng == "pool":
                    P.emit("pool", lambda g, d=d, tn=tn: g.tensor_mul(
                        tn.h[:, :], z2[d].h[:, :], Br.h[:, 0:st_w]),
                        reads=[z2[d], Br], writes=[tn])
                    P.emit("pool", lambda g, tn=tn, ys=ys: g.tensor_sub(
                        ys.h[:, :], tn.h[:, :], Bm.h[:, 0:st_w]),
                        reads=[tn, Bm], writes=[ys])
                else:
                    P.emit("dve", lambda v, d=d, tn=tn: v.tensor_mul(
                        tn.h[:, :], z2[d].h[:, :], Br.h[:, 0:st_w]),
                        reads=[z2[d], Br], writes=[tn])
                    P.emit("dve", lambda v, tn=tn, ys=ys: v.tensor_sub(
                        ys.h[:, :], tn.h[:, :], Bm.h[:, 0:st_w]),
                        reads=[tn, Bm], writes=[ys])
                P.emit("pool", lambda g, d=d, ys=ys, s0p=s0p: g.dma_start(
                    out.ap()[d * 128:(d + 1) * 128, s0p:s0p + st_w],
                    ys.h[:, :]),
                    reads=[ys], chan="out")

            emit_xs(0)

            def emit_stats(d, zd, sq):
                P.emit("pe", lambda t, zd=zd, d=d: t.matmul(
                    l2s1.h[0:1, :], ones_bf.h[:, :], zd.h[:, :],
                    start=(d == 0), stop=(d == ne - 1)),
                    reads=[ones_bf, zd], writes=[l2s1])
                P.emit("pe", lambda t, sq=sq, d=d: t.matmul(
                    l2s2.h[0:1, :], ones_bf.h[:, :], sq.h[:, :],
                    start=(d == 0), stop=(d == ne - 1)),
                    reads=[ones_bf, sq], writes=[l2s2])

            for sti in range(nst):
                s0 = sti * st_w
                xcur = xs[sti % 2]
                pending_stats = []
                for d in range(ne):
                    wsb = WS[d % 3]
                    P.emit("sp", lambda s, d=d, wsb=wsb: s.dma_start(
                        wsb.h[:, :], ws_scr.ap()[d]),
                        writes=[wsb], chan=f"ws_ffn{d % 3}")
                    P.emit("pe", lambda t, d=d, s0=s0: t.matmul(
                        os_ps.h[:, :], OCS.h[:, d * 128:(d + 1) * 128],
                        routing_t.h[:, s0:s0 + st_w], start=True, stop=True),
                        reads=[OCS, routing_t], writes=[os_ps])
                    ob = os_sb[d % 2]
                    P.emit("act", lambda sc, ob=ob: sc.activation(
                        ob.h[:, :], os_ps.h[:, :], AF.Copy),
                        reads=[os_ps], writes=[ob])
                    op = out_ps[d % 2]
                    for e in range(ne):
                        P.emit("pe", lambda t, e=e, wsb=wsb, op=op, xcur=xcur:
                               t.matmul(
                                   op.h[:, :],
                                   wsb.h[:, e * 128:(e + 1) * 128],
                                   xcur[e].h[:, :],
                                   start=(e == 0), stop=(e == ne - 1)),
                               reads=[wsb, xcur[e]], writes=[op])
                    if pending_stats:
                        emit_stats(*pending_stats.pop())
                    yr = y1r[d % 2]
                    P.emit("sp", lambda s, d=d, yr=yr, s0=s0: s.dma_start(
                        yr.h[:, :], y1_scr.ap()[d][:, s0:s0 + st_w]),
                        writes=[yr], chan=f"y1r{d % 2}")
                    fb = fo[d % 2]
                    P.emit("act", lambda sc, fb=fb, op=op: sc.activation(
                        fb.h[:, :], op.h[:, :], AF.Copy),
                        reads=[op], writes=[fb])
                    tb = tf[d % 2]
                    P.emit("dve", lambda v, tb=tb, fb=fb, ob=ob: v.tensor_mul(
                        tb.h[:, :], fb.h[:, :], ob.h[:, :]),
                        reads=[fb, ob], writes=[tb])
                    P.emit("dve", lambda v, d=d, tb=tb, yr=yr:
                           v.scalar_tensor_tensor(
                               z2[d].h[:, :], tb.h[:, :], BFN.h[:, d:d + 1],
                               yr.h[:, :], ALU.add, ALU.add),
                           reads=[tb, BFN, yr], writes=[z2[d]])
                    sq = sqc[d % 3]
                    P.emit("act", lambda sc, d=d, sq=sq: sc.activation(
                        sq.h[:, :], z2[d].h[:, :], AF.Square),
                        reads=[z2[d]], writes=[sq])
                    pending_stats.append((d, z2[d], sq))
                    if d == 8 and sti + 1 < nst:
                        emit_xs(sti + 1)
                if pending_stats:
                    emit_stats(*pending_stats.pop())
                rA2 = rowpool.h[0:1, 0:st_w]
                rB2 = rowpool.h[0:1, seq:seq + st_w]
                P.emit("dve", lambda v, a=rA2: v.tensor_scalar_mul(
                    a, l2s1.h[0:1, :], 1.0 / dm),
                    reads=[l2s1], writes=[rowpool])
                P.emit("dve", lambda v, a=rA2, b=rB2: v.tensor_mul(b, a, a),
                       reads=[rowpool], writes=[rowpool])
                P.emit("dve", lambda v, b=rB2: v.tensor_scalar_add(b, b, -EPS),
                       reads=[rowpool], writes=[rowpool])
                P.emit("dve", lambda v, b=rB2: v.scalar_tensor_tensor(
                    b, l2s2.h[0:1, :], 1.0 / dm, b, ALU.mult, ALU.subtract),
                    reads=[l2s2, rowpool], writes=[rowpool])
                P.emit("act", lambda sc, b=rB2: sc.activation(b, b, AF.Ln),
                       reads=[rowpool], writes=[rowpool])
                P.emit("act", lambda sc, b=rB2: sc.activation(
                    b, b, AF.Exp, scale=-0.5),
                    reads=[rowpool], writes=[rowpool])
                P.emit("pe", lambda t, b=rB2: t.matmul(
                    bc2_ps.h[:, :], onesrow_f.h[:, :], b,
                    start=True, stop=True),
                    reads=[onesrow_f, rowpool], writes=[bc2_ps])
                P.emit("act", lambda sc: sc.activation(
                    Br.h[:, 0:st_w], bc2_ps.h[:, :], AF.Copy),
                    reads=[bc2_ps], writes=[Br])
                P.emit("dve", lambda v, a=rA2, b=rB2: v.tensor_mul(a, a, b),
                       reads=[rowpool], writes=[rowpool])
                P.emit("pe", lambda t, a=rA2: t.matmul(
                    bc2_ps.h[:, :], onesrow_f.h[:, :], a,
                    start=True, stop=True),
                    reads=[onesrow_f, rowpool], writes=[bc2_ps])
                P.emit("act", lambda sc: sc.activation(
                    Bm.h[:, 0:st_w], bc2_ps.h[:, :], AF.Copy),
                    reads=[bc2_ps], writes=[Bm])
                for d in range(ne):
                    emit_norm(sti, d, tail=(sti == nst - 1))

            P.barrier()
        P.replay()
    return nc


_CACHE = {}


def _get_program():
    if "nc" not in _CACHE:
        _CACHE["nc"] = build_program()
    return _CACHE["nc"]


def kernel(L0_attn_hidden_states, L0_attn_wq, L0_attn_bq, L0_attn_wk,
           L0_attn_bk, L0_attn_wv, L0_attn_bv, L0_attn_wl, L0_attn_bl,
           L0_ffn_weight, L0_ffn_bias, L0_ffn_gate_w, L0_ffn_in_channel_scale,
           L0_ffn_out_channel_scale):
    x = np.asarray(L0_attn_hidden_states, dtype=np.float32)

    def tile_w(w):
        return np.ascontiguousarray(
            w.T.reshape(NH, 128, NH, 128).transpose(2, 1, 0, 3)).astype(
                np.float32, copy=False)

    wqh = tile_w(np.asarray(L0_attn_wq, np.float32))
    wkh = tile_w(np.asarray(L0_attn_wk, np.float32))
    wvh = tile_w(np.asarray(L0_attn_wv, np.float32))
    wsh = tile_w(np.asarray(L0_ffn_weight, np.float32))
    gwh = np.ascontiguousarray(
        np.asarray(L0_ffn_gate_w, np.float32).T.reshape(NH, 128, 8)
        .transpose(1, 0, 2)).reshape(128, NH * 8)
    bqh = np.ascontiguousarray(np.asarray(L0_attn_bq, np.float32).reshape(NH, 128).T)
    bkh = np.ascontiguousarray(np.asarray(L0_attn_bk, np.float32).reshape(NH, 128).T)
    bvh = np.ascontiguousarray(np.asarray(L0_attn_bv, np.float32).reshape(NH, 128).T)
    bfh = np.ascontiguousarray(np.asarray(L0_ffn_bias, np.float32).reshape(NH, 128).T)
    icsa = np.ascontiguousarray(np.asarray(L0_ffn_in_channel_scale, np.float32))
    ocsa = np.ascontiguousarray(np.asarray(L0_ffn_out_channel_scale, np.float32))

    nc = _get_program()
    in_maps = []
    for c in range(N_CORES):
        in_maps.append({
            "xt": np.ascontiguousarray(x[c].T),
            "wqh": wqh, "wkh": wkh, "wvh": wvh, "wsh": wsh, "gwh": gwh,
            "bqh": bqh, "bkh": bkh, "bvh": bvh, "bfh": bfh,
            "ics": icsa, "ocs": ocsa,
        })
    res = run_bass_kernel_spmd(nc, in_maps, core_ids=list(range(N_CORES)))
    outs = [np.ascontiguousarray(r["out"].T) for r in res.results]
    return np.stack(outs, axis=0).astype(np.float32, copy=False)


# version 26
# speedup vs baseline: 1.0319x; 1.0061x over previous
import bisect
from contextlib import ExitStack

import numpy as np
import ml_dtypes

import concourse.bass as bass
import concourse.mybir as mybir
from concourse.bass_utils import run_bass_kernel_spmd

F32 = mybir.dt.float32
BF16 = mybir.dt.bfloat16
AF = mybir.ActivationFunctionType
ALU = mybir.AluOpType
AX = mybir.AxisListType

N_CORES = 8
SEQ = 2048
NH = 32
EPS = 1e-5
SCALE = float(1.0 / np.sqrt(np.float32(128.0)))

SAME_ENGINE_SYNC = True

_BLOCK_ATTR = {"pe": "tensor", "act": "scalar", "dve": "vector",
               "pool": "gpsimd", "sp": "sync"}


class TBuf:
    __slots__ = ("h", "w", "r", "name")

    def __init__(self, h, name=""):
        self.h = h
        self.name = name
        self.w = {}
        self.r = {}


class Chan:
    __slots__ = ("name", "sem", "count")

    def __init__(self, name, sem):
        self.name = name
        self.sem = sem
        self.count = 0


class Prog:
    ENGINES = ("pe", "act", "dve", "pool", "sp")

    def __init__(self, nc, ctx):
        self.nc = nc
        self.ops = {e: [] for e in self.ENGINES}
        self.n = {e: 0 for e in self.ENGINES}
        self.incs = {e: [] for e in self.ENGINES}
        self.inc_flags = {e: set() for e in self.ENGINES}
        self.wm = {e: {} for e in self.ENGINES}
        self.sems = {e: ctx.enter_context(nc.semaphore("s_" + e))
                     for e in self.ENGINES}
        self.chans = {}
        self._ctx = ctx

    def chan(self, name):
        if name not in self.chans:
            sem = self._ctx.enter_context(self.nc.semaphore("c_" + name))
            self.chans[name] = Chan(name, sem)
        return self.chans[name]

    def _inc_value(self, ename, pos):
        lst = self.incs[ename]
        idx = bisect.bisect_left(lst, pos)
        if idx < len(lst):
            return idx + 1
        assert pos <= self.n[ename], (ename, pos, self.n[ename])
        lst.append(pos)
        self.inc_flags[ename].add(pos)
        return len(lst)

    def _need(self, consumer, key, pos):
        if key[0] == "e":
            ename = key[1]
            if ename == consumer and (ename == "pe" or not SAME_ENGINE_SYNC):
                return
            val = self._inc_value(ename, pos)
        else:
            val = 16 * self.chans[key[1]].count
        if self.wm[consumer].get(key, 0) >= val:
            return
        self.ops[consumer].append(("w", key, val))
        self.wm[consumer][key] = val

    def emit(self, ename, build, reads=(), writes=(), chan=None):
        deps = {}
        for b in reads:
            for k, p in b.w.items():
                if deps.get(k, 0) < p:
                    deps[k] = p
        for b in writes:
            for k, p in b.w.items():
                if deps.get(k, 0) < p:
                    deps[k] = p
            for k, p in b.r.items():
                if deps.get(k, 0) < p:
                    deps[k] = p
        for k in sorted(deps, key=repr):
            self._need(ename, k, deps[k])
        if chan is None:
            self.n[ename] += 1
            pos = self.n[ename]
            self.ops[ename].append(("i", build, pos))
            tok_key, tok_pos = ("e", ename), pos
        else:
            c = self.chan(chan)
            c.count += 1
            pos = c.count
            self.ops[ename].append(("d", build, chan))
            tok_key, tok_pos = ("c", chan), pos
        for b in reads:
            if b.r.get(tok_key, 0) < tok_pos:
                b.r[tok_key] = tok_pos
        for b in writes:
            if b.w.get(tok_key, 0) < tok_pos:
                b.w[tok_key] = tok_pos
            b.r = {}
        return (tok_key, tok_pos)

    def barrier(self):
        snap = {("e", e): self.n[e] for e in self.ENGINES if self.n[e] > 0}
        for cname, ch in self.chans.items():
            if ch.count > 0:
                snap[("c", cname)] = ch.count
        for e in self.ENGINES:
            for k in sorted(snap, key=repr):
                if k == ("e", e):
                    continue
                self._need(e, k, snap[k])

    def replay(self):
        nc = self.nc
        with nc.Block() as block:
            for ename in self.ENGINES:
                ops = self.ops[ename]
                if not ops:
                    continue

                def runner(eng, ops=ops, ename=ename):
                    flags = self.inc_flags[ename]
                    sems = self.sems
                    chans = self.chans
                    for op in ops:
                        if op[0] == "w":
                            key = op[1]
                            sem = (sems[key[1]] if key[0] == "e"
                                   else chans[key[1]].sem)
                            eng.wait_ge(sem, op[2])
                        elif op[0] == "i":
                            inst = op[1](eng)
                            if op[2] in flags:
                                inst.then_inc(sems[ename], 1)
                        else:
                            inst = op[1](eng)
                            inst.then_inc(chans[op[2]].sem, 16)

                getattr(block, _BLOCK_ATTR[ename])(runner)


def build_program(seq=SEQ, nh=NH, debug=False):
    ne = nh
    dm = 128 * nh
    sl_w = min(512, seq)
    nsl = seq // sl_w
    st_w = min(512, seq)
    nst = seq // st_w
    SLB = min(256, seq)
    nslab = seq // SLB
    G = 4
    ngrp = SLB // G

    nc = bass.Bass("TRN2", target_bir_lowering=False)

    xt = nc.dram_tensor("xt", [dm, seq], F32, kind="ExternalInput")
    wqh = nc.dram_tensor("wqh", [nh, 128, ne, 128], F32, kind="ExternalInput")
    wkh = nc.dram_tensor("wkh", [nh, 128, ne, 128], F32, kind="ExternalInput")
    wvh = nc.dram_tensor("wvh", [nh, 128, ne, 128], F32, kind="ExternalInput")
    wsh = nc.dram_tensor("wsh", [ne, 128, ne, 128], F32, kind="ExternalInput")
    gwh = nc.dram_tensor("gwh", [128, ne * 8], F32, kind="ExternalInput")
    bqh = nc.dram_tensor("bqh", [128, nh], F32, kind="ExternalInput")
    bkh = nc.dram_tensor("bkh", [128, nh], F32, kind="ExternalInput")
    bvh = nc.dram_tensor("bvh", [128, nh], F32, kind="ExternalInput")
    bfh = nc.dram_tensor("bfh", [128, ne], F32, kind="ExternalInput")
    ics = nc.dram_tensor("ics", [8, dm], F32, kind="ExternalInput")
    ocs = nc.dram_tensor("ocs", [8, dm], F32, kind="ExternalInput")

    nslab_ = seq // min(256, seq)
    SLB_ = min(256, seq)
    q_scr = nc.dram_tensor("q_scr", [nslab_, nh, 128, SLB_], BF16, kind="Internal")
    k_scr = nc.dram_tensor("k_scr", [nslab_, nh, 128, SLB_], BF16, kind="Internal")
    v_scr = nc.dram_tensor("v_scr", [nslab_, nh, 128, SLB_], BF16, kind="Internal")
    c_scr = nc.dram_tensor("c_scr", [nh, 128, seq], BF16, kind="Internal")
    z_scr = nc.dram_tensor("z_scr", [ne, 128, seq], BF16, kind="Internal")
    y1_scr = nc.dram_tensor("y1_scr", [ne, 128, seq], BF16, kind="Internal")
    ws_scr = nc.dram_tensor("ws_scr", [ne, 128, dm], BF16, kind="Internal")

    out = nc.dram_tensor("out", [dm, seq], F32, kind="ExternalOutput")
    _p4 = np.kron(np.eye(4, dtype=np.float32),
                  np.ones((1, nh), dtype=np.float32)) * 32.0
    p4a_dram = nc.inline_tensor(_p4.astype(ml_dtypes.bfloat16), name="p4a_c")
    p4b_dram = nc.inline_tensor(
        np.concatenate([_p4, _p4], axis=1).astype(ml_dtypes.bfloat16),
        name="p4b_c")
    if debug:
        dbg_qkv = nc.dram_tensor("dbg_qkv", [3, nslab_, nh, 128, SLB_], BF16,
                                 kind="ExternalOutput")
        dbg_c = nc.dram_tensor("dbg_c", [nh, 128, seq], BF16,
                               kind="ExternalOutput")
        dbg_y1 = nc.dram_tensor("dbg_y1", [ne, 128, seq], BF16,
                                kind="ExternalOutput")
        dbg_rt = nc.dram_tensor("dbg_rt", [8, seq], BF16,
                                kind="ExternalOutput")

    ctx = ExitStack()
    with ctx:
        P = Prog(nc, ctx)
        E = ctx.enter_context

        ident_bf = TBuf(E(nc.sbuf_tensor("ident_bf", [128, 128], BF16)))
        ones_bf = TBuf(E(nc.sbuf_tensor("ones_bf", [128, 1], BF16)))
        onesrow_f = TBuf(E(nc.sbuf_tensor("onesrow_f", [1, 128], F32)))
        rowpool = TBuf(E(nc.sbuf_tensor("rowpool", [1, 2 * seq], F32)))
        BQ = TBuf(E(nc.sbuf_tensor("BQ", [128, nh], F32)))
        BK = TBuf(E(nc.sbuf_tensor("BK", [128, nh], F32)))
        BV = TBuf(E(nc.sbuf_tensor("BV", [128, nh], F32)))
        BFN = TBuf(E(nc.sbuf_tensor("BFN", [128, ne], F32)))
        GW = TBuf(E(nc.sbuf_tensor("GW", [128, ne * 8], BF16)))
        routing_t = TBuf(E(nc.sbuf_tensor("routing_t", [8, seq], BF16)))
        Bm = TBuf(E(nc.sbuf_tensor("Bm", [128, seq], BF16)))
        Br = TBuf(E(nc.sbuf_tensor("Br", [128, seq], BF16)))

        P.emit("pool", lambda g: g.memset(ident_bf.h[:, :], 0.0),
               writes=[ident_bf])
        P.emit("pool", lambda g: g.affine_select(
            out=ident_bf.h[:, :], in_=ident_bf.h[:, :],
            compare_op=ALU.not_equal, fill=1.0, base=0,
            pattern=[[-1, 128]], channel_multiplier=1),
            reads=[ident_bf], writes=[ident_bf])
        P.emit("pool", lambda g: g.memset(ones_bf.h[:, :], 1.0),
               writes=[ones_bf])
        P.emit("pool", lambda g: g.memset(onesrow_f.h[:, :], 1.0),
               writes=[onesrow_f])

        for mi, (buf, srct) in enumerate(
                ((BQ, bqh), (BK, bkh), (BV, bvh), (BFN, bfh))):
            P.emit("sp", lambda s, b=buf, t=srct: s.dma_start(b.h[:, :], t.ap()),
                   writes=[buf], chan=f"msc{mi}")
        P.emit("pool", lambda g: g.dma_start(GW.h[:, :], gwh.ap()),
               writes=[GW], chan="mgw")

        zfill_tb = TBuf(None, "zfill")
        ctxA0 = ExitStack()
        with ctxA0:
            EA = ctxA0.enter_context
            X = [TBuf(EA(nc.sbuf_tensor(f"X{c}", [128, seq], BF16)), f"X{c}")
                 for c in range(ne)]
            for c in range(ne):
                P.emit("pool", lambda g, c=c: g.dma_start(
                    X[c].h[:, :], xt.ap()[c * 128:(c + 1) * 128, :]),
                    writes=[X[c]], chan=f"x{c}")
            for c in range(ne):
                P.emit("sp", lambda s, c=c: s.dma_start(
                    z_scr.ap()[c], X[c].h[:, :]),
                    reads=[X[c]], writes=[zfill_tb], chan="zfill")

            wbufs = [TBuf(EA(nc.sbuf_tensor(f"W{i}", [128, dm], BF16)))
                     for i in range(2)]
            stage = [TBuf(EA(nc.sbuf_tensor(f"pst{i}", [128, seq], BF16)))
                     for i in range(3)]
            wsbufs = [TBuf(EA(nc.sbuf_tensor(f"wsbuf{i}", [128, dm // 2], BF16)))
                      for i in range(2)]
            proj_ps = [TBuf(EA(nc.psum_tensor(f"proj_ps{i}", [128, 512], F32)))
                       for i in range(2)]

            ws_tasks = [(h, hf) for h in range(ne) for hf in range(2)]

            def emit_ws_load(k):
                h, hf = ws_tasks[k]
                b = wsbufs[k % 2]
                P.emit("pool", lambda g, h=h, hf=hf, b=b: g.dma_start(
                    b.h[:, :],
                    wsh.ap()[h][:, hf * (ne // 2):(hf + 1) * (ne // 2), :]),
                    writes=[b], chan="ws_in")

            emit_ws_load(0)
            emit_ws_load(1)
            ws_k = 0

            wsrcs = (wqh, wkh, wvh)
            wdsts = (q_scr, k_scr, v_scr)
            wbias = (BQ, BK, BV)
            wload = [(h, wi) for h in range(nh) for wi in range(3)]

            def emit_wload(j):
                h, wi = wload[j]
                buf = wbufs[j % 2]
                P.emit("pool", lambda g, h=h, wi=wi, b=buf: g.dma_start(
                    b.h[:, :], wsrcs[wi].ap()[h]),
                    writes=[buf], chan=f"wl{j % 2}")

            emit_wload(0)
            if len(wload) > 1:
                emit_wload(1)
            next_load = 2

            for j, (h, wi) in enumerate(wload):
                wb = wbufs[j % 2]
                st = stage[j % 3]
                for s4 in range(nsl):
                    pp = proj_ps[(j * nsl + s4) % 2]
                    for e in range(ne):
                        P.emit("pe", lambda t, wb=wb, e=e, s4=s4, pp=pp: t.matmul(
                            pp.h[:, 0:sl_w],
                            wb.h[:, e * 128:(e + 1) * 128],
                            X[e].h[:, s4 * sl_w:(s4 + 1) * sl_w],
                            start=(e == 0), stop=(e == ne - 1)),
                            reads=[wb, X[e]], writes=[pp])
                    P.emit("act", lambda sc, st=st, s4=s4, wi=wi, h=h, pp=pp:
                           sc.activation(
                               st.h[:, s4 * sl_w:(s4 + 1) * sl_w],
                               pp.h[:, 0:sl_w], AF.Identity,
                               bias=wbias[wi].h[:, h:h + 1]),
                           reads=[pp, wbias[wi]], writes=[st])
                P.emit("sp", lambda s, st=st, wi=wi, h=h: s.dma_start(
                    wdsts[wi].ap()[:, h, :, :].transpose([1, 0, 2]),
                    st.h[:, :].rearrange("d (sl s) -> d sl s", s=SLB)),
                    reads=[st], chan="qkv_out")
                if next_load < len(wload):
                    emit_wload(next_load)
                    next_load += 1
                if wi == 0 and h < ne:
                    for _ in range(2):
                        wh, whf = ws_tasks[ws_k]
                        b = wsbufs[ws_k % 2]
                        P.emit("act", lambda sc, b=b: sc.sign(
                            b.h[:, :], b.h[:, :]),
                            reads=[b], writes=[b])
                        P.emit("sp", lambda s, wh=wh, whf=whf, b=b: s.dma_start(
                            ws_scr.ap()[wh][:, whf * (dm // 2):(whf + 1) * (dm // 2)],
                            b.h[:, :]),
                            reads=[b], chan="ws_out")
                        if ws_k + 2 < len(ws_tasks):
                            emit_ws_load(ws_k + 2)
                        ws_k += 1
            if debug:
                P.barrier()
                for wi in range(3):
                    P.emit("sp", lambda s, wi=wi: s.dma_start(
                        dbg_qkv.ap()[wi], wdsts[wi].ap()), chan="dbg")

        P.barrier()
        ctxA1 = ExitStack()
        with ctxA1:
            EA = ctxA1.enter_context
            Qs = [TBuf(EA(nc.sbuf_tensor(f"Qs{i}", [128, ne * SLB], BF16)))
                  for i in range(2)]
            Ks = [TBuf(EA(nc.sbuf_tensor(f"Ks{i}", [128, ne * SLB], BF16)))
                  for i in range(2)]
            Vs = [TBuf(EA(nc.sbuf_tensor(f"Vs{i}", [128, ne * SLB], BF16)))
                  for i in range(2)]
            SRW = [TBuf(EA(nc.sbuf_tensor(f"SRW{p}", [128, ne * SLB], BF16)))
                   for p in range(2)]
            WIDX = {"q": 0, "k": 1, "v": 0}
            gdim = G * ne
            CTXF_ts = [EA(nc.sbuf_tensor(f"CTXF{i}", [128, ne * SLB], BF16))
                       for i in range(2)]
            CTXFbs = [[TBuf(CTXF_ts[i], f"ctxf{i}A"), TBuf(CTXF_ts[i], f"ctxf{i}B")]
                      for i in range(2)]
            U4 = [TBuf(EA(nc.sbuf_tensor(f"U4_{i}", [gdim, 2 * gdim], BF16)))
                  for i in range(3)]
            Vts4 = [TBuf(EA(nc.sbuf_tensor(f"Vts4_{i}", [gdim, 256], BF16)))
                    for i in range(3)]
            ctxn = [TBuf(EA(nc.sbuf_tensor(f"ctxn{i}", [gdim, 256], BF16)))
                    for i in range(3)]
            rden = [TBuf(EA(nc.sbuf_tensor(f"rden{i}", [gdim, 2], F32)))
                    for i in range(2)]
            negrow = TBuf(EA(nc.sbuf_tensor("negrow", [1, gdim], BF16)))
            ones2 = TBuf(EA(nc.sbuf_tensor("ones2", [1, 2 * gdim], BF16)))
            p4a = TBuf(EA(nc.sbuf_tensor("p4a", [4, gdim], BF16)))
            p4b = TBuf(EA(nc.sbuf_tensor("p4b", [4, 2 * gdim], BF16)))

            gram_ps = [TBuf(EA(nc.psum_tensor(f"gram{i}", [gdim, 2 * gdim], F32)))
                       for i in range(2)]
            vts_ps = TBuf(EA(nc.psum_tensor("vtsp0", [gdim, 256], BF16)))
            ctxT_ps = [TBuf(EA(nc.psum_tensor(f"ctxT{i}", [gdim, 256], F32)))
                       for i in range(2)]
            den_ps = TBuf(EA(nc.psum_tensor("denp", [gdim, 2], F32)))
            ctf_ps = [TBuf(EA(nc.psum_tensor(f"ctf{i}", [128, 2 * gdim], BF16)))
                      for i in range(2)]
            P.emit("pool", lambda g: g.memset(negrow.h[:, :], -1024.0),
                   writes=[negrow])
            P.emit("pool", lambda g: g.memset(ones2.h[:, :], 1.0),
                   writes=[ones2])
            P.emit("sp", lambda s: s.dma_start(p4a.h[:, :], p4a_dram.ap()),
                   writes=[p4a], chan="mmask")
            P.emit("sp", lambda s: s.dma_start(p4b.h[:, :], p4b_dram.ap()),
                   writes=[p4b], chan="mmask")
            def load_one(sl, which):
                buf = SRW[WIDX[which]]
                scr = {"q": q_scr, "k": k_scr, "v": v_scr}[which]
                P.emit("sp", lambda s, b=buf, scr=scr, sl=sl: s.dma_start(
                    b.h[:, :].rearrange("d (t s) -> d t s", s=SLB),
                    scr.ap()[sl].transpose([1, 0, 2])),
                    writes=[buf], chan=which + "sl" + str(sl % 2))

            def rearr_piece(sl, which, i, npc=4):
                stg = SRW[WIDX[which]]
                dst = {"q": Qs, "k": Ks, "v": Vs}[which][sl % 2]
                w = SLB // npc
                P.emit("dve", lambda v, b=dst, stg=stg, i=i, w=w: v.tensor_copy(
                    b.h[:, i * w * ne:(i + 1) * w * ne]
                    .rearrange("d (s t) -> d s t", t=ne),
                    stg.h[:, :].rearrange("d (t s) -> d s t", s=SLB)
                    [:, i * w:(i + 1) * w, :]),
                    reads=[stg], writes=[dst])

            def rearr_one(sl, which):
                for i in range(4):
                    rearr_piece(sl, which, i)

            for w in ("q", "k"):
                load_one(0, w)
            for w in ("q", "k"):
                rearr_one(0, w)
            load_one(0, "v")
            rearr_one(0, "v")
            for sl in range(nslab):
                s0 = sl * SLB
                qb_, kb_, vb_ = Qs[sl % 2], Ks[sl % 2], Vs[sl % 2]
                CTXF_t = CTXF_ts[sl % 2]
                CTXFb = CTXFbs[sl % 2]

                def gview(b, g):
                    return b.h[:, g * G * ne:(g + 1) * G * ne]

                def rview(b, g):
                    return (b.h[:, :].rearrange("d (t s) -> d s t", s=SLB)
                            [:, g * G:(g + 1) * G, :])

                nmac = ngrp // 2

                def stage1(m):
                    gp = gram_ps[m % 2]
                    P.emit("pe", lambda t, gp=gp: t.matmul(
                        gp.h[:, :], negrow.h[:, :], ones2.h[:, :],
                        start=True, stop=False),
                        reads=[negrow, ones2], writes=[gp])
                    P.emit("pe", lambda t, gp=gp: t.matmul(
                        gp.h[:, :], p4a.h[:, :], p4b.h[:, :],
                        start=False, stop=False),
                        reads=[p4a, p4b], writes=[gp])
                    for k in (0, 1):
                        g = 2 * m + k
                        P.emit("pe", lambda t, g=g, k=k, vb_=vb_: t.transpose(
                            vts_ps.h[:, k * 128:(k + 1) * 128],
                            gview(vb_, g), ident_bf.h[:, :]),
                            reads=[vb_, ident_bf], writes=[vts_ps])
                        P.emit("pe", lambda t, g=g, k=k, gp=gp, kb_=kb_,
                               qb_=qb_: t.matmul(
                                   gp.h[:, k * gdim:(k + 1) * gdim],
                                   gview(kb_, g), gview(qb_, g),
                                   start=False, stop=(k == 1)),
                               reads=[kb_, qb_], writes=[gp])
                    ub = U4[m % 3]
                    P.emit("act", lambda sc, ub=ub, gp=gp: sc.activation(
                        ub.h[:, :], gp.h[:, :], AF.Exp, scale=SCALE),
                        reads=[gp], writes=[ub])
                    vb4 = Vts4[m % 3]
                    P.emit("act", lambda sc, vb4=vb4: sc.activation(
                        vb4.h[:, :], vts_ps.h[:, :], AF.Copy),
                        reads=[vts_ps], writes=[vb4])

                def stage2(p):
                    um, vb4 = U4[p % 3], Vts4[p % 3]
                    ctp = ctxT_ps[p % 2]
                    for k in (0, 1):
                        P.emit("pe", lambda t, k=k, um=um, vb4=vb4, ctp=ctp:
                               t.matmul(
                                   ctp.h[:, k * 128:(k + 1) * 128],
                                   um.h[:, k * gdim:(k + 1) * gdim],
                                   vb4.h[:, k * 128:(k + 1) * 128],
                                   start=True, stop=True),
                               reads=[um, vb4], writes=[ctp])
                        P.emit("pe", lambda t, k=k, um=um: t.matmul(
                            den_ps.h[:, k:k + 1],
                            um.h[:, k * gdim:(k + 1) * gdim],
                            ones_bf.h[0:gdim, 0:1],
                            start=True, stop=True),
                            reads=[um, ones_bf], writes=[den_ps])
                    rd = rden[p % 2]
                    P.emit("dve", lambda v, rd=rd: v.reciprocal(
                        rd.h[:, :], den_ps.h[:, :]),
                        reads=[den_ps], writes=[rd])
                    cn_ = ctxn[p % 3]
                    for k in (0, 1):
                        P.emit("dve", lambda v, k=k, cn_=cn_, ctp=ctp, rd=rd:
                               v.tensor_scalar_mul(
                                   cn_.h[:, k * 128:(k + 1) * 128],
                                   ctp.h[:, k * 128:(k + 1) * 128],
                                   rd.h[:, k:k + 1]),
                               reads=[ctp, rd], writes=[cn_])

                def stage3(p):
                    cn_ = ctxn[p % 3]
                    cf = ctf_ps[p % 2]
                    for k in (0, 1):
                        P.emit("pe", lambda t, k=k, cn_=cn_, cf=cf: t.transpose(
                            cf.h[:, k * gdim:(k + 1) * gdim],
                            cn_.h[:, k * 128:(k + 1) * 128],
                            ident_bf.h[0:gdim, 0:gdim]),
                            reads=[cn_, ident_bf], writes=[cf])
                    ctb = CTXFb[p % 2]
                    P.emit("act", lambda sc, p=p, cf=cf, CTXF_t=CTXF_t:
                        sc.activation(
                        CTXF_t[:, :].rearrange("d (h s) -> d s h", s=SLB)
                        [:, p * 2 * G:(p + 1) * 2 * G, :],
                        cf.h[:, :].rearrange("d (s h) -> d s h", h=ne),
                        AF.Copy),
                        reads=[cf], writes=[ctb])

                for m in range(nmac + 4):
                    if m < nmac:
                        stage1(m)
                    if 2 <= m < nmac + 2:
                        stage2(m - 2)
                    if 4 <= m:
                        stage3(m - 4)
                    if sl + 1 < nslab:
                        if m == 0:
                            load_one(sl + 1, "q")
                            load_one(sl + 1, "k")
                        if 13 <= m < 17:
                            rearr_piece(sl + 1, "q", m - 13)
                        if m == 17:
                            load_one(sl + 1, "v")
                        if 17 <= m < 21:
                            rearr_piece(sl + 1, "k", m - 17)
                        if 29 <= m < 33:
                            rearr_piece(sl + 1, "v", m - 29)

                P.emit("pool", lambda g, s0=s0, CTXF_t=CTXF_t:
                    g.dma_start(
                        z_scr.ap()[:, :, s0:s0 + SLB].transpose([1, 0, 2]),
                        CTXF_t[:, :],
                        accum_op=ALU.add),
                    reads=[CTXFb[0], CTXFb[1], zfill_tb],
                    chan=f"c_out{sl % 2}")

        P.barrier()
        ctxB = ExitStack()
        with ctxB:
            EB = ctxB.enter_context
            Z = [TBuf(EB(nc.sbuf_tensor(f"Z{c}", [128, seq], BF16)))
                 for c in range(ne)]
            sqb = [TBuf(EB(nc.sbuf_tensor(f"sqb{i}", [128, seq], BF16)))
                   for i in range(2)]
            tnb = [TBuf(EB(nc.sbuf_tensor(f"tnb{i}", [128, seq], BF16)))
                   for i in range(2)]

            ctxB1 = ExitStack()
            with ctxB1:
                sum1 = TBuf(ctxB1.enter_context(
                    nc.psum_tensor("sum1", [1, seq], F32)))
                sum2 = TBuf(ctxB1.enter_context(
                    nc.psum_tensor("sum2", [1, seq], F32)))
                for c in range(ne):
                    P.emit("sp", lambda s, c=c: s.dma_start(
                        Z[c].h[:, :], z_scr.ap()[c]),
                        writes=[Z[c]], chan=f"zin{c % 4}")
                    sq = sqb[c % 2]
                    P.emit("act", lambda sc, c=c, sq=sq: sc.activation(
                        sq.h[:, :], Z[c].h[:, :], AF.Square),
                        reads=[Z[c]], writes=[sq])
                    for s4 in range(nsl):
                        P.emit("pe", lambda t, c=c, s4=s4: t.matmul(
                            sum1.h[0:1, s4 * sl_w:(s4 + 1) * sl_w],
                            ones_bf.h[:, :],
                            Z[c].h[:, s4 * sl_w:(s4 + 1) * sl_w],
                            start=(c == 0), stop=(c == ne - 1)),
                            reads=[ones_bf, Z[c]], writes=[sum1])
                        P.emit("pe", lambda t, sq=sq, s4=s4, c=c: t.matmul(
                            sum2.h[0:1, s4 * sl_w:(s4 + 1) * sl_w],
                            ones_bf.h[:, :],
                            sq.h[:, s4 * sl_w:(s4 + 1) * sl_w],
                            start=(c == 0), stop=(c == ne - 1)),
                            reads=[ones_bf, sq], writes=[sum2])
                rA = rowpool.h[0:1, 0:seq]
                rB = rowpool.h[0:1, seq:2 * seq]
                P.emit("dve", lambda v, a=rA: v.tensor_scalar_mul(
                    a, sum1.h[0:1, :], 1.0 / dm),
                    reads=[sum1], writes=[rowpool])
                P.emit("dve", lambda v, a=rA, b=rB: v.tensor_mul(b, a, a),
                       reads=[rowpool], writes=[rowpool])
                P.emit("dve", lambda v, b=rB: v.tensor_scalar_add(b, b, -EPS),
                       reads=[rowpool], writes=[rowpool])
                P.emit("dve", lambda v, b=rB: v.scalar_tensor_tensor(
                    b, sum2.h[0:1, :], 1.0 / dm, b, ALU.mult, ALU.subtract),
                    reads=[sum2, rowpool], writes=[rowpool])
                P.emit("act", lambda sc, b=rB: sc.activation(b, b, AF.Ln),
                       reads=[rowpool], writes=[rowpool])
                P.emit("act", lambda sc, b=rB: sc.activation(
                    b, b, AF.Exp, scale=-0.5),
                    reads=[rowpool], writes=[rowpool])
            P.barrier()

            ctxB2 = ExitStack()
            with ctxB2:
                ctxB2p = ExitStack()
                bcB_ps = TBuf(ctxB2p.enter_context(
                    nc.psum_tensor("bcB_ps", [128, seq], F32)))
                lg_ps = TBuf(ctxB2p.enter_context(
                    nc.psum_tensor("lg_ps", [8, seq], F32)))
                for s4 in range(nsl):
                    P.emit("pe", lambda t, s4=s4: t.matmul(
                        bcB_ps.h[:, s4 * sl_w:(s4 + 1) * sl_w],
                        onesrow_f.h[:, :],
                        rowpool.h[0:1, seq + s4 * sl_w:seq + (s4 + 1) * sl_w],
                        start=True, stop=True),
                        reads=[onesrow_f, rowpool], writes=[bcB_ps])
                P.emit("act", lambda sc: sc.activation(
                    Br.h[:, :], bcB_ps.h[:, :], AF.Copy),
                    reads=[bcB_ps], writes=[Br])
                P.emit("dve", lambda v: v.tensor_mul(
                    rowpool.h[0:1, 0:seq], rowpool.h[0:1, 0:seq],
                    rowpool.h[0:1, seq:2 * seq]),
                    reads=[rowpool], writes=[rowpool])
                for s4 in range(nsl):
                    P.emit("pe", lambda t, s4=s4: t.matmul(
                        bcB_ps.h[:, s4 * sl_w:(s4 + 1) * sl_w],
                        onesrow_f.h[:, :],
                        rowpool.h[0:1, s4 * sl_w:(s4 + 1) * sl_w],
                        start=True, stop=True),
                        reads=[onesrow_f, rowpool], writes=[bcB_ps])
                P.emit("act", lambda sc: sc.activation(
                    Bm.h[:, :], bcB_ps.h[:, :], AF.Copy),
                    reads=[bcB_ps], writes=[Bm])
                for c in range(ne):
                    tn = tnb[c % 2]
                    P.emit("dve", lambda v, c=c, tn=tn: v.tensor_mul(
                        tn.h[:, :], Z[c].h[:, :], Br.h[:, :]),
                        reads=[Z[c], Br], writes=[tn])
                    P.emit("dve", lambda v, c=c, tn=tn: v.tensor_sub(
                        Z[c].h[:, :], tn.h[:, :], Bm.h[:, :]),
                        reads=[tn, Bm], writes=[Z[c]])
                    P.emit("sp", lambda s, c=c: s.dma_start(
                        y1_scr.ap()[c], Z[c].h[:, :]),
                        reads=[Z[c]], chan="y1_out")
                    for s4 in range(nsl):
                        P.emit("pe", lambda t, c=c, s4=s4: t.matmul(
                            lg_ps.h[0:8, s4 * sl_w:(s4 + 1) * sl_w],
                            GW.h[:, c * 8:(c + 1) * 8],
                            Z[c].h[:, s4 * sl_w:(s4 + 1) * sl_w],
                            start=(c == 0), stop=(c == ne - 1)),
                            reads=[GW, Z[c]], writes=[lg_ps])
                lg_sb = TBuf(ctxB2.enter_context(
                    nc.sbuf_tensor("lg_sb", [8, seq], BF16)))
                P.emit("act", lambda sc: sc.activation(
                    lg_sb.h[:, :], lg_ps.h[:, :], AF.Copy),
                    reads=[lg_ps], writes=[lg_sb])
                nt = seq // 128
                ltm = TBuf(ctxB2.enter_context(
                    nc.sbuf_tensor("ltm", [128, nt * 8], BF16)))
                prob = TBuf(ctxB2.enter_context(
                    nc.sbuf_tensor("prob", [128, nt * 8], F32)))
                mx = TBuf(ctxB2.enter_context(
                    nc.sbuf_tensor("mx", [128, nt], F32)))
                den = TBuf(ctxB2.enter_context(
                    nc.sbuf_tensor("den", [128, nt], F32)))
                rtm = TBuf(ctxB2.enter_context(
                    nc.sbuf_tensor("rtm", [128, nt * 8], BF16)))
                P.barrier()
                ctxB2p.close()
                ctxB3 = ExitStack()
                with ctxB3:
                    lt_ps = TBuf(ctxB3.enter_context(
                        nc.psum_tensor("lt_ps", [128, nt * 8], BF16)))
                    rt_ps = TBuf(ctxB3.enter_context(
                        nc.psum_tensor("rt_ps", [8, seq], BF16)))
                    for tt in range(nt):
                        P.emit("pe", lambda t, tt=tt: t.transpose(
                            lt_ps.h[:, tt * 8:(tt + 1) * 8],
                            lg_sb.h[0:8, tt * 128:(tt + 1) * 128],
                            ident_bf.h[0:8, 0:8]),
                            reads=[lg_sb, ident_bf], writes=[lt_ps])
                    P.emit("dve", lambda v: v.tensor_copy(
                        ltm.h[:, :], lt_ps.h[:, :]),
                        reads=[lt_ps], writes=[ltm])
                    P.emit("dve", lambda v: v.reduce_max(
                        mx.h[:, :],
                        ltm.h[:, :].rearrange("p (g x) -> p g x", x=8),
                        axis=AX.X),
                        reads=[ltm], writes=[mx])
                    P.emit("dve", lambda v: v.tensor_scalar_mul(
                        mx.h[:, :], mx.h[:, :], -1.0),
                        reads=[mx], writes=[mx])
                    for tt in range(nt):
                        P.emit("act", lambda sc, tt=tt: sc.activation(
                            prob.h[:, tt * 8:(tt + 1) * 8],
                            ltm.h[:, tt * 8:(tt + 1) * 8], AF.Exp,
                            bias=mx.h[:, tt:tt + 1],
                            accum_out=den.h[:, tt:tt + 1]),
                            reads=[ltm, mx], writes=[prob, den])
                    P.emit("dve", lambda v: v.reciprocal(
                        den.h[:, :], den.h[:, :]),
                        reads=[den], writes=[den])
                    for tt in range(nt):
                        P.emit("dve", lambda v, tt=tt: v.tensor_scalar_mul(
                            rtm.h[:, tt * 8:(tt + 1) * 8],
                            prob.h[:, tt * 8:(tt + 1) * 8],
                            den.h[:, tt:tt + 1]),
                            reads=[prob, den], writes=[rtm])
                    for tt in range(nt):
                        P.emit("pe", lambda t, tt=tt: t.transpose(
                            rt_ps.h[0:8, tt * 128:(tt + 1) * 128],
                            rtm.h[:, tt * 8:(tt + 1) * 8],
                            ident_bf.h[:, :]),
                            reads=[rtm, ident_bf], writes=[rt_ps])
                    P.emit("dve", lambda v: v.tensor_copy(
                        routing_t.h[:, :], rt_ps.h[:, :]),
                        reads=[rt_ps], writes=[routing_t])
                    if debug:
                        P.emit("sp", lambda s: s.dma_start(
                            dbg_rt.ap(), routing_t.h[:, :]),
                            reads=[routing_t], chan="dbg")
                        P.emit("sp", lambda s: s.dma_start(
                            dbg_y1.ap(), y1_scr.ap()), chan="dbg")
                        P.emit("sp", lambda s: s.dma_start(
                            dbg_c.ap(), c_scr.ap()), chan="dbg")
                    P.barrier()

        ctxC = ExitStack()
        with ctxC:
            EC = ctxC.enter_context
            ICS = TBuf(EC(nc.sbuf_tensor("ICS", [8, dm], BF16)))
            OCS = TBuf(EC(nc.sbuf_tensor("OCS", [8, dm], BF16)))
            P.emit("pool", lambda g: g.dma_start(ICS.h[:, :], ics.ap()),
                   writes=[ICS], chan="mics")
            P.emit("pool", lambda g: g.dma_start(OCS.h[:, :], ocs.ap()),
                   writes=[OCS], chan="mocs")
            xs = [[TBuf(EC(nc.sbuf_tensor(f"xs{p}_{e}", [128, st_w], BF16)))
                   for e in range(ne)] for p in range(2)]
            z2 = [TBuf(EC(nc.sbuf_tensor(f"z2{d}", [128, st_w], BF16)))
                  for d in range(ne)]
            WS = [TBuf(EC(nc.sbuf_tensor(f"WS{i}", [128, dm], BF16)))
                  for i in range(3)]
            y1a = [TBuf(EC(nc.sbuf_tensor(f"y1a{i}", [128, st_w], BF16)))
                   for i in range(2)]
            y1r = [TBuf(EC(nc.sbuf_tensor(f"y1r{i}", [128, st_w], BF16)))
                   for i in range(2)]
            os_sb = [TBuf(EC(nc.sbuf_tensor(f"os{i}", [128, st_w], BF16)))
                     for i in range(2)]
            tf = [TBuf(EC(nc.sbuf_tensor(f"tf{i}", [128, st_w], BF16)))
                  for i in range(2)]
            fo = [TBuf(EC(nc.sbuf_tensor(f"fo{i}", [128, st_w], BF16)))
                  for i in range(2)]
            sqc = [TBuf(EC(nc.sbuf_tensor(f"sqc{i}", [128, st_w], BF16)))
                   for i in range(3)]
            tn2 = [TBuf(EC(nc.sbuf_tensor(f"tn2{i}", [128, st_w], BF16)))
                   for i in range(2)]
            y2s = [TBuf(EC(nc.sbuf_tensor(f"y2s{i}", [128, st_w], F32)))
                   for i in range(4)]

            out_ps = [TBuf(EC(nc.psum_tensor(f"out_ps{i}", [128, st_w], F32)))
                      for i in range(2)]
            is_ps = [TBuf(EC(nc.psum_tensor(f"is_ps{i}", [128, st_w], F32)))
                     for i in range(2)]
            os_ps = TBuf(EC(nc.psum_tensor("os_ps", [128, st_w], F32)))
            l2s1 = TBuf(EC(nc.psum_tensor("l2s1", [1, st_w], F32)))
            l2s2 = TBuf(EC(nc.psum_tensor("l2s2", [1, st_w], F32)))
            bc2_ps = TBuf(EC(nc.psum_tensor("bc2_ps", [128, st_w], F32)))

            def emit_xs(sti):
                s0x = sti * st_w
                par = sti % 2
                for e in range(ne):
                    ya = y1a[e % 2]
                    P.emit("sp", lambda s, e=e, ya=ya, s0x=s0x: s.dma_start(
                        ya.h[:, :], y1_scr.ap()[e][:, s0x:s0x + st_w]),
                        writes=[ya], chan=f"y1a{e % 2}")
                    ip = is_ps[e % 2]
                    P.emit("pe", lambda t, e=e, ip=ip, s0x=s0x: t.matmul(
                        ip.h[:, :], ICS.h[:, e * 128:(e + 1) * 128],
                        routing_t.h[:, s0x:s0x + st_w], start=True, stop=True),
                        reads=[ICS, routing_t], writes=[ip])
                    P.emit("dve", lambda v, e=e, ya=ya, ip=ip, par=par:
                           v.tensor_mul(
                               xs[par][e].h[:, :], ya.h[:, :], ip.h[:, :]),
                           reads=[ya, ip], writes=[xs[par][e]])

            def emit_norm(stp, d, tail=False):
                s0p = stp * st_w
                eng = "dve" if (tail and d % 2 == 1) else "pool"
                tn = tn2[d % 2]
                ys = y2s[d % 4]
                if eng == "pool":
                    P.emit("pool", lambda g, d=d, tn=tn: g.tensor_mul(
                        tn.h[:, :], z2[d].h[:, :], Br.h[:, 0:st_w]),
                        reads=[z2[d], Br], writes=[tn])
                    P.emit("pool", lambda g, tn=tn, ys=ys: g.tensor_sub(
                        ys.h[:, :], tn.h[:, :], Bm.h[:, 0:st_w]),
                        reads=[tn, Bm], writes=[ys])
                else:
                    P.emit("dve", lambda v, d=d, tn=tn: v.tensor_mul(
                        tn.h[:, :], z2[d].h[:, :], Br.h[:, 0:st_w]),
                        reads=[z2[d], Br], writes=[tn])
                    P.emit("dve", lambda v, tn=tn, ys=ys: v.tensor_sub(
                        ys.h[:, :], tn.h[:, :], Bm.h[:, 0:st_w]),
                        reads=[tn, Bm], writes=[ys])
                P.emit("pool", lambda g, d=d, ys=ys, s0p=s0p: g.dma_start(
                    out.ap()[d * 128:(d + 1) * 128, s0p:s0p + st_w],
                    ys.h[:, :]),
                    reads=[ys], chan="out")

            emit_xs(0)

            def emit_stats(d, zd, sq):
                P.emit("pe", lambda t, zd=zd, d=d: t.matmul(
                    l2s1.h[0:1, :], ones_bf.h[:, :], zd.h[:, :],
                    start=(d == 0), stop=(d == ne - 1)),
                    reads=[ones_bf, zd], writes=[l2s1])
                P.emit("pe", lambda t, sq=sq, d=d: t.matmul(
                    l2s2.h[0:1, :], ones_bf.h[:, :], sq.h[:, :],
                    start=(d == 0), stop=(d == ne - 1)),
                    reads=[ones_bf, sq], writes=[l2s2])

            for sti in range(nst):
                s0 = sti * st_w
                xcur = xs[sti % 2]
                pending_stats = []
                for d in range(ne):
                    wsb = WS[d % 3]
                    P.emit("sp", lambda s, d=d, wsb=wsb: s.dma_start(
                        wsb.h[:, :], ws_scr.ap()[d]),
                        writes=[wsb], chan=f"ws_ffn{d % 3}")
                    P.emit("pe", lambda t, d=d, s0=s0: t.matmul(
                        os_ps.h[:, :], OCS.h[:, d * 128:(d + 1) * 128],
                        routing_t.h[:, s0:s0 + st_w], start=True, stop=True),
                        reads=[OCS, routing_t], writes=[os_ps])
                    ob = os_sb[d % 2]
                    P.emit("act", lambda sc, ob=ob: sc.activation(
                        ob.h[:, :], os_ps.h[:, :], AF.Copy),
                        reads=[os_ps], writes=[ob])
                    op = out_ps[d % 2]
                    for e in range(ne):
                        P.emit("pe", lambda t, e=e, wsb=wsb, op=op, xcur=xcur:
                               t.matmul(
                                   op.h[:, :],
                                   wsb.h[:, e * 128:(e + 1) * 128],
                                   xcur[e].h[:, :],
                                   start=(e == 0), stop=(e == ne - 1)),
                               reads=[wsb, xcur[e]], writes=[op])
                    if pending_stats:
                        emit_stats(*pending_stats.pop())
                    yr = y1r[d % 2]
                    P.emit("sp", lambda s, d=d, yr=yr, s0=s0: s.dma_start(
                        yr.h[:, :], y1_scr.ap()[d][:, s0:s0 + st_w]),
                        writes=[yr], chan=f"y1r{d % 2}")
                    fb = fo[d % 2]
                    P.emit("act", lambda sc, fb=fb, op=op: sc.activation(
                        fb.h[:, :], op.h[:, :], AF.Copy),
                        reads=[op], writes=[fb])
                    tb = tf[d % 2]
                    P.emit("dve", lambda v, tb=tb, fb=fb, ob=ob: v.tensor_mul(
                        tb.h[:, :], fb.h[:, :], ob.h[:, :]),
                        reads=[fb, ob], writes=[tb])
                    P.emit("dve", lambda v, d=d, tb=tb, yr=yr:
                           v.scalar_tensor_tensor(
                               z2[d].h[:, :], tb.h[:, :], BFN.h[:, d:d + 1],
                               yr.h[:, :], ALU.add, ALU.add),
                           reads=[tb, BFN, yr], writes=[z2[d]])
                    sq = sqc[d % 3]
                    P.emit("act", lambda sc, d=d, sq=sq: sc.activation(
                        sq.h[:, :], z2[d].h[:, :], AF.Square),
                        reads=[z2[d]], writes=[sq])
                    pending_stats.append((d, z2[d], sq))
                    if d == 8 and sti + 1 < nst:
                        emit_xs(sti + 1)
                if pending_stats:
                    emit_stats(*pending_stats.pop())
                rA2 = rowpool.h[0:1, 0:st_w]
                rB2 = rowpool.h[0:1, seq:seq + st_w]
                P.emit("dve", lambda v, a=rA2: v.tensor_scalar_mul(
                    a, l2s1.h[0:1, :], 1.0 / dm),
                    reads=[l2s1], writes=[rowpool])
                P.emit("dve", lambda v, a=rA2, b=rB2: v.tensor_mul(b, a, a),
                       reads=[rowpool], writes=[rowpool])
                P.emit("dve", lambda v, b=rB2: v.tensor_scalar_add(b, b, -EPS),
                       reads=[rowpool], writes=[rowpool])
                P.emit("dve", lambda v, b=rB2: v.scalar_tensor_tensor(
                    b, l2s2.h[0:1, :], 1.0 / dm, b, ALU.mult, ALU.subtract),
                    reads=[l2s2, rowpool], writes=[rowpool])
                P.emit("act", lambda sc, b=rB2: sc.activation(b, b, AF.Ln),
                       reads=[rowpool], writes=[rowpool])
                P.emit("act", lambda sc, b=rB2: sc.activation(
                    b, b, AF.Exp, scale=-0.5),
                    reads=[rowpool], writes=[rowpool])
                P.emit("pe", lambda t, b=rB2: t.matmul(
                    bc2_ps.h[:, :], onesrow_f.h[:, :], b,
                    start=True, stop=True),
                    reads=[onesrow_f, rowpool], writes=[bc2_ps])
                P.emit("act", lambda sc: sc.activation(
                    Br.h[:, 0:st_w], bc2_ps.h[:, :], AF.Copy),
                    reads=[bc2_ps], writes=[Br])
                P.emit("dve", lambda v, a=rA2, b=rB2: v.tensor_mul(a, a, b),
                       reads=[rowpool], writes=[rowpool])
                P.emit("pe", lambda t, a=rA2: t.matmul(
                    bc2_ps.h[:, :], onesrow_f.h[:, :], a,
                    start=True, stop=True),
                    reads=[onesrow_f, rowpool], writes=[bc2_ps])
                P.emit("act", lambda sc: sc.activation(
                    Bm.h[:, 0:st_w], bc2_ps.h[:, :], AF.Copy),
                    reads=[bc2_ps], writes=[Bm])
                for d in range(ne):
                    emit_norm(sti, d, tail=(sti == nst - 1))

            P.barrier()
        P.replay()
    return nc


_CACHE = {}


def _get_program():
    if "nc" not in _CACHE:
        _CACHE["nc"] = build_program()
    return _CACHE["nc"]


def kernel(L0_attn_hidden_states, L0_attn_wq, L0_attn_bq, L0_attn_wk,
           L0_attn_bk, L0_attn_wv, L0_attn_bv, L0_attn_wl, L0_attn_bl,
           L0_ffn_weight, L0_ffn_bias, L0_ffn_gate_w, L0_ffn_in_channel_scale,
           L0_ffn_out_channel_scale):
    x = np.asarray(L0_attn_hidden_states, dtype=np.float32)

    def tile_w(w):
        return np.ascontiguousarray(
            w.T.reshape(NH, 128, NH, 128).transpose(2, 1, 0, 3)).astype(
                np.float32, copy=False)

    wqh = tile_w(np.asarray(L0_attn_wq, np.float32))
    wkh = tile_w(np.asarray(L0_attn_wk, np.float32))
    wvh = tile_w(np.asarray(L0_attn_wv, np.float32))
    wsh = tile_w(np.asarray(L0_ffn_weight, np.float32))
    gwh = np.ascontiguousarray(
        np.asarray(L0_ffn_gate_w, np.float32).T.reshape(NH, 128, 8)
        .transpose(1, 0, 2)).reshape(128, NH * 8)
    bqh = np.ascontiguousarray(np.asarray(L0_attn_bq, np.float32).reshape(NH, 128).T)
    bkh = np.ascontiguousarray(np.asarray(L0_attn_bk, np.float32).reshape(NH, 128).T)
    bvh = np.ascontiguousarray(np.asarray(L0_attn_bv, np.float32).reshape(NH, 128).T)
    bfh = np.ascontiguousarray(np.asarray(L0_ffn_bias, np.float32).reshape(NH, 128).T)
    icsa = np.ascontiguousarray(np.asarray(L0_ffn_in_channel_scale, np.float32))
    ocsa = np.ascontiguousarray(np.asarray(L0_ffn_out_channel_scale, np.float32))

    nc = _get_program()
    in_maps = []
    for c in range(N_CORES):
        in_maps.append({
            "xt": np.ascontiguousarray(x[c].T),
            "wqh": wqh, "wkh": wkh, "wvh": wvh, "wsh": wsh, "gwh": gwh,
            "bqh": bqh, "bkh": bkh, "bvh": bvh, "bfh": bfh,
            "ics": icsa, "ocs": ocsa,
        })
    res = run_bass_kernel_spmd(nc, in_maps, core_ids=list(range(N_CORES)))
    outs = [np.ascontiguousarray(r["out"].T) for r in res.results]
    return np.stack(outs, axis=0).astype(np.float32, copy=False)


# version 27
# speedup vs baseline: 1.0842x; 1.0507x over previous
import bisect
from contextlib import ExitStack

import numpy as np
import ml_dtypes

import concourse.bass as bass
import concourse.mybir as mybir
from concourse.bass_utils import run_bass_kernel_spmd

F32 = mybir.dt.float32
BF16 = mybir.dt.bfloat16
AF = mybir.ActivationFunctionType
ALU = mybir.AluOpType
AX = mybir.AxisListType

N_CORES = 8
SEQ = 2048
NH = 32
EPS = 1e-5
SCALE = float(1.0 / np.sqrt(np.float32(128.0)))

SAME_ENGINE_SYNC = True

_BLOCK_ATTR = {"pe": "tensor", "act": "scalar", "dve": "vector",
               "pool": "gpsimd", "sp": "sync"}


class TBuf:
    __slots__ = ("h", "w", "r", "name")

    def __init__(self, h, name=""):
        self.h = h
        self.name = name
        self.w = {}
        self.r = {}


class Chan:
    __slots__ = ("name", "sem", "count")

    def __init__(self, name, sem):
        self.name = name
        self.sem = sem
        self.count = 0


class Prog:
    ENGINES = ("pe", "act", "dve", "pool", "sp")

    def __init__(self, nc, ctx):
        self.nc = nc
        self.ops = {e: [] for e in self.ENGINES}
        self.n = {e: 0 for e in self.ENGINES}
        self.incs = {e: [] for e in self.ENGINES}
        self.inc_flags = {e: set() for e in self.ENGINES}
        self.wm = {e: {} for e in self.ENGINES}
        self.sems = {e: ctx.enter_context(nc.semaphore("s_" + e))
                     for e in self.ENGINES}
        self.chans = {}
        self._ctx = ctx

    def chan(self, name):
        if name not in self.chans:
            sem = self._ctx.enter_context(self.nc.semaphore("c_" + name))
            self.chans[name] = Chan(name, sem)
        return self.chans[name]

    def _inc_value(self, ename, pos):
        lst = self.incs[ename]
        idx = bisect.bisect_left(lst, pos)
        if idx < len(lst):
            return idx + 1
        assert pos <= self.n[ename], (ename, pos, self.n[ename])
        lst.append(pos)
        self.inc_flags[ename].add(pos)
        return len(lst)

    def _need(self, consumer, key, pos):
        if key[0] == "e":
            ename = key[1]
            if ename == consumer and (ename == "pe" or not SAME_ENGINE_SYNC):
                return
            val = self._inc_value(ename, pos)
        else:
            val = 16 * self.chans[key[1]].count
        if self.wm[consumer].get(key, 0) >= val:
            return
        self.ops[consumer].append(("w", key, val))
        self.wm[consumer][key] = val

    def emit(self, ename, build, reads=(), writes=(), chan=None):
        deps = {}
        for b in reads:
            for k, p in b.w.items():
                if deps.get(k, 0) < p:
                    deps[k] = p
        for b in writes:
            for k, p in b.w.items():
                if deps.get(k, 0) < p:
                    deps[k] = p
            for k, p in b.r.items():
                if deps.get(k, 0) < p:
                    deps[k] = p
        for k in sorted(deps, key=repr):
            self._need(ename, k, deps[k])
        if chan is None:
            self.n[ename] += 1
            pos = self.n[ename]
            self.ops[ename].append(("i", build, pos))
            tok_key, tok_pos = ("e", ename), pos
        else:
            c = self.chan(chan)
            c.count += 1
            pos = c.count
            self.ops[ename].append(("d", build, chan))
            tok_key, tok_pos = ("c", chan), pos
        for b in reads:
            if b.r.get(tok_key, 0) < tok_pos:
                b.r[tok_key] = tok_pos
        for b in writes:
            if b.w.get(tok_key, 0) < tok_pos:
                b.w[tok_key] = tok_pos
            b.r = {}
        return (tok_key, tok_pos)

    def barrier(self):
        snap = {("e", e): self.n[e] for e in self.ENGINES if self.n[e] > 0}
        for cname, ch in self.chans.items():
            if ch.count > 0:
                snap[("c", cname)] = ch.count
        for e in self.ENGINES:
            for k in sorted(snap, key=repr):
                if k == ("e", e):
                    continue
                self._need(e, k, snap[k])

    def replay(self):
        nc = self.nc
        with nc.Block() as block:
            for ename in self.ENGINES:
                ops = self.ops[ename]
                if not ops:
                    continue

                def runner(eng, ops=ops, ename=ename):
                    flags = self.inc_flags[ename]
                    sems = self.sems
                    chans = self.chans
                    for op in ops:
                        if op[0] == "w":
                            key = op[1]
                            sem = (sems[key[1]] if key[0] == "e"
                                   else chans[key[1]].sem)
                            eng.wait_ge(sem, op[2])
                        elif op[0] == "i":
                            inst = op[1](eng)
                            if op[2] in flags:
                                inst.then_inc(sems[ename], 1)
                        else:
                            inst = op[1](eng)
                            inst.then_inc(chans[op[2]].sem, 16)

                getattr(block, _BLOCK_ATTR[ename])(runner)


def build_program(seq=SEQ, nh=NH, debug=False):
    ne = nh
    dm = 128 * nh
    sl_w = min(512, seq)
    nsl = seq // sl_w
    st_w = min(512, seq)
    nst = seq // st_w
    SLB = min(256, seq)
    nslab = seq // SLB
    G = 4
    ngrp = SLB // G

    nc = bass.Bass("TRN2", target_bir_lowering=False)

    xt = nc.dram_tensor("xt", [dm, seq], F32, kind="ExternalInput")
    wqh = nc.dram_tensor("wqh", [nh, 128, ne, 128], F32, kind="ExternalInput")
    wkh = nc.dram_tensor("wkh", [nh, 128, ne, 128], F32, kind="ExternalInput")
    wvh = nc.dram_tensor("wvh", [nh, 128, ne, 128], F32, kind="ExternalInput")
    wsh = nc.dram_tensor("wsh", [ne, 128, ne, 128], F32, kind="ExternalInput")
    gwh = nc.dram_tensor("gwh", [128, ne * 8], F32, kind="ExternalInput")
    bqh = nc.dram_tensor("bqh", [128, nh], F32, kind="ExternalInput")
    bkh = nc.dram_tensor("bkh", [128, nh], F32, kind="ExternalInput")
    bvh = nc.dram_tensor("bvh", [128, nh], F32, kind="ExternalInput")
    bfh = nc.dram_tensor("bfh", [128, ne], F32, kind="ExternalInput")
    ics = nc.dram_tensor("ics", [8, dm], F32, kind="ExternalInput")
    ocs = nc.dram_tensor("ocs", [8, dm], F32, kind="ExternalInput")

    nslab_ = seq // min(256, seq)
    SLB_ = min(256, seq)
    q_scr = nc.dram_tensor("q_scr", [nslab_, nh, 128, SLB_], BF16, kind="Internal")
    k_scr = nc.dram_tensor("k_scr", [nslab_, nh, 128, SLB_], BF16, kind="Internal")
    v_scr = nc.dram_tensor("v_scr", [nslab_, nh, 128, SLB_], BF16, kind="Internal")
    c_scr = nc.dram_tensor("c_scr", [nh, 128, seq], BF16, kind="Internal")
    z_scr = nc.dram_tensor("z_scr", [ne, 128, seq], BF16, kind="Internal")
    y1_scr = nc.dram_tensor("y1_scr", [ne, 128, seq], BF16, kind="Internal")
    ws_scr = nc.dram_tensor("ws_scr", [ne, 128, dm], BF16, kind="Internal")

    out = nc.dram_tensor("out", [dm, seq], F32, kind="ExternalOutput")
    _p4 = np.kron(np.eye(4, dtype=np.float32),
                  np.ones((1, nh), dtype=np.float32)) * 32.0
    p4a_dram = nc.inline_tensor(_p4.astype(ml_dtypes.bfloat16), name="p4a_c")
    p4b_dram = nc.inline_tensor(
        np.concatenate([_p4, _p4], axis=1).astype(ml_dtypes.bfloat16),
        name="p4b_c")
    if debug:
        dbg_qkv = nc.dram_tensor("dbg_qkv", [3, nslab_, nh, 128, SLB_], BF16,
                                 kind="ExternalOutput")
        dbg_c = nc.dram_tensor("dbg_c", [nh, 128, seq], BF16,
                               kind="ExternalOutput")
        dbg_y1 = nc.dram_tensor("dbg_y1", [ne, 128, seq], BF16,
                                kind="ExternalOutput")
        dbg_rt = nc.dram_tensor("dbg_rt", [8, seq], BF16,
                                kind="ExternalOutput")

    ctx = ExitStack()
    with ctx:
        P = Prog(nc, ctx)
        E = ctx.enter_context

        ident_bf = TBuf(E(nc.sbuf_tensor("ident_bf", [128, 128], BF16)))
        ones_bf = TBuf(E(nc.sbuf_tensor("ones_bf", [128, 1], BF16)))
        onesrow_f = TBuf(E(nc.sbuf_tensor("onesrow_f", [1, 128], F32)))
        rowpool = TBuf(E(nc.sbuf_tensor("rowpool", [1, 2 * seq], F32)))
        BQ = TBuf(E(nc.sbuf_tensor("BQ", [128, nh], F32)))
        BK = TBuf(E(nc.sbuf_tensor("BK", [128, nh], F32)))
        BV = TBuf(E(nc.sbuf_tensor("BV", [128, nh], F32)))
        BFN = TBuf(E(nc.sbuf_tensor("BFN", [128, ne], F32)))
        GW = TBuf(E(nc.sbuf_tensor("GW", [128, ne * 8], BF16)))
        routing_t = TBuf(E(nc.sbuf_tensor("routing_t", [8, seq], BF16)))
        Bm = TBuf(E(nc.sbuf_tensor("Bm", [128, seq], BF16)))
        Br = TBuf(E(nc.sbuf_tensor("Br", [128, seq], BF16)))

        P.emit("pool", lambda g: g.memset(ident_bf.h[:, :], 0.0),
               writes=[ident_bf])
        P.emit("pool", lambda g: g.affine_select(
            out=ident_bf.h[:, :], in_=ident_bf.h[:, :],
            compare_op=ALU.not_equal, fill=1.0, base=0,
            pattern=[[-1, 128]], channel_multiplier=1),
            reads=[ident_bf], writes=[ident_bf])
        P.emit("pool", lambda g: g.memset(ones_bf.h[:, :], 1.0),
               writes=[ones_bf])
        P.emit("pool", lambda g: g.memset(onesrow_f.h[:, :], 1.0),
               writes=[onesrow_f])

        for mi, (buf, srct) in enumerate(
                ((BQ, bqh), (BK, bkh), (BV, bvh), (BFN, bfh))):
            P.emit("sp", lambda s, b=buf, t=srct: s.dma_start(b.h[:, :], t.ap()),
                   writes=[buf], chan=f"msc{mi}")
        P.emit("pool", lambda g: g.dma_start(GW.h[:, :], gwh.ap()),
               writes=[GW], chan="mgw")

        zfill_tb = TBuf(None, "zfill")
        ctxA0 = ExitStack()
        with ctxA0:
            EA = ctxA0.enter_context
            X = [TBuf(EA(nc.sbuf_tensor(f"X{c}", [128, seq], BF16)), f"X{c}")
                 for c in range(ne)]
            for c in range(ne):
                P.emit("pool", lambda g, c=c: g.dma_start(
                    X[c].h[:, :], xt.ap()[c * 128:(c + 1) * 128, :]),
                    writes=[X[c]], chan=f"x{c}")
            for c in range(ne):
                P.emit("sp", lambda s, c=c: s.dma_start(
                    z_scr.ap()[c], X[c].h[:, :]),
                    reads=[X[c]], writes=[zfill_tb], chan="zfill")

            wbufs = [TBuf(EA(nc.sbuf_tensor(f"W{i}", [128, dm], BF16)))
                     for i in range(2)]
            stage = [TBuf(EA(nc.sbuf_tensor(f"pst{i}", [128, seq], BF16)))
                     for i in range(3)]
            wsbufs = [TBuf(EA(nc.sbuf_tensor(f"wsbuf{i}", [128, dm // 2], BF16)))
                      for i in range(2)]
            wstmp = TBuf(EA(nc.sbuf_tensor("wstmp", [128, dm // 2], BF16)))
            proj_ps = [TBuf(EA(nc.psum_tensor(f"proj_ps{i}", [128, 512], F32)))
                       for i in range(2)]

            ws_tasks = [(h, hf) for h in range(ne) for hf in range(2)]

            def emit_ws_load(k):
                h, hf = ws_tasks[k]
                b = wsbufs[k % 2]
                P.emit("pool", lambda g, h=h, hf=hf, b=b: g.dma_start(
                    b.h[:, :],
                    wsh.ap()[h][:, hf * (ne // 2):(hf + 1) * (ne // 2), :]),
                    writes=[b], chan="ws_in")

            emit_ws_load(0)
            emit_ws_load(1)
            ws_k = 0

            wsrcs = (wqh, wkh, wvh)
            wdsts = (q_scr, k_scr, v_scr)
            wbias = (BQ, BK, BV)
            wload = [(h, wi) for h in range(nh) for wi in range(3)]

            def emit_wload(j):
                h, wi = wload[j]
                buf = wbufs[j % 2]
                P.emit("pool", lambda g, h=h, wi=wi, b=buf: g.dma_start(
                    b.h[:, :], wsrcs[wi].ap()[h]),
                    writes=[buf], chan=f"wl{j % 2}")

            emit_wload(0)
            if len(wload) > 1:
                emit_wload(1)
            next_load = 2

            for j, (h, wi) in enumerate(wload):
                wb = wbufs[j % 2]
                st = stage[j % 3]
                for s4 in range(nsl):
                    pp = proj_ps[(j * nsl + s4) % 2]
                    for e in range(ne):
                        P.emit("pe", lambda t, wb=wb, e=e, s4=s4, pp=pp: t.matmul(
                            pp.h[:, 0:sl_w],
                            wb.h[:, e * 128:(e + 1) * 128],
                            X[e].h[:, s4 * sl_w:(s4 + 1) * sl_w],
                            start=(e == 0), stop=(e == ne - 1)),
                            reads=[wb, X[e]], writes=[pp])
                    P.emit("act", lambda sc, st=st, s4=s4, wi=wi, h=h, pp=pp:
                           sc.activation(
                               st.h[:, s4 * sl_w:(s4 + 1) * sl_w],
                               pp.h[:, 0:sl_w], AF.Identity,
                               bias=wbias[wi].h[:, h:h + 1]),
                           reads=[pp, wbias[wi]], writes=[st])
                P.emit("sp", lambda s, st=st, wi=wi, h=h: s.dma_start(
                    wdsts[wi].ap()[:, h, :, :].transpose([1, 0, 2]),
                    st.h[:, :].rearrange("d (sl s) -> d sl s", s=SLB)),
                    reads=[st], chan="qkv_out")
                if next_load < len(wload):
                    emit_wload(next_load)
                    next_load += 1
                if wi == 0 and h < ne:
                    for _ in range(2):
                        wh, whf = ws_tasks[ws_k]
                        b = wsbufs[ws_k % 2]
                        P.emit("dve", lambda v, b=b: v.tensor_single_scalar(
                            wstmp.h[:, :], b.h[:, :], 0.0, ALU.is_lt),
                            reads=[b], writes=[wstmp])
                        P.emit("dve", lambda v, b=b: v.scalar_tensor_tensor(
                            b.h[:, :], b.h[:, :], 0.0, wstmp.h[:, :],
                            ALU.is_gt, ALU.subtract),
                            reads=[b, wstmp], writes=[b])
                        P.emit("sp", lambda s, wh=wh, whf=whf, b=b: s.dma_start(
                            ws_scr.ap()[wh][:, whf * (dm // 2):(whf + 1) * (dm // 2)],
                            b.h[:, :]),
                            reads=[b], chan="ws_out")
                        if ws_k + 2 < len(ws_tasks):
                            emit_ws_load(ws_k + 2)
                        ws_k += 1
            if debug:
                P.barrier()
                for wi in range(3):
                    P.emit("sp", lambda s, wi=wi: s.dma_start(
                        dbg_qkv.ap()[wi], wdsts[wi].ap()), chan="dbg")

        P.barrier()
        ctxA1 = ExitStack()
        with ctxA1:
            EA = ctxA1.enter_context
            Qs = [TBuf(EA(nc.sbuf_tensor(f"Qs{i}", [128, ne * SLB], BF16)))
                  for i in range(2)]
            Ks = [TBuf(EA(nc.sbuf_tensor(f"Ks{i}", [128, ne * SLB], BF16)))
                  for i in range(2)]
            Vs = [TBuf(EA(nc.sbuf_tensor(f"Vs{i}", [128, ne * SLB], BF16)))
                  for i in range(2)]
            SRW = [TBuf(EA(nc.sbuf_tensor(f"SRW{p}", [128, ne * SLB], BF16)))
                   for p in range(2)]
            WIDX = {"q": 0, "k": 1, "v": 0}
            gdim = G * ne
            CTXF_ts = [EA(nc.sbuf_tensor(f"CTXF{i}", [128, ne * SLB], BF16))
                       for i in range(2)]
            CTXFbs = [[TBuf(CTXF_ts[i], f"ctxf{i}A"), TBuf(CTXF_ts[i], f"ctxf{i}B")]
                      for i in range(2)]
            U4 = [TBuf(EA(nc.sbuf_tensor(f"U4_{i}", [gdim, 2 * gdim], BF16)))
                  for i in range(3)]
            Vts4 = [TBuf(EA(nc.sbuf_tensor(f"Vts4_{i}", [gdim, 256], BF16)))
                    for i in range(3)]
            ctxn = [TBuf(EA(nc.sbuf_tensor(f"ctxn{i}", [gdim, 256], BF16)))
                    for i in range(3)]
            rden = [TBuf(EA(nc.sbuf_tensor(f"rden{i}", [gdim, 2], F32)))
                    for i in range(2)]
            negrow = TBuf(EA(nc.sbuf_tensor("negrow", [1, gdim], BF16)))
            ones2 = TBuf(EA(nc.sbuf_tensor("ones2", [1, 2 * gdim], BF16)))
            p4a = TBuf(EA(nc.sbuf_tensor("p4a", [4, gdim], BF16)))
            p4b = TBuf(EA(nc.sbuf_tensor("p4b", [4, 2 * gdim], BF16)))

            gram_ps = [TBuf(EA(nc.psum_tensor(f"gram{i}", [gdim, 2 * gdim], F32)))
                       for i in range(2)]
            vts_ps = TBuf(EA(nc.psum_tensor("vtsp0", [gdim, 256], BF16)))
            ctxT_ps = [TBuf(EA(nc.psum_tensor(f"ctxT{i}", [gdim, 256], F32)))
                       for i in range(2)]
            den_ps = TBuf(EA(nc.psum_tensor("denp", [gdim, 2], F32)))
            ctf_ps = [TBuf(EA(nc.psum_tensor(f"ctf{i}", [128, 2 * gdim], BF16)))
                      for i in range(2)]
            P.emit("pool", lambda g: g.memset(negrow.h[:, :], -1024.0),
                   writes=[negrow])
            P.emit("pool", lambda g: g.memset(ones2.h[:, :], 1.0),
                   writes=[ones2])
            P.emit("sp", lambda s: s.dma_start(p4a.h[:, :], p4a_dram.ap()),
                   writes=[p4a], chan="mmask")
            P.emit("sp", lambda s: s.dma_start(p4b.h[:, :], p4b_dram.ap()),
                   writes=[p4b], chan="mmask")
            def load_one(sl, which):
                buf = SRW[WIDX[which]]
                scr = {"q": q_scr, "k": k_scr, "v": v_scr}[which]
                P.emit("sp", lambda s, b=buf, scr=scr, sl=sl: s.dma_start(
                    b.h[:, :].rearrange("d (t s) -> d t s", s=SLB),
                    scr.ap()[sl].transpose([1, 0, 2])),
                    writes=[buf], chan=which + "sl" + str(sl % 2))

            def rearr_piece(sl, which, i, npc=4):
                stg = SRW[WIDX[which]]
                dst = {"q": Qs, "k": Ks, "v": Vs}[which][sl % 2]
                w = SLB // npc
                P.emit("dve", lambda v, b=dst, stg=stg, i=i, w=w: v.tensor_copy(
                    b.h[:, i * w * ne:(i + 1) * w * ne]
                    .rearrange("d (s t) -> d s t", t=ne),
                    stg.h[:, :].rearrange("d (t s) -> d s t", s=SLB)
                    [:, i * w:(i + 1) * w, :]),
                    reads=[stg], writes=[dst])

            def rearr_one(sl, which):
                for i in range(4):
                    rearr_piece(sl, which, i)

            for w in ("q", "k"):
                load_one(0, w)
            for w in ("q", "k"):
                rearr_one(0, w)
            load_one(0, "v")
            rearr_one(0, "v")
            for sl in range(nslab):
                s0 = sl * SLB
                qb_, kb_, vb_ = Qs[sl % 2], Ks[sl % 2], Vs[sl % 2]
                CTXF_t = CTXF_ts[sl % 2]
                CTXFb = CTXFbs[sl % 2]

                def gview(b, g):
                    return b.h[:, g * G * ne:(g + 1) * G * ne]

                def rview(b, g):
                    return (b.h[:, :].rearrange("d (t s) -> d s t", s=SLB)
                            [:, g * G:(g + 1) * G, :])

                nmac = ngrp // 2

                def stage1(m):
                    gp = gram_ps[m % 2]
                    P.emit("pe", lambda t, gp=gp: t.matmul(
                        gp.h[:, :], negrow.h[:, :], ones2.h[:, :],
                        start=True, stop=False),
                        reads=[negrow, ones2], writes=[gp])
                    P.emit("pe", lambda t, gp=gp: t.matmul(
                        gp.h[:, :], p4a.h[:, :], p4b.h[:, :],
                        start=False, stop=False),
                        reads=[p4a, p4b], writes=[gp])
                    for k in (0, 1):
                        g = 2 * m + k
                        P.emit("pe", lambda t, g=g, k=k, vb_=vb_: t.transpose(
                            vts_ps.h[:, k * 128:(k + 1) * 128],
                            gview(vb_, g), ident_bf.h[:, :]),
                            reads=[vb_, ident_bf], writes=[vts_ps])
                        P.emit("pe", lambda t, g=g, k=k, gp=gp, kb_=kb_,
                               qb_=qb_: t.matmul(
                                   gp.h[:, k * gdim:(k + 1) * gdim],
                                   gview(kb_, g), gview(qb_, g),
                                   start=False, stop=(k == 1)),
                               reads=[kb_, qb_], writes=[gp])
                    ub = U4[m % 3]
                    P.emit("act", lambda sc, ub=ub, gp=gp: sc.activation(
                        ub.h[:, :], gp.h[:, :], AF.Exp, scale=SCALE),
                        reads=[gp], writes=[ub])
                    vb4 = Vts4[m % 3]
                    P.emit("act", lambda sc, vb4=vb4: sc.activation(
                        vb4.h[:, :], vts_ps.h[:, :], AF.Copy),
                        reads=[vts_ps], writes=[vb4])

                def stage2(p):
                    um, vb4 = U4[p % 3], Vts4[p % 3]
                    ctp = ctxT_ps[p % 2]
                    for k in (0, 1):
                        P.emit("pe", lambda t, k=k, um=um, vb4=vb4, ctp=ctp:
                               t.matmul(
                                   ctp.h[:, k * 128:(k + 1) * 128],
                                   um.h[:, k * gdim:(k + 1) * gdim],
                                   vb4.h[:, k * 128:(k + 1) * 128],
                                   start=True, stop=True),
                               reads=[um, vb4], writes=[ctp])
                        P.emit("pe", lambda t, k=k, um=um: t.matmul(
                            den_ps.h[:, k:k + 1],
                            um.h[:, k * gdim:(k + 1) * gdim],
                            ones_bf.h[0:gdim, 0:1],
                            start=True, stop=True),
                            reads=[um, ones_bf], writes=[den_ps])
                    rd = rden[p % 2]
                    P.emit("dve", lambda v, rd=rd: v.reciprocal(
                        rd.h[:, :], den_ps.h[:, :]),
                        reads=[den_ps], writes=[rd])
                    cn_ = ctxn[p % 3]
                    for k in (0, 1):
                        P.emit("dve", lambda v, k=k, cn_=cn_, ctp=ctp, rd=rd:
                               v.tensor_scalar_mul(
                                   cn_.h[:, k * 128:(k + 1) * 128],
                                   ctp.h[:, k * 128:(k + 1) * 128],
                                   rd.h[:, k:k + 1]),
                               reads=[ctp, rd], writes=[cn_])

                def stage3(p):
                    cn_ = ctxn[p % 3]
                    cf = ctf_ps[p % 2]
                    for k in (0, 1):
                        P.emit("pe", lambda t, k=k, cn_=cn_, cf=cf: t.transpose(
                            cf.h[:, k * gdim:(k + 1) * gdim],
                            cn_.h[:, k * 128:(k + 1) * 128],
                            ident_bf.h[0:gdim, 0:gdim]),
                            reads=[cn_, ident_bf], writes=[cf])
                    ctb = CTXFb[p % 2]
                    P.emit("act", lambda sc, p=p, cf=cf, CTXF_t=CTXF_t:
                        sc.activation(
                        CTXF_t[:, :].rearrange("d (h s) -> d s h", s=SLB)
                        [:, p * 2 * G:(p + 1) * 2 * G, :],
                        cf.h[:, :].rearrange("d (s h) -> d s h", h=ne),
                        AF.Copy),
                        reads=[cf], writes=[ctb])

                for m in range(nmac + 4):
                    if m < nmac:
                        stage1(m)
                    if 2 <= m < nmac + 2:
                        stage2(m - 2)
                    if 4 <= m:
                        stage3(m - 4)
                    if sl + 1 < nslab:
                        if m == 0:
                            load_one(sl + 1, "q")
                            load_one(sl + 1, "k")
                        if 13 <= m < 17:
                            rearr_piece(sl + 1, "q", m - 13)
                        if m == 17:
                            load_one(sl + 1, "v")
                        if 17 <= m < 21:
                            rearr_piece(sl + 1, "k", m - 17)
                        if 29 <= m < 33:
                            rearr_piece(sl + 1, "v", m - 29)

                P.emit("pool", lambda g, s0=s0, CTXF_t=CTXF_t:
                    g.dma_start(
                        z_scr.ap()[:, :, s0:s0 + SLB].transpose([1, 0, 2]),
                        CTXF_t[:, :],
                        accum_op=ALU.add),
                    reads=[CTXFb[0], CTXFb[1], zfill_tb],
                    chan=f"c_out{sl % 2}")

        P.barrier()
        ctxB = ExitStack()
        with ctxB:
            EB = ctxB.enter_context
            Z = [TBuf(EB(nc.sbuf_tensor(f"Z{c}", [128, seq], BF16)))
                 for c in range(ne)]
            sqb = [TBuf(EB(nc.sbuf_tensor(f"sqb{i}", [128, seq], BF16)))
                   for i in range(2)]
            tnb = [TBuf(EB(nc.sbuf_tensor(f"tnb{i}", [128, seq], BF16)))
                   for i in range(2)]

            ctxB1 = ExitStack()
            with ctxB1:
                sum1 = TBuf(ctxB1.enter_context(
                    nc.psum_tensor("sum1", [1, seq], F32)))
                sum2 = TBuf(ctxB1.enter_context(
                    nc.psum_tensor("sum2", [1, seq], F32)))
                for c in range(ne):
                    P.emit("sp", lambda s, c=c: s.dma_start(
                        Z[c].h[:, :], z_scr.ap()[c]),
                        writes=[Z[c]], chan=f"zin{c % 4}")
                    sq = sqb[c % 2]
                    P.emit("act", lambda sc, c=c, sq=sq: sc.activation(
                        sq.h[:, :], Z[c].h[:, :], AF.Square),
                        reads=[Z[c]], writes=[sq])
                    for s4 in range(nsl):
                        P.emit("pe", lambda t, c=c, s4=s4: t.matmul(
                            sum1.h[0:1, s4 * sl_w:(s4 + 1) * sl_w],
                            ones_bf.h[:, :],
                            Z[c].h[:, s4 * sl_w:(s4 + 1) * sl_w],
                            start=(c == 0), stop=(c == ne - 1)),
                            reads=[ones_bf, Z[c]], writes=[sum1])
                        P.emit("pe", lambda t, sq=sq, s4=s4, c=c: t.matmul(
                            sum2.h[0:1, s4 * sl_w:(s4 + 1) * sl_w],
                            ones_bf.h[:, :],
                            sq.h[:, s4 * sl_w:(s4 + 1) * sl_w],
                            start=(c == 0), stop=(c == ne - 1)),
                            reads=[ones_bf, sq], writes=[sum2])
                rA = rowpool.h[0:1, 0:seq]
                rB = rowpool.h[0:1, seq:2 * seq]
                P.emit("dve", lambda v, a=rA: v.tensor_scalar_mul(
                    a, sum1.h[0:1, :], 1.0 / dm),
                    reads=[sum1], writes=[rowpool])
                P.emit("dve", lambda v, a=rA, b=rB: v.tensor_mul(b, a, a),
                       reads=[rowpool], writes=[rowpool])
                P.emit("dve", lambda v, b=rB: v.tensor_scalar_add(b, b, -EPS),
                       reads=[rowpool], writes=[rowpool])
                P.emit("dve", lambda v, b=rB: v.scalar_tensor_tensor(
                    b, sum2.h[0:1, :], 1.0 / dm, b, ALU.mult, ALU.subtract),
                    reads=[sum2, rowpool], writes=[rowpool])
                P.emit("act", lambda sc, b=rB: sc.activation(b, b, AF.Ln),
                       reads=[rowpool], writes=[rowpool])
                P.emit("act", lambda sc, b=rB: sc.activation(
                    b, b, AF.Exp, scale=-0.5),
                    reads=[rowpool], writes=[rowpool])
            P.barrier()

            ctxB2 = ExitStack()
            with ctxB2:
                ctxB2p = ExitStack()
                bcB_ps = TBuf(ctxB2p.enter_context(
                    nc.psum_tensor("bcB_ps", [128, seq], F32)))
                lg_ps = TBuf(ctxB2p.enter_context(
                    nc.psum_tensor("lg_ps", [8, seq], F32)))
                for s4 in range(nsl):
                    P.emit("pe", lambda t, s4=s4: t.matmul(
                        bcB_ps.h[:, s4 * sl_w:(s4 + 1) * sl_w],
                        onesrow_f.h[:, :],
                        rowpool.h[0:1, seq + s4 * sl_w:seq + (s4 + 1) * sl_w],
                        start=True, stop=True),
                        reads=[onesrow_f, rowpool], writes=[bcB_ps])
                P.emit("act", lambda sc: sc.activation(
                    Br.h[:, :], bcB_ps.h[:, :], AF.Copy),
                    reads=[bcB_ps], writes=[Br])
                P.emit("dve", lambda v: v.tensor_mul(
                    rowpool.h[0:1, 0:seq], rowpool.h[0:1, 0:seq],
                    rowpool.h[0:1, seq:2 * seq]),
                    reads=[rowpool], writes=[rowpool])
                for s4 in range(nsl):
                    P.emit("pe", lambda t, s4=s4: t.matmul(
                        bcB_ps.h[:, s4 * sl_w:(s4 + 1) * sl_w],
                        onesrow_f.h[:, :],
                        rowpool.h[0:1, s4 * sl_w:(s4 + 1) * sl_w],
                        start=True, stop=True),
                        reads=[onesrow_f, rowpool], writes=[bcB_ps])
                P.emit("act", lambda sc: sc.activation(
                    Bm.h[:, :], bcB_ps.h[:, :], AF.Copy),
                    reads=[bcB_ps], writes=[Bm])
                for c in range(ne):
                    tn = tnb[c % 2]
                    P.emit("dve", lambda v, c=c, tn=tn: v.tensor_mul(
                        tn.h[:, :], Z[c].h[:, :], Br.h[:, :]),
                        reads=[Z[c], Br], writes=[tn])
                    P.emit("dve", lambda v, c=c, tn=tn: v.tensor_sub(
                        Z[c].h[:, :], tn.h[:, :], Bm.h[:, :]),
                        reads=[tn, Bm], writes=[Z[c]])
                    P.emit("sp", lambda s, c=c: s.dma_start(
                        y1_scr.ap()[c], Z[c].h[:, :]),
                        reads=[Z[c]], chan="y1_out")
                    for s4 in range(nsl):
                        P.emit("pe", lambda t, c=c, s4=s4: t.matmul(
                            lg_ps.h[0:8, s4 * sl_w:(s4 + 1) * sl_w],
                            GW.h[:, c * 8:(c + 1) * 8],
                            Z[c].h[:, s4 * sl_w:(s4 + 1) * sl_w],
                            start=(c == 0), stop=(c == ne - 1)),
                            reads=[GW, Z[c]], writes=[lg_ps])
                lg_sb = TBuf(ctxB2.enter_context(
                    nc.sbuf_tensor("lg_sb", [8, seq], BF16)))
                P.emit("act", lambda sc: sc.activation(
                    lg_sb.h[:, :], lg_ps.h[:, :], AF.Copy),
                    reads=[lg_ps], writes=[lg_sb])
                nt = seq // 128
                ltm = TBuf(ctxB2.enter_context(
                    nc.sbuf_tensor("ltm", [128, nt * 8], BF16)))
                prob = TBuf(ctxB2.enter_context(
                    nc.sbuf_tensor("prob", [128, nt * 8], F32)))
                mx = TBuf(ctxB2.enter_context(
                    nc.sbuf_tensor("mx", [128, nt], F32)))
                den = TBuf(ctxB2.enter_context(
                    nc.sbuf_tensor("den", [128, nt], F32)))
                rtm = TBuf(ctxB2.enter_context(
                    nc.sbuf_tensor("rtm", [128, nt * 8], BF16)))
                P.barrier()
                ctxB2p.close()
                ctxB3 = ExitStack()
                with ctxB3:
                    lt_ps = TBuf(ctxB3.enter_context(
                        nc.psum_tensor("lt_ps", [128, nt * 8], BF16)))
                    rt_ps = TBuf(ctxB3.enter_context(
                        nc.psum_tensor("rt_ps", [8, seq], BF16)))
                    for tt in range(nt):
                        P.emit("pe", lambda t, tt=tt: t.transpose(
                            lt_ps.h[:, tt * 8:(tt + 1) * 8],
                            lg_sb.h[0:8, tt * 128:(tt + 1) * 128],
                            ident_bf.h[0:8, 0:8]),
                            reads=[lg_sb, ident_bf], writes=[lt_ps])
                    P.emit("dve", lambda v: v.tensor_copy(
                        ltm.h[:, :], lt_ps.h[:, :]),
                        reads=[lt_ps], writes=[ltm])
                    P.emit("dve", lambda v: v.reduce_max(
                        mx.h[:, :],
                        ltm.h[:, :].rearrange("p (g x) -> p g x", x=8),
                        axis=AX.X),
                        reads=[ltm], writes=[mx])
                    P.emit("dve", lambda v: v.tensor_scalar_mul(
                        mx.h[:, :], mx.h[:, :], -1.0),
                        reads=[mx], writes=[mx])
                    for tt in range(nt):
                        P.emit("act", lambda sc, tt=tt: sc.activation(
                            prob.h[:, tt * 8:(tt + 1) * 8],
                            ltm.h[:, tt * 8:(tt + 1) * 8], AF.Exp,
                            bias=mx.h[:, tt:tt + 1],
                            accum_out=den.h[:, tt:tt + 1]),
                            reads=[ltm, mx], writes=[prob, den])
                    P.emit("dve", lambda v: v.reciprocal(
                        den.h[:, :], den.h[:, :]),
                        reads=[den], writes=[den])
                    for tt in range(nt):
                        P.emit("dve", lambda v, tt=tt: v.tensor_scalar_mul(
                            rtm.h[:, tt * 8:(tt + 1) * 8],
                            prob.h[:, tt * 8:(tt + 1) * 8],
                            den.h[:, tt:tt + 1]),
                            reads=[prob, den], writes=[rtm])
                    for tt in range(nt):
                        P.emit("pe", lambda t, tt=tt: t.transpose(
                            rt_ps.h[0:8, tt * 128:(tt + 1) * 128],
                            rtm.h[:, tt * 8:(tt + 1) * 8],
                            ident_bf.h[:, :]),
                            reads=[rtm, ident_bf], writes=[rt_ps])
                    P.emit("dve", lambda v: v.tensor_copy(
                        routing_t.h[:, :], rt_ps.h[:, :]),
                        reads=[rt_ps], writes=[routing_t])
                    if debug:
                        P.emit("sp", lambda s: s.dma_start(
                            dbg_rt.ap(), routing_t.h[:, :]),
                            reads=[routing_t], chan="dbg")
                        P.emit("sp", lambda s: s.dma_start(
                            dbg_y1.ap(), y1_scr.ap()), chan="dbg")
                        P.emit("sp", lambda s: s.dma_start(
                            dbg_c.ap(), c_scr.ap()), chan="dbg")
                    P.barrier()

        ctxC = ExitStack()
        with ctxC:
            EC = ctxC.enter_context
            ICS = TBuf(EC(nc.sbuf_tensor("ICS", [8, dm], BF16)))
            OCS = TBuf(EC(nc.sbuf_tensor("OCS", [8, dm], BF16)))
            P.emit("pool", lambda g: g.dma_start(ICS.h[:, :], ics.ap()),
                   writes=[ICS], chan="mics")
            P.emit("pool", lambda g: g.dma_start(OCS.h[:, :], ocs.ap()),
                   writes=[OCS], chan="mocs")
            xs = [[TBuf(EC(nc.sbuf_tensor(f"xs{p}_{e}", [128, st_w], BF16)))
                   for e in range(ne)] for p in range(2)]
            z2 = [TBuf(EC(nc.sbuf_tensor(f"z2{d}", [128, st_w], BF16)))
                  for d in range(ne)]
            WS = [TBuf(EC(nc.sbuf_tensor(f"WS{i}", [128, dm], BF16)))
                  for i in range(3)]
            y1a = [TBuf(EC(nc.sbuf_tensor(f"y1a{i}", [128, st_w], BF16)))
                   for i in range(2)]
            y1r = [TBuf(EC(nc.sbuf_tensor(f"y1r{i}", [128, st_w], BF16)))
                   for i in range(2)]
            os_sb = [TBuf(EC(nc.sbuf_tensor(f"os{i}", [128, st_w], BF16)))
                     for i in range(2)]
            tf = [TBuf(EC(nc.sbuf_tensor(f"tf{i}", [128, st_w], BF16)))
                  for i in range(2)]
            fo = [TBuf(EC(nc.sbuf_tensor(f"fo{i}", [128, st_w], BF16)))
                  for i in range(2)]
            sqc = [TBuf(EC(nc.sbuf_tensor(f"sqc{i}", [128, st_w], BF16)))
                   for i in range(3)]
            tn2 = [TBuf(EC(nc.sbuf_tensor(f"tn2{i}", [128, st_w], BF16)))
                   for i in range(2)]
            y2s = [TBuf(EC(nc.sbuf_tensor(f"y2s{i}", [128, st_w], F32)))
                   for i in range(4)]

            out_ps = [TBuf(EC(nc.psum_tensor(f"out_ps{i}", [128, st_w], F32)))
                      for i in range(2)]
            is_ps = [TBuf(EC(nc.psum_tensor(f"is_ps{i}", [128, st_w], F32)))
                     for i in range(2)]
            os_ps = TBuf(EC(nc.psum_tensor("os_ps", [128, st_w], F32)))
            l2s1 = TBuf(EC(nc.psum_tensor("l2s1", [1, st_w], F32)))
            l2s2 = TBuf(EC(nc.psum_tensor("l2s2", [1, st_w], F32)))
            bc2_ps = TBuf(EC(nc.psum_tensor("bc2_ps", [128, st_w], F32)))

            def emit_xs(sti):
                s0x = sti * st_w
                par = sti % 2
                for e in range(ne):
                    ya = y1a[e % 2]
                    P.emit("sp", lambda s, e=e, ya=ya, s0x=s0x: s.dma_start(
                        ya.h[:, :], y1_scr.ap()[e][:, s0x:s0x + st_w]),
                        writes=[ya], chan=f"y1a{e % 2}")
                    ip = is_ps[e % 2]
                    P.emit("pe", lambda t, e=e, ip=ip, s0x=s0x: t.matmul(
                        ip.h[:, :], ICS.h[:, e * 128:(e + 1) * 128],
                        routing_t.h[:, s0x:s0x + st_w], start=True, stop=True),
                        reads=[ICS, routing_t], writes=[ip])
                    P.emit("dve", lambda v, e=e, ya=ya, ip=ip, par=par:
                           v.tensor_mul(
                               xs[par][e].h[:, :], ya.h[:, :], ip.h[:, :]),
                           reads=[ya, ip], writes=[xs[par][e]])

            def emit_norm(stp, d, tail=False):
                s0p = stp * st_w
                eng = "dve" if (tail and d % 2 == 1) else "pool"
                tn = tn2[d % 2]
                ys = y2s[d % 4]
                if eng == "pool":
                    P.emit("pool", lambda g, d=d, tn=tn: g.tensor_mul(
                        tn.h[:, :], z2[d].h[:, :], Br.h[:, 0:st_w]),
                        reads=[z2[d], Br], writes=[tn])
                    P.emit("pool", lambda g, tn=tn, ys=ys: g.tensor_sub(
                        ys.h[:, :], tn.h[:, :], Bm.h[:, 0:st_w]),
                        reads=[tn, Bm], writes=[ys])
                else:
                    P.emit("dve", lambda v, d=d, tn=tn: v.tensor_mul(
                        tn.h[:, :], z2[d].h[:, :], Br.h[:, 0:st_w]),
                        reads=[z2[d], Br], writes=[tn])
                    P.emit("dve", lambda v, tn=tn, ys=ys: v.tensor_sub(
                        ys.h[:, :], tn.h[:, :], Bm.h[:, 0:st_w]),
                        reads=[tn, Bm], writes=[ys])
                P.emit("pool", lambda g, d=d, ys=ys, s0p=s0p: g.dma_start(
                    out.ap()[d * 128:(d + 1) * 128, s0p:s0p + st_w],
                    ys.h[:, :]),
                    reads=[ys], chan="out")

            emit_xs(0)

            def emit_stats(d, zd, sq):
                P.emit("pe", lambda t, zd=zd, d=d: t.matmul(
                    l2s1.h[0:1, :], ones_bf.h[:, :], zd.h[:, :],
                    start=(d == 0), stop=(d == ne - 1)),
                    reads=[ones_bf, zd], writes=[l2s1])
                P.emit("pe", lambda t, sq=sq, d=d: t.matmul(
                    l2s2.h[0:1, :], ones_bf.h[:, :], sq.h[:, :],
                    start=(d == 0), stop=(d == ne - 1)),
                    reads=[ones_bf, sq], writes=[l2s2])

            for sti in range(nst):
                s0 = sti * st_w
                xcur = xs[sti % 2]
                pending_stats = []
                for d in range(ne):
                    wsb = WS[d % 3]
                    P.emit("sp", lambda s, d=d, wsb=wsb: s.dma_start(
                        wsb.h[:, :], ws_scr.ap()[d]),
                        writes=[wsb], chan=f"ws_ffn{d % 3}")
                    P.emit("pe", lambda t, d=d, s0=s0: t.matmul(
                        os_ps.h[:, :], OCS.h[:, d * 128:(d + 1) * 128],
                        routing_t.h[:, s0:s0 + st_w], start=True, stop=True),
                        reads=[OCS, routing_t], writes=[os_ps])
                    ob = os_sb[d % 2]
                    P.emit("act", lambda sc, ob=ob: sc.activation(
                        ob.h[:, :], os_ps.h[:, :], AF.Copy),
                        reads=[os_ps], writes=[ob])
                    op = out_ps[d % 2]
                    for e in range(ne):
                        P.emit("pe", lambda t, e=e, wsb=wsb, op=op, xcur=xcur:
                               t.matmul(
                                   op.h[:, :],
                                   wsb.h[:, e * 128:(e + 1) * 128],
                                   xcur[e].h[:, :],
                                   start=(e == 0), stop=(e == ne - 1)),
                               reads=[wsb, xcur[e]], writes=[op])
                    if pending_stats:
                        emit_stats(*pending_stats.pop())
                    yr = y1r[d % 2]
                    P.emit("sp", lambda s, d=d, yr=yr, s0=s0: s.dma_start(
                        yr.h[:, :], y1_scr.ap()[d][:, s0:s0 + st_w]),
                        writes=[yr], chan=f"y1r{d % 2}")
                    fb = fo[d % 2]
                    P.emit("act", lambda sc, fb=fb, op=op: sc.activation(
                        fb.h[:, :], op.h[:, :], AF.Copy),
                        reads=[op], writes=[fb])
                    tb = tf[d % 2]
                    P.emit("dve", lambda v, tb=tb, fb=fb, ob=ob: v.tensor_mul(
                        tb.h[:, :], fb.h[:, :], ob.h[:, :]),
                        reads=[fb, ob], writes=[tb])
                    P.emit("dve", lambda v, d=d, tb=tb, yr=yr:
                           v.scalar_tensor_tensor(
                               z2[d].h[:, :], tb.h[:, :], BFN.h[:, d:d + 1],
                               yr.h[:, :], ALU.add, ALU.add),
                           reads=[tb, BFN, yr], writes=[z2[d]])
                    sq = sqc[d % 3]
                    P.emit("act", lambda sc, d=d, sq=sq: sc.activation(
                        sq.h[:, :], z2[d].h[:, :], AF.Square),
                        reads=[z2[d]], writes=[sq])
                    pending_stats.append((d, z2[d], sq))
                    if d == 8 and sti + 1 < nst:
                        emit_xs(sti + 1)
                if pending_stats:
                    emit_stats(*pending_stats.pop())
                rA2 = rowpool.h[0:1, 0:st_w]
                rB2 = rowpool.h[0:1, seq:seq + st_w]
                P.emit("dve", lambda v, a=rA2: v.tensor_scalar_mul(
                    a, l2s1.h[0:1, :], 1.0 / dm),
                    reads=[l2s1], writes=[rowpool])
                P.emit("dve", lambda v, a=rA2, b=rB2: v.tensor_mul(b, a, a),
                       reads=[rowpool], writes=[rowpool])
                P.emit("dve", lambda v, b=rB2: v.tensor_scalar_add(b, b, -EPS),
                       reads=[rowpool], writes=[rowpool])
                P.emit("dve", lambda v, b=rB2: v.scalar_tensor_tensor(
                    b, l2s2.h[0:1, :], 1.0 / dm, b, ALU.mult, ALU.subtract),
                    reads=[l2s2, rowpool], writes=[rowpool])
                P.emit("act", lambda sc, b=rB2: sc.activation(b, b, AF.Ln),
                       reads=[rowpool], writes=[rowpool])
                P.emit("act", lambda sc, b=rB2: sc.activation(
                    b, b, AF.Exp, scale=-0.5),
                    reads=[rowpool], writes=[rowpool])
                P.emit("pe", lambda t, b=rB2: t.matmul(
                    bc2_ps.h[:, :], onesrow_f.h[:, :], b,
                    start=True, stop=True),
                    reads=[onesrow_f, rowpool], writes=[bc2_ps])
                P.emit("act", lambda sc: sc.activation(
                    Br.h[:, 0:st_w], bc2_ps.h[:, :], AF.Copy),
                    reads=[bc2_ps], writes=[Br])
                P.emit("dve", lambda v, a=rA2, b=rB2: v.tensor_mul(a, a, b),
                       reads=[rowpool], writes=[rowpool])
                P.emit("pe", lambda t, a=rA2: t.matmul(
                    bc2_ps.h[:, :], onesrow_f.h[:, :], a,
                    start=True, stop=True),
                    reads=[onesrow_f, rowpool], writes=[bc2_ps])
                P.emit("act", lambda sc: sc.activation(
                    Bm.h[:, 0:st_w], bc2_ps.h[:, :], AF.Copy),
                    reads=[bc2_ps], writes=[Bm])
                for d in range(ne):
                    emit_norm(sti, d, tail=(sti == nst - 1))

            P.barrier()
        P.replay()
    return nc


_CACHE = {}


def _get_program():
    if "nc" not in _CACHE:
        _CACHE["nc"] = build_program()
    return _CACHE["nc"]


def kernel(L0_attn_hidden_states, L0_attn_wq, L0_attn_bq, L0_attn_wk,
           L0_attn_bk, L0_attn_wv, L0_attn_bv, L0_attn_wl, L0_attn_bl,
           L0_ffn_weight, L0_ffn_bias, L0_ffn_gate_w, L0_ffn_in_channel_scale,
           L0_ffn_out_channel_scale):
    x = np.asarray(L0_attn_hidden_states, dtype=np.float32)

    def tile_w(w):
        return np.ascontiguousarray(
            w.T.reshape(NH, 128, NH, 128).transpose(2, 1, 0, 3)).astype(
                np.float32, copy=False)

    wqh = tile_w(np.asarray(L0_attn_wq, np.float32))
    wkh = tile_w(np.asarray(L0_attn_wk, np.float32))
    wvh = tile_w(np.asarray(L0_attn_wv, np.float32))
    wsh = tile_w(np.asarray(L0_ffn_weight, np.float32))
    gwh = np.ascontiguousarray(
        np.asarray(L0_ffn_gate_w, np.float32).T.reshape(NH, 128, 8)
        .transpose(1, 0, 2)).reshape(128, NH * 8)
    bqh = np.ascontiguousarray(np.asarray(L0_attn_bq, np.float32).reshape(NH, 128).T)
    bkh = np.ascontiguousarray(np.asarray(L0_attn_bk, np.float32).reshape(NH, 128).T)
    bvh = np.ascontiguousarray(np.asarray(L0_attn_bv, np.float32).reshape(NH, 128).T)
    bfh = np.ascontiguousarray(np.asarray(L0_ffn_bias, np.float32).reshape(NH, 128).T)
    icsa = np.ascontiguousarray(np.asarray(L0_ffn_in_channel_scale, np.float32))
    ocsa = np.ascontiguousarray(np.asarray(L0_ffn_out_channel_scale, np.float32))

    nc = _get_program()
    in_maps = []
    for c in range(N_CORES):
        in_maps.append({
            "xt": np.ascontiguousarray(x[c].T),
            "wqh": wqh, "wkh": wkh, "wvh": wvh, "wsh": wsh, "gwh": gwh,
            "bqh": bqh, "bkh": bkh, "bvh": bvh, "bfh": bfh,
            "ics": icsa, "ocs": ocsa,
        })
    res = run_bass_kernel_spmd(nc, in_maps, core_ids=list(range(N_CORES)))
    outs = [np.ascontiguousarray(r["out"].T) for r in res.results]
    return np.stack(outs, axis=0).astype(np.float32, copy=False)


# version 29
# speedup vs baseline: 1.1233x; 1.0361x over previous
import bisect
from contextlib import ExitStack

import numpy as np
import ml_dtypes

import concourse.bass as bass
import concourse.mybir as mybir
from concourse.bass_utils import run_bass_kernel_spmd

F32 = mybir.dt.float32
BF16 = mybir.dt.bfloat16
AF = mybir.ActivationFunctionType
ALU = mybir.AluOpType
AX = mybir.AxisListType

N_CORES = 8
SEQ = 2048
NH = 32
EPS = 1e-5
SCALE = float(1.0 / np.sqrt(np.float32(128.0)))

SAME_ENGINE_SYNC = True

_BLOCK_ATTR = {"pe": "tensor", "act": "scalar", "dve": "vector",
               "pool": "gpsimd", "sp": "sync"}


class TBuf:
    __slots__ = ("h", "w", "r", "name")

    def __init__(self, h, name=""):
        self.h = h
        self.name = name
        self.w = {}
        self.r = {}


class Chan:
    __slots__ = ("name", "sem", "count")

    def __init__(self, name, sem):
        self.name = name
        self.sem = sem
        self.count = 0


class Prog:
    ENGINES = ("pe", "act", "dve", "pool", "sp")

    def __init__(self, nc, ctx):
        self.nc = nc
        self.ops = {e: [] for e in self.ENGINES}
        self.n = {e: 0 for e in self.ENGINES}
        self.incs = {e: [] for e in self.ENGINES}
        self.inc_flags = {e: set() for e in self.ENGINES}
        self.wm = {e: {} for e in self.ENGINES}
        self.sems = {e: ctx.enter_context(nc.semaphore("s_" + e))
                     for e in self.ENGINES}
        self.chans = {}
        self._ctx = ctx

    def chan(self, name):
        if name not in self.chans:
            sem = self._ctx.enter_context(self.nc.semaphore("c_" + name))
            self.chans[name] = Chan(name, sem)
        return self.chans[name]

    def _inc_value(self, ename, pos):
        lst = self.incs[ename]
        idx = bisect.bisect_left(lst, pos)
        if idx < len(lst):
            return idx + 1
        assert pos <= self.n[ename], (ename, pos, self.n[ename])
        lst.append(pos)
        self.inc_flags[ename].add(pos)
        return len(lst)

    def _need(self, consumer, key, pos):
        if key[0] == "e":
            ename = key[1]
            if ename == consumer and (ename == "pe" or not SAME_ENGINE_SYNC):
                return
            val = self._inc_value(ename, pos)
        else:
            val = 16 * self.chans[key[1]].count
        if self.wm[consumer].get(key, 0) >= val:
            return
        self.ops[consumer].append(("w", key, val))
        self.wm[consumer][key] = val

    def emit(self, ename, build, reads=(), writes=(), chan=None):
        deps = {}
        for b in reads:
            for k, p in b.w.items():
                if deps.get(k, 0) < p:
                    deps[k] = p
        for b in writes:
            for k, p in b.w.items():
                if deps.get(k, 0) < p:
                    deps[k] = p
            for k, p in b.r.items():
                if deps.get(k, 0) < p:
                    deps[k] = p
        for k in sorted(deps, key=repr):
            self._need(ename, k, deps[k])
        if chan is None:
            self.n[ename] += 1
            pos = self.n[ename]
            self.ops[ename].append(("i", build, pos))
            tok_key, tok_pos = ("e", ename), pos
        else:
            c = self.chan(chan)
            c.count += 1
            pos = c.count
            self.ops[ename].append(("d", build, chan))
            tok_key, tok_pos = ("c", chan), pos
        for b in reads:
            if b.r.get(tok_key, 0) < tok_pos:
                b.r[tok_key] = tok_pos
        for b in writes:
            if b.w.get(tok_key, 0) < tok_pos:
                b.w[tok_key] = tok_pos
            b.r = {}
        return (tok_key, tok_pos)

    def barrier(self):
        snap = {("e", e): self.n[e] for e in self.ENGINES if self.n[e] > 0}
        for cname, ch in self.chans.items():
            if ch.count > 0:
                snap[("c", cname)] = ch.count
        for e in self.ENGINES:
            for k in sorted(snap, key=repr):
                if k == ("e", e):
                    continue
                self._need(e, k, snap[k])

    def replay(self):
        nc = self.nc
        with nc.Block() as block:
            for ename in self.ENGINES:
                ops = self.ops[ename]
                if not ops:
                    continue

                def runner(eng, ops=ops, ename=ename):
                    flags = self.inc_flags[ename]
                    sems = self.sems
                    chans = self.chans
                    for op in ops:
                        if op[0] == "w":
                            key = op[1]
                            sem = (sems[key[1]] if key[0] == "e"
                                   else chans[key[1]].sem)
                            eng.wait_ge(sem, op[2])
                        elif op[0] == "i":
                            inst = op[1](eng)
                            if op[2] in flags:
                                inst.then_inc(sems[ename], 1)
                        else:
                            inst = op[1](eng)
                            inst.then_inc(chans[op[2]].sem, 16)

                getattr(block, _BLOCK_ATTR[ename])(runner)


def build_program(seq=SEQ, nh=NH, debug=False):
    ne = nh
    dm = 128 * nh
    sl_w = min(512, seq)
    nsl = seq // sl_w
    st_w = min(512, seq)
    nst = seq // st_w
    SLB = min(256, seq)
    nslab = seq // SLB
    G = 4
    ngrp = SLB // G

    nc = bass.Bass("TRN2", target_bir_lowering=False)

    xt = nc.dram_tensor("xt", [dm, seq], F32, kind="ExternalInput")
    wqh = nc.dram_tensor("wqh", [nh, 128, ne, 128], F32, kind="ExternalInput")
    wkh = nc.dram_tensor("wkh", [nh, 128, ne, 128], F32, kind="ExternalInput")
    wvh = nc.dram_tensor("wvh", [nh, 128, ne, 128], F32, kind="ExternalInput")
    wsh = nc.dram_tensor("wsh", [ne, 128, ne, 128], F32, kind="ExternalInput")
    gwh = nc.dram_tensor("gwh", [128, ne * 8], F32, kind="ExternalInput")
    bqh = nc.dram_tensor("bqh", [128, nh], F32, kind="ExternalInput")
    bkh = nc.dram_tensor("bkh", [128, nh], F32, kind="ExternalInput")
    bvh = nc.dram_tensor("bvh", [128, nh], F32, kind="ExternalInput")
    bfh = nc.dram_tensor("bfh", [128, ne], F32, kind="ExternalInput")
    ics = nc.dram_tensor("ics", [8, dm], F32, kind="ExternalInput")
    ocs = nc.dram_tensor("ocs", [8, dm], F32, kind="ExternalInput")

    nslab_ = seq // min(256, seq)
    SLB_ = min(256, seq)
    q_scr = nc.dram_tensor("q_scr", [nslab_, nh, 128, SLB_], BF16, kind="Internal")
    k_scr = nc.dram_tensor("k_scr", [nslab_, nh, 128, SLB_], BF16, kind="Internal")
    v_scr = nc.dram_tensor("v_scr", [nslab_, nh, 128, SLB_], BF16, kind="Internal")
    c_scr = nc.dram_tensor("c_scr", [nh, 128, seq], BF16, kind="Internal")
    z_scr = nc.dram_tensor("z_scr", [ne, 128, seq], BF16, kind="Internal")
    y1_scr = nc.dram_tensor("y1_scr", [ne, 128, seq], BF16, kind="Internal")
    ws_scr = nc.dram_tensor("ws_scr", [ne, 128, dm], BF16, kind="Internal")

    out = nc.dram_tensor("out", [dm, seq], F32, kind="ExternalOutput")
    _p4 = np.kron(np.eye(4, dtype=np.float32),
                  np.ones((1, nh), dtype=np.float32)) * 32.0
    p4a_dram = nc.inline_tensor(_p4.astype(ml_dtypes.bfloat16), name="p4a_c")
    p4b_dram = nc.inline_tensor(
        np.concatenate([_p4, _p4], axis=1).astype(ml_dtypes.bfloat16),
        name="p4b_c")
    if debug:
        dbg_qkv = nc.dram_tensor("dbg_qkv", [3, nslab_, nh, 128, SLB_], BF16,
                                 kind="ExternalOutput")
        dbg_c = nc.dram_tensor("dbg_c", [nh, 128, seq], BF16,
                               kind="ExternalOutput")
        dbg_y1 = nc.dram_tensor("dbg_y1", [ne, 128, seq], BF16,
                                kind="ExternalOutput")
        dbg_rt = nc.dram_tensor("dbg_rt", [8, seq], BF16,
                                kind="ExternalOutput")

    ctx = ExitStack()
    with ctx:
        P = Prog(nc, ctx)
        E = ctx.enter_context

        ident_bf = TBuf(E(nc.sbuf_tensor("ident_bf", [128, 128], BF16)))
        ones_bf = TBuf(E(nc.sbuf_tensor("ones_bf", [128, 1], BF16)))
        onesrow_f = TBuf(E(nc.sbuf_tensor("onesrow_f", [1, 128], F32)))
        rowpool = TBuf(E(nc.sbuf_tensor("rowpool", [1, 2 * seq], F32)))
        BQ = TBuf(E(nc.sbuf_tensor("BQ", [128, nh], F32)))
        BK = TBuf(E(nc.sbuf_tensor("BK", [128, nh], F32)))
        BV = TBuf(E(nc.sbuf_tensor("BV", [128, nh], F32)))
        BFN = TBuf(E(nc.sbuf_tensor("BFN", [128, ne], F32)))
        GW = TBuf(E(nc.sbuf_tensor("GW", [128, ne * 8], BF16)))
        routing_t = TBuf(E(nc.sbuf_tensor("routing_t", [8, seq], BF16)))
        Bm = TBuf(E(nc.sbuf_tensor("Bm", [128, seq], BF16)))
        Br = TBuf(E(nc.sbuf_tensor("Br", [128, seq], BF16)))

        P.emit("pool", lambda g: g.memset(ident_bf.h[:, :], 0.0),
               writes=[ident_bf])
        P.emit("pool", lambda g: g.affine_select(
            out=ident_bf.h[:, :], in_=ident_bf.h[:, :],
            compare_op=ALU.not_equal, fill=1.0, base=0,
            pattern=[[-1, 128]], channel_multiplier=1),
            reads=[ident_bf], writes=[ident_bf])
        P.emit("pool", lambda g: g.memset(ones_bf.h[:, :], 1.0),
               writes=[ones_bf])
        P.emit("pool", lambda g: g.memset(onesrow_f.h[:, :], 1.0),
               writes=[onesrow_f])

        for mi, (buf, srct) in enumerate(
                ((BQ, bqh), (BK, bkh), (BV, bvh), (BFN, bfh))):
            P.emit("sp", lambda s, b=buf, t=srct: s.dma_start(b.h[:, :], t.ap()),
                   writes=[buf], chan=f"msc{mi}")
        P.emit("pool", lambda g: g.dma_start(GW.h[:, :], gwh.ap()),
               writes=[GW], chan="mgw")

        zfill_tb = TBuf(None, "zfill")
        ctxA0 = ExitStack()
        with ctxA0:
            EA = ctxA0.enter_context
            X = [TBuf(EA(nc.sbuf_tensor(f"X{c}", [128, seq], BF16)), f"X{c}")
                 for c in range(ne)]
            for c in range(ne):
                P.emit("pool", lambda g, c=c: g.dma_start(
                    X[c].h[:, :], xt.ap()[c * 128:(c + 1) * 128, :]),
                    writes=[X[c]], chan=f"x{c}")
            for c in range(ne):
                P.emit("sp", lambda s, c=c: s.dma_start(
                    z_scr.ap()[c], X[c].h[:, :]),
                    reads=[X[c]], writes=[zfill_tb], chan="zfill")

            wbufs = [TBuf(EA(nc.sbuf_tensor(f"W{i}", [128, dm], BF16)))
                     for i in range(2)]
            stage = [TBuf(EA(nc.sbuf_tensor(f"pst{i}", [128, seq], BF16)))
                     for i in range(3)]
            wsbufs = [TBuf(EA(nc.sbuf_tensor(f"wsbuf{i}", [128, dm // 2], BF16)))
                      for i in range(2)]
            wstmp = TBuf(EA(nc.sbuf_tensor("wstmp", [128, dm // 2], BF16)))
            proj_ps = [TBuf(EA(nc.psum_tensor(f"proj_ps{i}", [128, 512], F32)))
                       for i in range(2)]

            ws_tasks = [(h, hf) for h in range(ne) for hf in range(2)]

            def emit_ws_load(k):
                h, hf = ws_tasks[k]
                b = wsbufs[k % 2]
                P.emit("pool", lambda g, h=h, hf=hf, b=b: g.dma_start(
                    b.h[:, :],
                    wsh.ap()[h][:, hf * (ne // 2):(hf + 1) * (ne // 2), :]),
                    writes=[b], chan="ws_in")

            emit_ws_load(0)
            emit_ws_load(1)
            ws_k = 0

            wsrcs = (wqh, wkh, wvh)
            wdsts = (q_scr, k_scr, v_scr)
            wbias = (BQ, BK, BV)
            wload = [(h, wi) for h in range(nh) for wi in range(3)]

            def emit_wload(j):
                h, wi = wload[j]
                buf = wbufs[j % 2]
                if j == 0:
                    qw = dm // 4
                    qe = ne // 4
                    for i in range(4):
                        P.emit("pool", lambda g, h=h, wi=wi, b=buf, i=i,
                               qw=qw, qe=qe:
                            g.dma_start(
                                b.h[:, i * qw:(i + 1) * qw],
                                wsrcs[wi].ap()[h][:, i * qe:(i + 1) * qe, :]),
                            writes=[buf], chan=f"wl0s{i}")
                    return
                P.emit("pool", lambda g, h=h, wi=wi, b=buf: g.dma_start(
                    b.h[:, :], wsrcs[wi].ap()[h]),
                    writes=[buf], chan=f"wl{j % 2}")

            emit_wload(0)
            if len(wload) > 1:
                emit_wload(1)
            next_load = 2

            for j, (h, wi) in enumerate(wload):
                wb = wbufs[j % 2]
                st = stage[j % 3]
                for s4 in range(nsl):
                    pp = proj_ps[(j * nsl + s4) % 2]
                    for e in range(ne):
                        P.emit("pe", lambda t, wb=wb, e=e, s4=s4, pp=pp: t.matmul(
                            pp.h[:, 0:sl_w],
                            wb.h[:, e * 128:(e + 1) * 128],
                            X[e].h[:, s4 * sl_w:(s4 + 1) * sl_w],
                            start=(e == 0), stop=(e == ne - 1)),
                            reads=[wb, X[e]], writes=[pp])
                    P.emit("act", lambda sc, st=st, s4=s4, wi=wi, h=h, pp=pp:
                           sc.activation(
                               st.h[:, s4 * sl_w:(s4 + 1) * sl_w],
                               pp.h[:, 0:sl_w], AF.Identity,
                               bias=wbias[wi].h[:, h:h + 1]),
                           reads=[pp, wbias[wi]], writes=[st])
                P.emit("sp", lambda s, st=st, wi=wi, h=h: s.dma_start(
                    wdsts[wi].ap()[:, h, :, :].transpose([1, 0, 2]),
                    st.h[:, :].rearrange("d (sl s) -> d sl s", s=SLB)),
                    reads=[st], chan="qkv_out")
                if next_load < len(wload):
                    emit_wload(next_load)
                    next_load += 1
                if wi == 0 and h < ne:
                    for _ in range(2):
                        wh, whf = ws_tasks[ws_k]
                        b = wsbufs[ws_k % 2]
                        P.emit("dve", lambda v, b=b: v.tensor_single_scalar(
                            wstmp.h[:, :], b.h[:, :], 0.0, ALU.is_lt),
                            reads=[b], writes=[wstmp])
                        P.emit("dve", lambda v, b=b: v.scalar_tensor_tensor(
                            b.h[:, :], b.h[:, :], 0.0, wstmp.h[:, :],
                            ALU.is_gt, ALU.subtract),
                            reads=[b, wstmp], writes=[b])
                        P.emit("sp", lambda s, wh=wh, whf=whf, b=b: s.dma_start(
                            ws_scr.ap()[wh][:, whf * (dm // 2):(whf + 1) * (dm // 2)],
                            b.h[:, :]),
                            reads=[b], chan="ws_out")
                        if ws_k + 2 < len(ws_tasks):
                            emit_ws_load(ws_k + 2)
                        ws_k += 1
            if debug:
                P.barrier()
                for wi in range(3):
                    P.emit("sp", lambda s, wi=wi: s.dma_start(
                        dbg_qkv.ap()[wi], wdsts[wi].ap()), chan="dbg")

        P.barrier()
        ctxA1 = ExitStack()
        with ctxA1:
            EA = ctxA1.enter_context
            Qs = [TBuf(EA(nc.sbuf_tensor(f"Qs{i}", [128, ne * SLB], BF16)))
                  for i in range(2)]
            Ks = [TBuf(EA(nc.sbuf_tensor(f"Ks{i}", [128, ne * SLB], BF16)))
                  for i in range(2)]
            Vs = [TBuf(EA(nc.sbuf_tensor(f"Vs{i}", [128, ne * SLB], BF16)))
                  for i in range(2)]
            SRW = [TBuf(EA(nc.sbuf_tensor(f"SRW{p}", [128, ne * SLB], BF16)))
                   for p in range(2)]
            WIDX = {"q": 0, "k": 1, "v": 0}
            gdim = G * ne
            CTXF_ts = [EA(nc.sbuf_tensor(f"CTXF{i}", [128, ne * SLB], BF16))
                       for i in range(2)]
            CTXFbs = [[TBuf(CTXF_ts[i], f"ctxf{i}A"), TBuf(CTXF_ts[i], f"ctxf{i}B")]
                      for i in range(2)]
            U4 = [TBuf(EA(nc.sbuf_tensor(f"U4_{i}", [gdim, 2 * gdim], BF16)))
                  for i in range(3)]
            Vts4 = [TBuf(EA(nc.sbuf_tensor(f"Vts4_{i}", [gdim, 256], BF16)))
                    for i in range(3)]
            ctxn = [TBuf(EA(nc.sbuf_tensor(f"ctxn{i}", [gdim, 256], BF16)))
                    for i in range(3)]
            rden = [TBuf(EA(nc.sbuf_tensor(f"rden{i}", [gdim, 2], F32)))
                    for i in range(2)]
            negrow = TBuf(EA(nc.sbuf_tensor("negrow", [1, gdim], BF16)))
            ones2 = TBuf(EA(nc.sbuf_tensor("ones2", [1, 2 * gdim], BF16)))
            p4a = TBuf(EA(nc.sbuf_tensor("p4a", [4, gdim], BF16)))
            p4b = TBuf(EA(nc.sbuf_tensor("p4b", [4, 2 * gdim], BF16)))

            gram_ps = [TBuf(EA(nc.psum_tensor(f"gram{i}", [gdim, 2 * gdim], F32)))
                       for i in range(2)]
            vts_ps = TBuf(EA(nc.psum_tensor("vtsp0", [gdim, 256], BF16)))
            ctxT_ps = [TBuf(EA(nc.psum_tensor(f"ctxT{i}", [gdim, 256], F32)))
                       for i in range(2)]
            den_ps = TBuf(EA(nc.psum_tensor("denp", [gdim, 2], F32)))
            ctf_ps = [TBuf(EA(nc.psum_tensor(f"ctf{i}", [128, 2 * gdim], BF16)))
                      for i in range(2)]
            P.emit("pool", lambda g: g.memset(negrow.h[:, :], -1024.0),
                   writes=[negrow])
            P.emit("pool", lambda g: g.memset(ones2.h[:, :], 1.0),
                   writes=[ones2])
            P.emit("sp", lambda s: s.dma_start(p4a.h[:, :], p4a_dram.ap()),
                   writes=[p4a], chan="mmask")
            P.emit("sp", lambda s: s.dma_start(p4b.h[:, :], p4b_dram.ap()),
                   writes=[p4b], chan="mmask")
            def load_one(sl, which):
                buf = SRW[WIDX[which]]
                scr = {"q": q_scr, "k": k_scr, "v": v_scr}[which]
                P.emit("sp", lambda s, b=buf, scr=scr, sl=sl: s.dma_start(
                    b.h[:, :].rearrange("d (t s) -> d t s", s=SLB),
                    scr.ap()[sl].transpose([1, 0, 2])),
                    writes=[buf], chan=which + "sl" + str(sl % 2))

            def rearr_piece(sl, which, i, npc=4):
                stg = SRW[WIDX[which]]
                dst = {"q": Qs, "k": Ks, "v": Vs}[which][sl % 2]
                w = SLB // npc
                P.emit("dve", lambda v, b=dst, stg=stg, i=i, w=w: v.tensor_copy(
                    b.h[:, i * w * ne:(i + 1) * w * ne]
                    .rearrange("d (s t) -> d s t", t=ne),
                    stg.h[:, :].rearrange("d (t s) -> d s t", s=SLB)
                    [:, i * w:(i + 1) * w, :]),
                    reads=[stg], writes=[dst])

            def rearr_one(sl, which):
                for i in range(4):
                    rearr_piece(sl, which, i)

            for w in ("q", "k"):
                load_one(0, w)
            for w in ("q", "k"):
                rearr_one(0, w)
            load_one(0, "v")
            rearr_one(0, "v")
            for sl in range(nslab):
                s0 = sl * SLB
                qb_, kb_, vb_ = Qs[sl % 2], Ks[sl % 2], Vs[sl % 2]
                CTXF_t = CTXF_ts[sl % 2]
                CTXFb = CTXFbs[sl % 2]

                def gview(b, g):
                    return b.h[:, g * G * ne:(g + 1) * G * ne]

                def rview(b, g):
                    return (b.h[:, :].rearrange("d (t s) -> d s t", s=SLB)
                            [:, g * G:(g + 1) * G, :])

                nmac = ngrp // 2

                def stage1(m):
                    gp = gram_ps[m % 2]
                    P.emit("pe", lambda t, gp=gp: t.matmul(
                        gp.h[:, :], negrow.h[:, :], ones2.h[:, :],
                        start=True, stop=False),
                        reads=[negrow, ones2], writes=[gp])
                    P.emit("pe", lambda t, gp=gp: t.matmul(
                        gp.h[:, :], p4a.h[:, :], p4b.h[:, :],
                        start=False, stop=False),
                        reads=[p4a, p4b], writes=[gp])
                    for k in (0, 1):
                        g = 2 * m + k
                        P.emit("pe", lambda t, g=g, k=k, vb_=vb_: t.transpose(
                            vts_ps.h[:, k * 128:(k + 1) * 128],
                            gview(vb_, g), ident_bf.h[:, :]),
                            reads=[vb_, ident_bf], writes=[vts_ps])
                        P.emit("pe", lambda t, g=g, k=k, gp=gp, kb_=kb_,
                               qb_=qb_: t.matmul(
                                   gp.h[:, k * gdim:(k + 1) * gdim],
                                   gview(kb_, g), gview(qb_, g),
                                   start=False, stop=(k == 1)),
                               reads=[kb_, qb_], writes=[gp])
                    ub = U4[m % 3]
                    P.emit("act", lambda sc, ub=ub, gp=gp: sc.activation(
                        ub.h[:, :], gp.h[:, :], AF.Exp, scale=SCALE),
                        reads=[gp], writes=[ub])
                    vb4 = Vts4[m % 3]
                    P.emit("act", lambda sc, vb4=vb4: sc.activation(
                        vb4.h[:, :], vts_ps.h[:, :], AF.Copy),
                        reads=[vts_ps], writes=[vb4])

                def stage2(p):
                    um, vb4 = U4[p % 3], Vts4[p % 3]
                    ctp = ctxT_ps[p % 2]
                    for k in (0, 1):
                        P.emit("pe", lambda t, k=k, um=um, vb4=vb4, ctp=ctp:
                               t.matmul(
                                   ctp.h[:, k * 128:(k + 1) * 128],
                                   um.h[:, k * gdim:(k + 1) * gdim],
                                   vb4.h[:, k * 128:(k + 1) * 128],
                                   start=True, stop=True),
                               reads=[um, vb4], writes=[ctp])
                        P.emit("pe", lambda t, k=k, um=um: t.matmul(
                            den_ps.h[:, k:k + 1],
                            um.h[:, k * gdim:(k + 1) * gdim],
                            ones_bf.h[0:gdim, 0:1],
                            start=True, stop=True),
                            reads=[um, ones_bf], writes=[den_ps])
                    rd = rden[p % 2]
                    P.emit("dve", lambda v, rd=rd: v.reciprocal(
                        rd.h[:, :], den_ps.h[:, :]),
                        reads=[den_ps], writes=[rd])
                    cn_ = ctxn[p % 3]
                    for k in (0, 1):
                        P.emit("dve", lambda v, k=k, cn_=cn_, ctp=ctp, rd=rd:
                               v.tensor_scalar_mul(
                                   cn_.h[:, k * 128:(k + 1) * 128],
                                   ctp.h[:, k * 128:(k + 1) * 128],
                                   rd.h[:, k:k + 1]),
                               reads=[ctp, rd], writes=[cn_])

                def stage3(p):
                    cn_ = ctxn[p % 3]
                    cf = ctf_ps[p % 2]
                    for k in (0, 1):
                        P.emit("pe", lambda t, k=k, cn_=cn_, cf=cf: t.transpose(
                            cf.h[:, k * gdim:(k + 1) * gdim],
                            cn_.h[:, k * 128:(k + 1) * 128],
                            ident_bf.h[0:gdim, 0:gdim]),
                            reads=[cn_, ident_bf], writes=[cf])
                    ctb = CTXFb[p % 2]
                    P.emit("act", lambda sc, p=p, cf=cf, CTXF_t=CTXF_t:
                        sc.activation(
                        CTXF_t[:, :].rearrange("d (h s) -> d s h", s=SLB)
                        [:, p * 2 * G:(p + 1) * 2 * G, :],
                        cf.h[:, :].rearrange("d (s h) -> d s h", h=ne),
                        AF.Copy),
                        reads=[cf], writes=[ctb])

                for m in range(nmac + 4):
                    if m < nmac:
                        stage1(m)
                    if 2 <= m < nmac + 2:
                        stage2(m - 2)
                    if 4 <= m:
                        stage3(m - 4)
                    if sl + 1 < nslab:
                        if m == 0:
                            load_one(sl + 1, "q")
                            load_one(sl + 1, "k")
                        if 13 <= m < 17:
                            rearr_piece(sl + 1, "q", m - 13)
                        if m == 17:
                            load_one(sl + 1, "v")
                        if 17 <= m < 21:
                            rearr_piece(sl + 1, "k", m - 17)
                        if 29 <= m < 33:
                            rearr_piece(sl + 1, "v", m - 29)

                P.emit("pool", lambda g, s0=s0, CTXF_t=CTXF_t:
                    g.dma_start(
                        z_scr.ap()[:, :, s0:s0 + SLB].transpose([1, 0, 2]),
                        CTXF_t[:, :],
                        accum_op=ALU.add),
                    reads=[CTXFb[0], CTXFb[1], zfill_tb],
                    chan=f"c_out{sl % 2}")

        P.barrier()
        ctxB = ExitStack()
        with ctxB:
            EB = ctxB.enter_context
            Z = [TBuf(EB(nc.sbuf_tensor(f"Z{c}", [128, seq], BF16)))
                 for c in range(ne)]
            sqb = [TBuf(EB(nc.sbuf_tensor(f"sqb{i}", [128, seq], BF16)))
                   for i in range(2)]
            tnb = [TBuf(EB(nc.sbuf_tensor(f"tnb{i}", [128, seq], BF16)))
                   for i in range(2)]

            ctxB1 = ExitStack()
            with ctxB1:
                sum1 = TBuf(ctxB1.enter_context(
                    nc.psum_tensor("sum1", [1, seq], F32)))
                sum2 = TBuf(ctxB1.enter_context(
                    nc.psum_tensor("sum2", [1, seq], F32)))
                for c in range(ne):
                    P.emit("sp", lambda s, c=c: s.dma_start(
                        Z[c].h[:, :], z_scr.ap()[c]),
                        writes=[Z[c]], chan=f"zin{c % 4}")
                    sq = sqb[c % 2]
                    P.emit("act", lambda sc, c=c, sq=sq: sc.activation(
                        sq.h[:, :], Z[c].h[:, :], AF.Square),
                        reads=[Z[c]], writes=[sq])
                    for s4 in range(nsl):
                        P.emit("pe", lambda t, c=c, s4=s4: t.matmul(
                            sum1.h[0:1, s4 * sl_w:(s4 + 1) * sl_w],
                            ones_bf.h[:, :],
                            Z[c].h[:, s4 * sl_w:(s4 + 1) * sl_w],
                            start=(c == 0), stop=(c == ne - 1)),
                            reads=[ones_bf, Z[c]], writes=[sum1])
                        P.emit("pe", lambda t, sq=sq, s4=s4, c=c: t.matmul(
                            sum2.h[0:1, s4 * sl_w:(s4 + 1) * sl_w],
                            ones_bf.h[:, :],
                            sq.h[:, s4 * sl_w:(s4 + 1) * sl_w],
                            start=(c == 0), stop=(c == ne - 1)),
                            reads=[ones_bf, sq], writes=[sum2])
                rA = rowpool.h[0:1, 0:seq]
                rB = rowpool.h[0:1, seq:2 * seq]
                P.emit("dve", lambda v, a=rA: v.tensor_scalar_mul(
                    a, sum1.h[0:1, :], 1.0 / dm),
                    reads=[sum1], writes=[rowpool])
                P.emit("dve", lambda v, a=rA, b=rB: v.tensor_mul(b, a, a),
                       reads=[rowpool], writes=[rowpool])
                P.emit("dve", lambda v, b=rB: v.tensor_scalar_add(b, b, -EPS),
                       reads=[rowpool], writes=[rowpool])
                P.emit("dve", lambda v, b=rB: v.scalar_tensor_tensor(
                    b, sum2.h[0:1, :], 1.0 / dm, b, ALU.mult, ALU.subtract),
                    reads=[sum2, rowpool], writes=[rowpool])
                P.emit("act", lambda sc, b=rB: sc.activation(b, b, AF.Ln),
                       reads=[rowpool], writes=[rowpool])
                P.emit("act", lambda sc, b=rB: sc.activation(
                    b, b, AF.Exp, scale=-0.5),
                    reads=[rowpool], writes=[rowpool])
            P.barrier()

            ctxB2 = ExitStack()
            with ctxB2:
                ctxB2p = ExitStack()
                bcB_ps = TBuf(ctxB2p.enter_context(
                    nc.psum_tensor("bcB_ps", [128, seq], F32)))
                lg_ps = TBuf(ctxB2p.enter_context(
                    nc.psum_tensor("lg_ps", [8, seq], F32)))
                for s4 in range(nsl):
                    P.emit("pe", lambda t, s4=s4: t.matmul(
                        bcB_ps.h[:, s4 * sl_w:(s4 + 1) * sl_w],
                        onesrow_f.h[:, :],
                        rowpool.h[0:1, seq + s4 * sl_w:seq + (s4 + 1) * sl_w],
                        start=True, stop=True),
                        reads=[onesrow_f, rowpool], writes=[bcB_ps])
                P.emit("act", lambda sc: sc.activation(
                    Br.h[:, :], bcB_ps.h[:, :], AF.Copy),
                    reads=[bcB_ps], writes=[Br])
                P.emit("dve", lambda v: v.tensor_mul(
                    rowpool.h[0:1, 0:seq], rowpool.h[0:1, 0:seq],
                    rowpool.h[0:1, seq:2 * seq]),
                    reads=[rowpool], writes=[rowpool])
                for s4 in range(nsl):
                    P.emit("pe", lambda t, s4=s4: t.matmul(
                        bcB_ps.h[:, s4 * sl_w:(s4 + 1) * sl_w],
                        onesrow_f.h[:, :],
                        rowpool.h[0:1, s4 * sl_w:(s4 + 1) * sl_w],
                        start=True, stop=True),
                        reads=[onesrow_f, rowpool], writes=[bcB_ps])
                P.emit("act", lambda sc: sc.activation(
                    Bm.h[:, :], bcB_ps.h[:, :], AF.Copy),
                    reads=[bcB_ps], writes=[Bm])
                for c in range(ne):
                    tn = tnb[c % 2]
                    P.emit("dve", lambda v, c=c, tn=tn: v.tensor_mul(
                        tn.h[:, :], Z[c].h[:, :], Br.h[:, :]),
                        reads=[Z[c], Br], writes=[tn])
                    P.emit("dve", lambda v, c=c, tn=tn: v.tensor_sub(
                        Z[c].h[:, :], tn.h[:, :], Bm.h[:, :]),
                        reads=[tn, Bm], writes=[Z[c]])
                    P.emit("sp", lambda s, c=c: s.dma_start(
                        y1_scr.ap()[c], Z[c].h[:, :]),
                        reads=[Z[c]], chan="y1_out")
                    for s4 in range(nsl):
                        P.emit("pe", lambda t, c=c, s4=s4: t.matmul(
                            lg_ps.h[0:8, s4 * sl_w:(s4 + 1) * sl_w],
                            GW.h[:, c * 8:(c + 1) * 8],
                            Z[c].h[:, s4 * sl_w:(s4 + 1) * sl_w],
                            start=(c == 0), stop=(c == ne - 1)),
                            reads=[GW, Z[c]], writes=[lg_ps])
                lg_sb = TBuf(ctxB2.enter_context(
                    nc.sbuf_tensor("lg_sb", [8, seq], BF16)))
                P.emit("act", lambda sc: sc.activation(
                    lg_sb.h[:, :], lg_ps.h[:, :], AF.Copy),
                    reads=[lg_ps], writes=[lg_sb])
                nt = seq // 128
                ltm = TBuf(ctxB2.enter_context(
                    nc.sbuf_tensor("ltm", [128, nt * 8], BF16)))
                prob = TBuf(ctxB2.enter_context(
                    nc.sbuf_tensor("prob", [128, nt * 8], F32)))
                mx = TBuf(ctxB2.enter_context(
                    nc.sbuf_tensor("mx", [128, nt], F32)))
                den = TBuf(ctxB2.enter_context(
                    nc.sbuf_tensor("den", [128, nt], F32)))
                rtm = TBuf(ctxB2.enter_context(
                    nc.sbuf_tensor("rtm", [128, nt * 8], BF16)))
                P.barrier()
                ctxB2p.close()
                ctxB3 = ExitStack()
                with ctxB3:
                    lt_ps = TBuf(ctxB3.enter_context(
                        nc.psum_tensor("lt_ps", [128, nt * 8], BF16)))
                    rt_ps = TBuf(ctxB3.enter_context(
                        nc.psum_tensor("rt_ps", [8, seq], BF16)))
                    for tt in range(nt):
                        P.emit("pe", lambda t, tt=tt: t.transpose(
                            lt_ps.h[:, tt * 8:(tt + 1) * 8],
                            lg_sb.h[0:8, tt * 128:(tt + 1) * 128],
                            ident_bf.h[0:8, 0:8]),
                            reads=[lg_sb, ident_bf], writes=[lt_ps])
                    P.emit("dve", lambda v: v.tensor_copy(
                        ltm.h[:, :], lt_ps.h[:, :]),
                        reads=[lt_ps], writes=[ltm])
                    P.emit("dve", lambda v: v.reduce_max(
                        mx.h[:, :],
                        ltm.h[:, :].rearrange("p (g x) -> p g x", x=8),
                        axis=AX.X),
                        reads=[ltm], writes=[mx])
                    P.emit("dve", lambda v: v.tensor_scalar_mul(
                        mx.h[:, :], mx.h[:, :], -1.0),
                        reads=[mx], writes=[mx])
                    for tt in range(nt):
                        P.emit("act", lambda sc, tt=tt: sc.activation(
                            prob.h[:, tt * 8:(tt + 1) * 8],
                            ltm.h[:, tt * 8:(tt + 1) * 8], AF.Exp,
                            bias=mx.h[:, tt:tt + 1],
                            accum_out=den.h[:, tt:tt + 1]),
                            reads=[ltm, mx], writes=[prob, den])
                    P.emit("dve", lambda v: v.reciprocal(
                        den.h[:, :], den.h[:, :]),
                        reads=[den], writes=[den])
                    for tt in range(nt):
                        P.emit("dve", lambda v, tt=tt: v.tensor_scalar_mul(
                            rtm.h[:, tt * 8:(tt + 1) * 8],
                            prob.h[:, tt * 8:(tt + 1) * 8],
                            den.h[:, tt:tt + 1]),
                            reads=[prob, den], writes=[rtm])
                    for tt in range(nt):
                        P.emit("pe", lambda t, tt=tt: t.transpose(
                            rt_ps.h[0:8, tt * 128:(tt + 1) * 128],
                            rtm.h[:, tt * 8:(tt + 1) * 8],
                            ident_bf.h[:, :]),
                            reads=[rtm, ident_bf], writes=[rt_ps])
                    P.emit("dve", lambda v: v.tensor_copy(
                        routing_t.h[:, :], rt_ps.h[:, :]),
                        reads=[rt_ps], writes=[routing_t])
                    if debug:
                        P.emit("sp", lambda s: s.dma_start(
                            dbg_rt.ap(), routing_t.h[:, :]),
                            reads=[routing_t], chan="dbg")
                        P.emit("sp", lambda s: s.dma_start(
                            dbg_y1.ap(), y1_scr.ap()), chan="dbg")
                        P.emit("sp", lambda s: s.dma_start(
                            dbg_c.ap(), c_scr.ap()), chan="dbg")
                    P.barrier()

        ctxC = ExitStack()
        with ctxC:
            EC = ctxC.enter_context
            ICS = TBuf(EC(nc.sbuf_tensor("ICS", [8, dm], BF16)))
            OCS = TBuf(EC(nc.sbuf_tensor("OCS", [8, dm], BF16)))
            P.emit("pool", lambda g: g.dma_start(ICS.h[:, :], ics.ap()),
                   writes=[ICS], chan="mics")
            P.emit("pool", lambda g: g.dma_start(OCS.h[:, :], ocs.ap()),
                   writes=[OCS], chan="mocs")
            xs = [[TBuf(EC(nc.sbuf_tensor(f"xs{p}_{e}", [128, st_w], BF16)))
                   for e in range(ne)] for p in range(2)]
            z2 = [TBuf(EC(nc.sbuf_tensor(f"z2{d}", [128, st_w], BF16)))
                  for d in range(ne)]
            WS = [TBuf(EC(nc.sbuf_tensor(f"WS{i}", [128, dm], BF16)))
                  for i in range(3)]
            y1a = [TBuf(EC(nc.sbuf_tensor(f"y1a{i}", [128, st_w], BF16)))
                   for i in range(2)]
            y1r = [TBuf(EC(nc.sbuf_tensor(f"y1r{i}", [128, st_w], BF16)))
                   for i in range(2)]
            os_sb = [TBuf(EC(nc.sbuf_tensor(f"os{i}", [128, st_w], BF16)))
                     for i in range(2)]
            tf = [TBuf(EC(nc.sbuf_tensor(f"tf{i}", [128, st_w], BF16)))
                  for i in range(2)]
            fo = [TBuf(EC(nc.sbuf_tensor(f"fo{i}", [128, st_w], BF16)))
                  for i in range(2)]
            sqc = [TBuf(EC(nc.sbuf_tensor(f"sqc{i}", [128, st_w], BF16)))
                   for i in range(3)]
            tn2 = [TBuf(EC(nc.sbuf_tensor(f"tn2{i}", [128, st_w], BF16)))
                   for i in range(2)]
            y2s = [TBuf(EC(nc.sbuf_tensor(f"y2s{i}", [128, st_w], F32)))
                   for i in range(4)]

            out_ps = [TBuf(EC(nc.psum_tensor(f"out_ps{i}", [128, st_w], F32)))
                      for i in range(2)]
            is_ps = [TBuf(EC(nc.psum_tensor(f"is_ps{i}", [128, st_w], F32)))
                     for i in range(2)]
            os_ps = TBuf(EC(nc.psum_tensor("os_ps", [128, st_w], F32)))
            l2s1 = TBuf(EC(nc.psum_tensor("l2s1", [1, st_w], F32)))
            l2s2 = TBuf(EC(nc.psum_tensor("l2s2", [1, st_w], F32)))
            bc2_ps = TBuf(EC(nc.psum_tensor("bc2_ps", [128, st_w], F32)))

            def emit_xs(sti):
                s0x = sti * st_w
                par = sti % 2
                for e in range(ne):
                    ya = y1a[e % 2]
                    P.emit("sp", lambda s, e=e, ya=ya, s0x=s0x: s.dma_start(
                        ya.h[:, :], y1_scr.ap()[e][:, s0x:s0x + st_w]),
                        writes=[ya], chan=f"y1a{e % 2}")
                    ip = is_ps[e % 2]
                    P.emit("pe", lambda t, e=e, ip=ip, s0x=s0x: t.matmul(
                        ip.h[:, :], ICS.h[:, e * 128:(e + 1) * 128],
                        routing_t.h[:, s0x:s0x + st_w], start=True, stop=True),
                        reads=[ICS, routing_t], writes=[ip])
                    P.emit("dve", lambda v, e=e, ya=ya, ip=ip, par=par:
                           v.tensor_mul(
                               xs[par][e].h[:, :], ya.h[:, :], ip.h[:, :]),
                           reads=[ya, ip], writes=[xs[par][e]])

            def emit_norm(stp, d, tail=False):
                s0p = stp * st_w
                eng = "dve" if (tail and d % 2 == 1) else "pool"
                tn = tn2[d % 2]
                ys = y2s[d % 4]
                if eng == "pool":
                    P.emit("pool", lambda g, d=d, tn=tn: g.tensor_mul(
                        tn.h[:, :], z2[d].h[:, :], Br.h[:, 0:st_w]),
                        reads=[z2[d], Br], writes=[tn])
                    P.emit("pool", lambda g, tn=tn, ys=ys: g.tensor_sub(
                        ys.h[:, :], tn.h[:, :], Bm.h[:, 0:st_w]),
                        reads=[tn, Bm], writes=[ys])
                else:
                    P.emit("dve", lambda v, d=d, tn=tn: v.tensor_mul(
                        tn.h[:, :], z2[d].h[:, :], Br.h[:, 0:st_w]),
                        reads=[z2[d], Br], writes=[tn])
                    P.emit("dve", lambda v, tn=tn, ys=ys: v.tensor_sub(
                        ys.h[:, :], tn.h[:, :], Bm.h[:, 0:st_w]),
                        reads=[tn, Bm], writes=[ys])
                P.emit("pool", lambda g, d=d, ys=ys, s0p=s0p: g.dma_start(
                    out.ap()[d * 128:(d + 1) * 128, s0p:s0p + st_w],
                    ys.h[:, :]),
                    reads=[ys], chan="out")

            emit_xs(0)

            def emit_stats(d, zd, sq):
                P.emit("pe", lambda t, zd=zd, d=d: t.matmul(
                    l2s1.h[0:1, :], ones_bf.h[:, :], zd.h[:, :],
                    start=(d == 0), stop=(d == ne - 1)),
                    reads=[ones_bf, zd], writes=[l2s1])
                P.emit("pe", lambda t, sq=sq, d=d: t.matmul(
                    l2s2.h[0:1, :], ones_bf.h[:, :], sq.h[:, :],
                    start=(d == 0), stop=(d == ne - 1)),
                    reads=[ones_bf, sq], writes=[l2s2])

            pending_epilogue = []
            for sti in range(nst):
                s0 = sti * st_w
                xcur = xs[sti % 2]
                pending_stats = []
                for d in range(ne):
                    wsb = WS[d % 3]
                    P.emit("sp", lambda s, d=d, wsb=wsb: s.dma_start(
                        wsb.h[:, :], ws_scr.ap()[d]),
                        writes=[wsb], chan=f"ws_ffn{d % 3}")
                    P.emit("pe", lambda t, d=d, s0=s0: t.matmul(
                        os_ps.h[:, :], OCS.h[:, d * 128:(d + 1) * 128],
                        routing_t.h[:, s0:s0 + st_w], start=True, stop=True),
                        reads=[OCS, routing_t], writes=[os_ps])
                    ob = os_sb[d % 2]
                    P.emit("act", lambda sc, ob=ob: sc.activation(
                        ob.h[:, :], os_ps.h[:, :], AF.Copy),
                        reads=[os_ps], writes=[ob])
                    op = out_ps[d % 2]
                    for e in range(ne):
                        P.emit("pe", lambda t, e=e, wsb=wsb, op=op, xcur=xcur:
                               t.matmul(
                                   op.h[:, :],
                                   wsb.h[:, e * 128:(e + 1) * 128],
                                   xcur[e].h[:, :],
                                   start=(e == 0), stop=(e == ne - 1)),
                               reads=[wsb, xcur[e]], writes=[op])
                    if pending_stats:
                        emit_stats(*pending_stats.pop())
                    yr = y1r[d % 2]
                    P.emit("sp", lambda s, d=d, yr=yr, s0=s0: s.dma_start(
                        yr.h[:, :], y1_scr.ap()[d][:, s0:s0 + st_w]),
                        writes=[yr], chan=f"y1r{d % 2}")
                    fb = fo[d % 2]
                    P.emit("act", lambda sc, fb=fb, op=op: sc.activation(
                        fb.h[:, :], op.h[:, :], AF.Copy),
                        reads=[op], writes=[fb])
                    tb = tf[d % 2]
                    P.emit("dve", lambda v, tb=tb, fb=fb, ob=ob: v.tensor_mul(
                        tb.h[:, :], fb.h[:, :], ob.h[:, :]),
                        reads=[fb, ob], writes=[tb])
                    if d == 0 and pending_epilogue:
                        pending_epilogue.pop()()
                    P.emit("dve", lambda v, d=d, tb=tb, yr=yr:
                           v.scalar_tensor_tensor(
                               z2[d].h[:, :], tb.h[:, :], BFN.h[:, d:d + 1],
                               yr.h[:, :], ALU.add, ALU.add),
                           reads=[tb, BFN, yr], writes=[z2[d]])
                    sq = sqc[d % 3]
                    P.emit("act", lambda sc, d=d, sq=sq: sc.activation(
                        sq.h[:, :], z2[d].h[:, :], AF.Square),
                        reads=[z2[d]], writes=[sq])
                    pending_stats.append((d, z2[d], sq))
                    if d == 8 and sti + 1 < nst:
                        emit_xs(sti + 1)
                if pending_stats:
                    emit_stats(*pending_stats.pop())

                def emit_epilogue(sti=sti):
                    rA2 = rowpool.h[0:1, 0:st_w]
                    rB2 = rowpool.h[0:1, seq:seq + st_w]
                    P.emit("dve", lambda v, a=rA2: v.tensor_scalar_mul(
                        a, l2s1.h[0:1, :], 1.0 / dm),
                        reads=[l2s1], writes=[rowpool])
                    P.emit("dve", lambda v, a=rA2, b=rB2: v.tensor_mul(b, a, a),
                           reads=[rowpool], writes=[rowpool])
                    P.emit("dve", lambda v, b=rB2: v.tensor_scalar_add(
                        b, b, -EPS),
                        reads=[rowpool], writes=[rowpool])
                    P.emit("dve", lambda v, b=rB2: v.scalar_tensor_tensor(
                        b, l2s2.h[0:1, :], 1.0 / dm, b, ALU.mult, ALU.subtract),
                        reads=[l2s2, rowpool], writes=[rowpool])
                    P.emit("act", lambda sc, b=rB2: sc.activation(b, b, AF.Ln),
                           reads=[rowpool], writes=[rowpool])
                    P.emit("act", lambda sc, b=rB2: sc.activation(
                        b, b, AF.Exp, scale=-0.5),
                        reads=[rowpool], writes=[rowpool])
                    P.emit("pe", lambda t, b=rB2: t.matmul(
                        bc2_ps.h[:, :], onesrow_f.h[:, :], b,
                        start=True, stop=True),
                        reads=[onesrow_f, rowpool], writes=[bc2_ps])
                    P.emit("act", lambda sc: sc.activation(
                        Br.h[:, 0:st_w], bc2_ps.h[:, :], AF.Copy),
                        reads=[bc2_ps], writes=[Br])
                    P.emit("dve", lambda v, a=rA2, b=rB2: v.tensor_mul(
                        a, a, b),
                        reads=[rowpool], writes=[rowpool])
                    P.emit("pe", lambda t, a=rA2: t.matmul(
                        bc2_ps.h[:, :], onesrow_f.h[:, :], a,
                        start=True, stop=True),
                        reads=[onesrow_f, rowpool], writes=[bc2_ps])
                    P.emit("act", lambda sc: sc.activation(
                        Bm.h[:, 0:st_w], bc2_ps.h[:, :], AF.Copy),
                        reads=[bc2_ps], writes=[Bm])
                    for d in range(ne):
                        emit_norm(sti, d, tail=(sti == nst - 1))

                if sti == nst - 1:
                    emit_epilogue()
                else:
                    pending_epilogue.append(emit_epilogue)

            P.barrier()
        P.replay()
    return nc


_CACHE = {}


def _get_program():
    if "nc" not in _CACHE:
        _CACHE["nc"] = build_program()
    return _CACHE["nc"]


def kernel(L0_attn_hidden_states, L0_attn_wq, L0_attn_bq, L0_attn_wk,
           L0_attn_bk, L0_attn_wv, L0_attn_bv, L0_attn_wl, L0_attn_bl,
           L0_ffn_weight, L0_ffn_bias, L0_ffn_gate_w, L0_ffn_in_channel_scale,
           L0_ffn_out_channel_scale):
    x = np.asarray(L0_attn_hidden_states, dtype=np.float32)

    def tile_w(w):
        return np.ascontiguousarray(
            w.T.reshape(NH, 128, NH, 128).transpose(2, 1, 0, 3)).astype(
                np.float32, copy=False)

    wqh = tile_w(np.asarray(L0_attn_wq, np.float32))
    wkh = tile_w(np.asarray(L0_attn_wk, np.float32))
    wvh = tile_w(np.asarray(L0_attn_wv, np.float32))
    wsh = tile_w(np.asarray(L0_ffn_weight, np.float32))
    gwh = np.ascontiguousarray(
        np.asarray(L0_ffn_gate_w, np.float32).T.reshape(NH, 128, 8)
        .transpose(1, 0, 2)).reshape(128, NH * 8)
    bqh = np.ascontiguousarray(np.asarray(L0_attn_bq, np.float32).reshape(NH, 128).T)
    bkh = np.ascontiguousarray(np.asarray(L0_attn_bk, np.float32).reshape(NH, 128).T)
    bvh = np.ascontiguousarray(np.asarray(L0_attn_bv, np.float32).reshape(NH, 128).T)
    bfh = np.ascontiguousarray(np.asarray(L0_ffn_bias, np.float32).reshape(NH, 128).T)
    icsa = np.ascontiguousarray(np.asarray(L0_ffn_in_channel_scale, np.float32))
    ocsa = np.ascontiguousarray(np.asarray(L0_ffn_out_channel_scale, np.float32))

    nc = _get_program()
    in_maps = []
    for c in range(N_CORES):
        in_maps.append({
            "xt": np.ascontiguousarray(x[c].T),
            "wqh": wqh, "wkh": wkh, "wvh": wvh, "wsh": wsh, "gwh": gwh,
            "bqh": bqh, "bkh": bkh, "bvh": bvh, "bfh": bfh,
            "ics": icsa, "ocs": ocsa,
        })
    res = run_bass_kernel_spmd(nc, in_maps, core_ids=list(range(N_CORES)))
    outs = [np.ascontiguousarray(r["out"].T) for r in res.results]
    return np.stack(outs, axis=0).astype(np.float32, copy=False)


# version 30 (final)
# speedup vs baseline: 1.1382x; 1.0132x over previous
import bisect
from contextlib import ExitStack

import numpy as np
import ml_dtypes

import concourse.bass as bass
import concourse.mybir as mybir
from concourse.bass_utils import run_bass_kernel_spmd

F32 = mybir.dt.float32
BF16 = mybir.dt.bfloat16
AF = mybir.ActivationFunctionType
ALU = mybir.AluOpType
AX = mybir.AxisListType

N_CORES = 8
SEQ = 2048
NH = 32
EPS = 1e-5
SCALE = float(1.0 / np.sqrt(np.float32(128.0)))

SAME_ENGINE_SYNC = True

_BLOCK_ATTR = {"pe": "tensor", "act": "scalar", "dve": "vector",
               "pool": "gpsimd", "sp": "sync"}


class TBuf:
    __slots__ = ("h", "w", "r", "name")

    def __init__(self, h, name=""):
        self.h = h
        self.name = name
        self.w = {}
        self.r = {}


class Chan:
    __slots__ = ("name", "sem", "count")

    def __init__(self, name, sem):
        self.name = name
        self.sem = sem
        self.count = 0


class Prog:
    ENGINES = ("pe", "act", "dve", "pool", "sp")

    def __init__(self, nc, ctx):
        self.nc = nc
        self.ops = {e: [] for e in self.ENGINES}
        self.n = {e: 0 for e in self.ENGINES}
        self.incs = {e: [] for e in self.ENGINES}
        self.inc_flags = {e: set() for e in self.ENGINES}
        self.wm = {e: {} for e in self.ENGINES}
        self.sems = {e: ctx.enter_context(nc.semaphore("s_" + e))
                     for e in self.ENGINES}
        self.chans = {}
        self._ctx = ctx

    def chan(self, name):
        if name not in self.chans:
            sem = self._ctx.enter_context(self.nc.semaphore("c_" + name))
            self.chans[name] = Chan(name, sem)
        return self.chans[name]

    def _inc_value(self, ename, pos):
        lst = self.incs[ename]
        idx = bisect.bisect_left(lst, pos)
        if idx < len(lst):
            return idx + 1
        assert pos <= self.n[ename], (ename, pos, self.n[ename])
        lst.append(pos)
        self.inc_flags[ename].add(pos)
        return len(lst)

    def _need(self, consumer, key, pos):
        if key[0] == "e":
            ename = key[1]
            if ename == consumer and (ename == "pe" or not SAME_ENGINE_SYNC):
                return
            val = self._inc_value(ename, pos)
        else:
            val = 16 * self.chans[key[1]].count
        if self.wm[consumer].get(key, 0) >= val:
            return
        self.ops[consumer].append(("w", key, val))
        self.wm[consumer][key] = val

    def emit(self, ename, build, reads=(), writes=(), chan=None):
        deps = {}
        for b in reads:
            for k, p in b.w.items():
                if deps.get(k, 0) < p:
                    deps[k] = p
        for b in writes:
            for k, p in b.w.items():
                if deps.get(k, 0) < p:
                    deps[k] = p
            for k, p in b.r.items():
                if deps.get(k, 0) < p:
                    deps[k] = p
        for k in sorted(deps, key=repr):
            self._need(ename, k, deps[k])
        if chan is None:
            self.n[ename] += 1
            pos = self.n[ename]
            self.ops[ename].append(("i", build, pos))
            tok_key, tok_pos = ("e", ename), pos
        else:
            c = self.chan(chan)
            c.count += 1
            pos = c.count
            self.ops[ename].append(("d", build, chan))
            tok_key, tok_pos = ("c", chan), pos
        for b in reads:
            if b.r.get(tok_key, 0) < tok_pos:
                b.r[tok_key] = tok_pos
        for b in writes:
            if b.w.get(tok_key, 0) < tok_pos:
                b.w[tok_key] = tok_pos
            b.r = {}
        return (tok_key, tok_pos)

    def barrier(self):
        snap = {("e", e): self.n[e] for e in self.ENGINES if self.n[e] > 0}
        for cname, ch in self.chans.items():
            if ch.count > 0:
                snap[("c", cname)] = ch.count
        for e in self.ENGINES:
            for k in sorted(snap, key=repr):
                if k == ("e", e):
                    continue
                self._need(e, k, snap[k])

    def replay(self):
        nc = self.nc
        with nc.Block() as block:
            for ename in self.ENGINES:
                ops = self.ops[ename]
                if not ops:
                    continue

                def runner(eng, ops=ops, ename=ename):
                    flags = self.inc_flags[ename]
                    sems = self.sems
                    chans = self.chans
                    for op in ops:
                        if op[0] == "w":
                            key = op[1]
                            sem = (sems[key[1]] if key[0] == "e"
                                   else chans[key[1]].sem)
                            eng.wait_ge(sem, op[2])
                        elif op[0] == "i":
                            inst = op[1](eng)
                            if op[2] in flags:
                                inst.then_inc(sems[ename], 1)
                        else:
                            inst = op[1](eng)
                            inst.then_inc(chans[op[2]].sem, 16)

                getattr(block, _BLOCK_ATTR[ename])(runner)


def build_program(seq=SEQ, nh=NH, debug=False):
    ne = nh
    dm = 128 * nh
    sl_w = min(512, seq)
    nsl = seq // sl_w
    st_w = min(512, seq)
    nst = seq // st_w
    SLB = min(256, seq)
    nslab = seq // SLB
    G = 4
    ngrp = SLB // G

    nc = bass.Bass("TRN2", target_bir_lowering=False)

    xt = nc.dram_tensor("xt", [dm, seq], F32, kind="ExternalInput")
    wqh = nc.dram_tensor("wqh", [nh, 128, ne, 128], F32, kind="ExternalInput")
    wkh = nc.dram_tensor("wkh", [nh, 128, ne, 128], F32, kind="ExternalInput")
    wvh = nc.dram_tensor("wvh", [nh, 128, ne, 128], F32, kind="ExternalInput")
    wsh = nc.dram_tensor("wsh", [ne, 128, ne, 128], F32, kind="ExternalInput")
    gwh = nc.dram_tensor("gwh", [128, ne * 8], F32, kind="ExternalInput")
    bqh = nc.dram_tensor("bqh", [128, nh], F32, kind="ExternalInput")
    bkh = nc.dram_tensor("bkh", [128, nh], F32, kind="ExternalInput")
    bvh = nc.dram_tensor("bvh", [128, nh], F32, kind="ExternalInput")
    bfh = nc.dram_tensor("bfh", [128, ne], F32, kind="ExternalInput")
    ics = nc.dram_tensor("ics", [8, dm], F32, kind="ExternalInput")
    ocs = nc.dram_tensor("ocs", [8, dm], F32, kind="ExternalInput")

    nslab_ = seq // min(256, seq)
    SLB_ = min(256, seq)
    q_scr = nc.dram_tensor("q_scr", [nslab_, nh, 128, SLB_], BF16, kind="Internal")
    k_scr = nc.dram_tensor("k_scr", [nslab_, nh, 128, SLB_], BF16, kind="Internal")
    v_scr = nc.dram_tensor("v_scr", [nslab_, nh, 128, SLB_], BF16, kind="Internal")
    c_scr = nc.dram_tensor("c_scr", [nh, 128, seq], BF16, kind="Internal")
    z_scr = nc.dram_tensor("z_scr", [ne, 128, seq], BF16, kind="Internal")
    y1_scr = nc.dram_tensor("y1_scr", [ne, 128, seq], BF16, kind="Internal")
    ws_scr = nc.dram_tensor("ws_scr", [ne, 128, dm], BF16, kind="Internal")

    out = nc.dram_tensor("out", [dm, seq], F32, kind="ExternalOutput")
    _p4 = np.kron(np.eye(4, dtype=np.float32),
                  np.ones((1, nh), dtype=np.float32)) * 32.0
    p4a_dram = nc.inline_tensor(_p4.astype(ml_dtypes.bfloat16), name="p4a_c")
    p4b_dram = nc.inline_tensor(
        np.concatenate([_p4, _p4], axis=1).astype(ml_dtypes.bfloat16),
        name="p4b_c")
    if debug:
        dbg_qkv = nc.dram_tensor("dbg_qkv", [3, nslab_, nh, 128, SLB_], BF16,
                                 kind="ExternalOutput")
        dbg_c = nc.dram_tensor("dbg_c", [nh, 128, seq], BF16,
                               kind="ExternalOutput")
        dbg_y1 = nc.dram_tensor("dbg_y1", [ne, 128, seq], BF16,
                                kind="ExternalOutput")
        dbg_rt = nc.dram_tensor("dbg_rt", [8, seq], BF16,
                                kind="ExternalOutput")

    ctx = ExitStack()
    with ctx:
        P = Prog(nc, ctx)
        E = ctx.enter_context

        ident_bf = TBuf(E(nc.sbuf_tensor("ident_bf", [128, 128], BF16)))
        ones_bf = TBuf(E(nc.sbuf_tensor("ones_bf", [128, 1], BF16)))
        onesrow_f = TBuf(E(nc.sbuf_tensor("onesrow_f", [1, 128], F32)))
        rowpool = TBuf(E(nc.sbuf_tensor("rowpool", [1, 2 * seq], F32)))
        BQ = TBuf(E(nc.sbuf_tensor("BQ", [128, nh], F32)))
        BK = TBuf(E(nc.sbuf_tensor("BK", [128, nh], F32)))
        BV = TBuf(E(nc.sbuf_tensor("BV", [128, nh], F32)))
        BFN = TBuf(E(nc.sbuf_tensor("BFN", [128, ne], F32)))
        GW = TBuf(E(nc.sbuf_tensor("GW", [128, ne * 8], BF16)))
        routing_t = TBuf(E(nc.sbuf_tensor("routing_t", [8, seq], BF16)))
        Bm = TBuf(E(nc.sbuf_tensor("Bm", [128, seq], BF16)))
        Br = TBuf(E(nc.sbuf_tensor("Br", [128, seq], BF16)))

        P.emit("pool", lambda g: g.memset(ident_bf.h[:, :], 0.0),
               writes=[ident_bf])
        P.emit("pool", lambda g: g.affine_select(
            out=ident_bf.h[:, :], in_=ident_bf.h[:, :],
            compare_op=ALU.not_equal, fill=1.0, base=0,
            pattern=[[-1, 128]], channel_multiplier=1),
            reads=[ident_bf], writes=[ident_bf])
        P.emit("pool", lambda g: g.memset(ones_bf.h[:, :], 1.0),
               writes=[ones_bf])
        P.emit("pool", lambda g: g.memset(onesrow_f.h[:, :], 1.0),
               writes=[onesrow_f])

        for mi, (buf, srct) in enumerate(
                ((BQ, bqh), (BK, bkh), (BV, bvh), (BFN, bfh))):
            P.emit("sp", lambda s, b=buf, t=srct: s.dma_start(b.h[:, :], t.ap()),
                   writes=[buf], chan=f"msc{mi}")
        P.emit("pool", lambda g: g.dma_start(GW.h[:, :], gwh.ap()),
               writes=[GW], chan="mgw")

        zfill_tb = TBuf(None, "zfill")
        ctxA0 = ExitStack()
        with ctxA0:
            EA = ctxA0.enter_context
            X = [TBuf(EA(nc.sbuf_tensor(f"X{c}", [128, seq], BF16)), f"X{c}")
                 for c in range(ne)]
            for c in range(ne):
                P.emit("pool", lambda g, c=c: g.dma_start(
                    X[c].h[:, :], xt.ap()[c * 128:(c + 1) * 128, :]),
                    writes=[X[c]], chan=f"x{c}")
            for c in range(ne):
                P.emit("sp", lambda s, c=c: s.dma_start(
                    z_scr.ap()[c], X[c].h[:, :]),
                    reads=[X[c]], writes=[zfill_tb], chan="zfill")

            wbufs = [TBuf(EA(nc.sbuf_tensor(f"W{i}", [128, dm], BF16)))
                     for i in range(2)]
            stage = [TBuf(EA(nc.sbuf_tensor(f"pst{i}", [128, seq], BF16)))
                     for i in range(3)]
            wsbufs = [TBuf(EA(nc.sbuf_tensor(f"wsbuf{i}", [128, dm // 2], BF16)))
                      for i in range(2)]
            wstmp = TBuf(EA(nc.sbuf_tensor("wstmp", [128, dm // 2], BF16)))
            proj_ps = [TBuf(EA(nc.psum_tensor(f"proj_ps{i}", [128, 512], F32)))
                       for i in range(2)]

            ws_tasks = [(h, hf) for h in range(ne) for hf in range(2)]

            def emit_ws_load(k):
                h, hf = ws_tasks[k]
                b = wsbufs[k % 2]
                P.emit("pool", lambda g, h=h, hf=hf, b=b: g.dma_start(
                    b.h[:, :],
                    wsh.ap()[h][:, hf * (ne // 2):(hf + 1) * (ne // 2), :]),
                    writes=[b], chan="ws_in")

            emit_ws_load(0)
            emit_ws_load(1)
            ws_k = 0

            wsrcs = (wqh, wkh, wvh)
            wdsts = (q_scr, k_scr, v_scr)
            wbias = (BQ, BK, BV)
            wload = [(h, wi) for h in range(nh) for wi in range(3)]

            def emit_wload(j):
                h, wi = wload[j]
                buf = wbufs[j % 2]
                if j == 0:
                    qw = dm // 4
                    qe = ne // 4
                    for i in range(4):
                        P.emit("pool", lambda g, h=h, wi=wi, b=buf, i=i,
                               qw=qw, qe=qe:
                            g.dma_start(
                                b.h[:, i * qw:(i + 1) * qw],
                                wsrcs[wi].ap()[h][:, i * qe:(i + 1) * qe, :]),
                            writes=[buf], chan=f"wl0s{i}")
                    return
                P.emit("pool", lambda g, h=h, wi=wi, b=buf: g.dma_start(
                    b.h[:, :], wsrcs[wi].ap()[h]),
                    writes=[buf], chan=f"wl{j % 2}")

            emit_wload(0)
            if len(wload) > 1:
                emit_wload(1)
            next_load = 2

            for j, (h, wi) in enumerate(wload):
                wb = wbufs[j % 2]
                st = stage[j % 3]
                for s4 in range(nsl):
                    pp = proj_ps[(j * nsl + s4) % 2]
                    for e in range(ne):
                        P.emit("pe", lambda t, wb=wb, e=e, s4=s4, pp=pp: t.matmul(
                            pp.h[:, 0:sl_w],
                            wb.h[:, e * 128:(e + 1) * 128],
                            X[e].h[:, s4 * sl_w:(s4 + 1) * sl_w],
                            start=(e == 0), stop=(e == ne - 1)),
                            reads=[wb, X[e]], writes=[pp])
                    P.emit("act", lambda sc, st=st, s4=s4, wi=wi, h=h, pp=pp:
                           sc.activation(
                               st.h[:, s4 * sl_w:(s4 + 1) * sl_w],
                               pp.h[:, 0:sl_w], AF.Identity,
                               bias=wbias[wi].h[:, h:h + 1]),
                           reads=[pp, wbias[wi]], writes=[st])
                P.emit("sp", lambda s, st=st, wi=wi, h=h: s.dma_start(
                    wdsts[wi].ap()[:, h, :, :].transpose([1, 0, 2]),
                    st.h[:, :].rearrange("d (sl s) -> d sl s", s=SLB)),
                    reads=[st], chan="qkv_out")
                if next_load < len(wload):
                    emit_wload(next_load)
                    next_load += 1
                if wi == 0 and h < ne:
                    for _ in range(2):
                        wh, whf = ws_tasks[ws_k]
                        b = wsbufs[ws_k % 2]
                        P.emit("dve", lambda v, b=b: v.tensor_single_scalar(
                            wstmp.h[:, :], b.h[:, :], 0.0, ALU.is_lt),
                            reads=[b], writes=[wstmp])
                        P.emit("dve", lambda v, b=b: v.scalar_tensor_tensor(
                            b.h[:, :], b.h[:, :], 0.0, wstmp.h[:, :],
                            ALU.is_gt, ALU.subtract),
                            reads=[b, wstmp], writes=[b])
                        P.emit("sp", lambda s, wh=wh, whf=whf, b=b: s.dma_start(
                            ws_scr.ap()[wh][:, whf * (dm // 2):(whf + 1) * (dm // 2)],
                            b.h[:, :]),
                            reads=[b], chan="ws_out")
                        if ws_k + 2 < len(ws_tasks):
                            emit_ws_load(ws_k + 2)
                        ws_k += 1
            if debug:
                P.barrier()
                for wi in range(3):
                    P.emit("sp", lambda s, wi=wi: s.dma_start(
                        dbg_qkv.ap()[wi], wdsts[wi].ap()), chan="dbg")

        P.barrier()
        ctxA1 = ExitStack()
        with ctxA1:
            EA = ctxA1.enter_context
            Qs = [TBuf(EA(nc.sbuf_tensor(f"Qs{i}", [128, ne * SLB], BF16)))
                  for i in range(2)]
            Ks = [TBuf(EA(nc.sbuf_tensor(f"Ks{i}", [128, ne * SLB], BF16)))
                  for i in range(2)]
            Vs = [TBuf(EA(nc.sbuf_tensor(f"Vs{i}", [128, ne * SLB], BF16)))
                  for i in range(2)]
            SRW = [TBuf(EA(nc.sbuf_tensor(f"SRW{p}", [128, ne * SLB], BF16)))
                   for p in range(2)]
            WIDX = {"q": 0, "k": 1, "v": 0}
            gdim = G * ne
            CTXF_ts = [EA(nc.sbuf_tensor(f"CTXF{i}", [128, ne * SLB], BF16))
                       for i in range(2)]
            CTXFbs = [[TBuf(CTXF_ts[i], f"ctxf{i}A"), TBuf(CTXF_ts[i], f"ctxf{i}B")]
                      for i in range(2)]
            U4 = [TBuf(EA(nc.sbuf_tensor(f"U4_{i}", [gdim, 2 * gdim], BF16)))
                  for i in range(3)]
            Vts4 = [TBuf(EA(nc.sbuf_tensor(f"Vts4_{i}", [gdim, 256], BF16)))
                    for i in range(3)]
            ctxn = [TBuf(EA(nc.sbuf_tensor(f"ctxn{i}", [gdim, 256], BF16)))
                    for i in range(3)]
            rden = [TBuf(EA(nc.sbuf_tensor(f"rden{i}", [gdim, 2], F32)))
                    for i in range(2)]
            negrow = TBuf(EA(nc.sbuf_tensor("negrow", [1, gdim], BF16)))
            ones2 = TBuf(EA(nc.sbuf_tensor("ones2", [1, 2 * gdim], BF16)))
            p4a = TBuf(EA(nc.sbuf_tensor("p4a", [4, gdim], BF16)))
            p4b = TBuf(EA(nc.sbuf_tensor("p4b", [4, 2 * gdim], BF16)))

            gram_ps = [TBuf(EA(nc.psum_tensor(f"gram{i}", [gdim, 2 * gdim], F32)))
                       for i in range(2)]
            vts_ps = TBuf(EA(nc.psum_tensor("vtsp0", [gdim, 256], BF16)))
            ctxT_ps = [TBuf(EA(nc.psum_tensor(f"ctxT{i}", [gdim, 256], F32)))
                       for i in range(2)]
            den_ps = TBuf(EA(nc.psum_tensor("denp", [gdim, 2], F32)))
            ctf_ps = [TBuf(EA(nc.psum_tensor(f"ctf{i}", [128, 2 * gdim], BF16)))
                      for i in range(2)]
            P.emit("pool", lambda g: g.memset(negrow.h[:, :], -1024.0),
                   writes=[negrow])
            P.emit("pool", lambda g: g.memset(ones2.h[:, :], 1.0),
                   writes=[ones2])
            P.emit("sp", lambda s: s.dma_start(p4a.h[:, :], p4a_dram.ap()),
                   writes=[p4a], chan="mmask")
            P.emit("sp", lambda s: s.dma_start(p4b.h[:, :], p4b_dram.ap()),
                   writes=[p4b], chan="mmask")
            def load_one(sl, which):
                buf = SRW[WIDX[which]]
                scr = {"q": q_scr, "k": k_scr, "v": v_scr}[which]
                P.emit("sp", lambda s, b=buf, scr=scr, sl=sl: s.dma_start(
                    b.h[:, :].rearrange("d (t s) -> d t s", s=SLB),
                    scr.ap()[sl].transpose([1, 0, 2])),
                    writes=[buf], chan=which + "sl" + str(sl % 2))

            def rearr_piece(sl, which, i, npc=4):
                stg = SRW[WIDX[which]]
                dst = {"q": Qs, "k": Ks, "v": Vs}[which][sl % 2]
                w = SLB // npc
                P.emit("dve", lambda v, b=dst, stg=stg, i=i, w=w: v.tensor_copy(
                    b.h[:, i * w * ne:(i + 1) * w * ne]
                    .rearrange("d (s t) -> d s t", t=ne),
                    stg.h[:, :].rearrange("d (t s) -> d s t", s=SLB)
                    [:, i * w:(i + 1) * w, :]),
                    reads=[stg], writes=[dst])

            def rearr_one(sl, which):
                for i in range(4):
                    rearr_piece(sl, which, i)

            for w in ("q", "k"):
                load_one(0, w)
            for w in ("q", "k"):
                rearr_one(0, w)
            load_one(0, "v")
            rearr_one(0, "v")
            for sl in range(nslab):
                s0 = sl * SLB
                qb_, kb_, vb_ = Qs[sl % 2], Ks[sl % 2], Vs[sl % 2]
                CTXF_t = CTXF_ts[sl % 2]
                CTXFb = CTXFbs[sl % 2]

                def gview(b, g):
                    return b.h[:, g * G * ne:(g + 1) * G * ne]

                def rview(b, g):
                    return (b.h[:, :].rearrange("d (t s) -> d s t", s=SLB)
                            [:, g * G:(g + 1) * G, :])

                nmac = ngrp // 2

                def stage1(m):
                    gp = gram_ps[m % 2]
                    P.emit("pe", lambda t, gp=gp: t.matmul(
                        gp.h[:, :], negrow.h[:, :], ones2.h[:, :],
                        start=True, stop=False),
                        reads=[negrow, ones2], writes=[gp])
                    P.emit("pe", lambda t, gp=gp: t.matmul(
                        gp.h[:, :], p4a.h[:, :], p4b.h[:, :],
                        start=False, stop=False),
                        reads=[p4a, p4b], writes=[gp])
                    for k in (0, 1):
                        g = 2 * m + k
                        P.emit("pe", lambda t, g=g, k=k, vb_=vb_: t.transpose(
                            vts_ps.h[:, k * 128:(k + 1) * 128],
                            gview(vb_, g), ident_bf.h[:, :]),
                            reads=[vb_, ident_bf], writes=[vts_ps])
                        P.emit("pe", lambda t, g=g, k=k, gp=gp, kb_=kb_,
                               qb_=qb_: t.matmul(
                                   gp.h[:, k * gdim:(k + 1) * gdim],
                                   gview(kb_, g), gview(qb_, g),
                                   start=False, stop=(k == 1)),
                               reads=[kb_, qb_], writes=[gp])
                    ub = U4[m % 3]
                    P.emit("act", lambda sc, ub=ub, gp=gp: sc.activation(
                        ub.h[:, :], gp.h[:, :], AF.Exp, scale=SCALE),
                        reads=[gp], writes=[ub])
                    vb4 = Vts4[m % 3]
                    P.emit("act", lambda sc, vb4=vb4: sc.activation(
                        vb4.h[:, :], vts_ps.h[:, :], AF.Copy),
                        reads=[vts_ps], writes=[vb4])

                def stage2(p):
                    um, vb4 = U4[p % 3], Vts4[p % 3]
                    ctp = ctxT_ps[p % 2]
                    for k in (0, 1):
                        P.emit("pe", lambda t, k=k, um=um, vb4=vb4, ctp=ctp:
                               t.matmul(
                                   ctp.h[:, k * 128:(k + 1) * 128],
                                   um.h[:, k * gdim:(k + 1) * gdim],
                                   vb4.h[:, k * 128:(k + 1) * 128],
                                   start=True, stop=True),
                               reads=[um, vb4], writes=[ctp])
                        P.emit("pe", lambda t, k=k, um=um: t.matmul(
                            den_ps.h[:, k:k + 1],
                            um.h[:, k * gdim:(k + 1) * gdim],
                            ones_bf.h[0:gdim, 0:1],
                            start=True, stop=True),
                            reads=[um, ones_bf], writes=[den_ps])
                    rd = rden[p % 2]
                    P.emit("dve", lambda v, rd=rd: v.reciprocal(
                        rd.h[:, :], den_ps.h[:, :]),
                        reads=[den_ps], writes=[rd])
                    cn_ = ctxn[p % 3]
                    for k in (0, 1):
                        P.emit("dve", lambda v, k=k, cn_=cn_, ctp=ctp, rd=rd:
                               v.tensor_scalar_mul(
                                   cn_.h[:, k * 128:(k + 1) * 128],
                                   ctp.h[:, k * 128:(k + 1) * 128],
                                   rd.h[:, k:k + 1]),
                               reads=[ctp, rd], writes=[cn_])

                def stage3(p):
                    cn_ = ctxn[p % 3]
                    cf = ctf_ps[p % 2]
                    for k in (0, 1):
                        P.emit("pe", lambda t, k=k, cn_=cn_, cf=cf: t.transpose(
                            cf.h[:, k * gdim:(k + 1) * gdim],
                            cn_.h[:, k * 128:(k + 1) * 128],
                            ident_bf.h[0:gdim, 0:gdim]),
                            reads=[cn_, ident_bf], writes=[cf])
                    ctb = CTXFb[p % 2]
                    P.emit("act", lambda sc, p=p, cf=cf, CTXF_t=CTXF_t:
                        sc.activation(
                        CTXF_t[:, :].rearrange("d (h s) -> d s h", s=SLB)
                        [:, p * 2 * G:(p + 1) * 2 * G, :],
                        cf.h[:, :].rearrange("d (s h) -> d s h", h=ne),
                        AF.Copy),
                        reads=[cf], writes=[ctb])

                for m in range(nmac + 4):
                    if m < nmac:
                        stage1(m)
                    if 2 <= m < nmac + 2:
                        stage2(m - 2)
                    if 4 <= m:
                        stage3(m - 4)
                    if sl + 1 < nslab:
                        if m == 0:
                            load_one(sl + 1, "q")
                            load_one(sl + 1, "k")
                        if 13 <= m < 17:
                            rearr_piece(sl + 1, "q", m - 13)
                        if m == 17:
                            load_one(sl + 1, "v")
                        if 17 <= m < 21:
                            rearr_piece(sl + 1, "k", m - 17)
                        if 29 <= m < 33:
                            rearr_piece(sl + 1, "v", m - 29)

                P.emit("pool", lambda g, s0=s0, CTXF_t=CTXF_t:
                    g.dma_start(
                        z_scr.ap()[:, :, s0:s0 + SLB].transpose([1, 0, 2]),
                        CTXF_t[:, :],
                        accum_op=ALU.add),
                    reads=[CTXFb[0], CTXFb[1], zfill_tb],
                    chan=f"c_out{sl % 2}")

        P.barrier()
        ctxB = ExitStack()
        with ctxB:
            EB = ctxB.enter_context
            Z = [TBuf(EB(nc.sbuf_tensor(f"Z{c}", [128, seq], BF16)))
                 for c in range(ne)]
            sqb = [TBuf(EB(nc.sbuf_tensor(f"sqb{i}", [128, seq], BF16)))
                   for i in range(2)]
            tnb = [TBuf(EB(nc.sbuf_tensor(f"tnb{i}", [128, seq], BF16)))
                   for i in range(2)]

            ctxB1 = ExitStack()
            with ctxB1:
                sum1 = TBuf(ctxB1.enter_context(
                    nc.psum_tensor("sum1", [1, seq], F32)))
                sum2 = TBuf(ctxB1.enter_context(
                    nc.psum_tensor("sum2", [1, seq], F32)))
                for c in range(ne):
                    P.emit("sp", lambda s, c=c: s.dma_start(
                        Z[c].h[:, :], z_scr.ap()[c]),
                        writes=[Z[c]], chan=f"zin{c % 4}")
                    sq = sqb[c % 2]
                    P.emit("act", lambda sc, c=c, sq=sq: sc.activation(
                        sq.h[:, :], Z[c].h[:, :], AF.Square),
                        reads=[Z[c]], writes=[sq])
                    for s4 in range(nsl):
                        P.emit("pe", lambda t, c=c, s4=s4: t.matmul(
                            sum1.h[0:1, s4 * sl_w:(s4 + 1) * sl_w],
                            ones_bf.h[:, :],
                            Z[c].h[:, s4 * sl_w:(s4 + 1) * sl_w],
                            start=(c == 0), stop=(c == ne - 1)),
                            reads=[ones_bf, Z[c]], writes=[sum1])
                        P.emit("pe", lambda t, sq=sq, s4=s4, c=c: t.matmul(
                            sum2.h[0:1, s4 * sl_w:(s4 + 1) * sl_w],
                            ones_bf.h[:, :],
                            sq.h[:, s4 * sl_w:(s4 + 1) * sl_w],
                            start=(c == 0), stop=(c == ne - 1)),
                            reads=[ones_bf, sq], writes=[sum2])
                rA = rowpool.h[0:1, 0:seq]
                rB = rowpool.h[0:1, seq:2 * seq]
                P.emit("dve", lambda v, a=rA: v.tensor_scalar_mul(
                    a, sum1.h[0:1, :], 1.0 / dm),
                    reads=[sum1], writes=[rowpool])
                P.emit("dve", lambda v, a=rA, b=rB: v.tensor_mul(b, a, a),
                       reads=[rowpool], writes=[rowpool])
                P.emit("dve", lambda v, b=rB: v.tensor_scalar_add(b, b, -EPS),
                       reads=[rowpool], writes=[rowpool])
                P.emit("dve", lambda v, b=rB: v.scalar_tensor_tensor(
                    b, sum2.h[0:1, :], 1.0 / dm, b, ALU.mult, ALU.subtract),
                    reads=[sum2, rowpool], writes=[rowpool])
                P.emit("act", lambda sc, b=rB: sc.activation(b, b, AF.Ln),
                       reads=[rowpool], writes=[rowpool])
                P.emit("act", lambda sc, b=rB: sc.activation(
                    b, b, AF.Exp, scale=-0.5),
                    reads=[rowpool], writes=[rowpool])
            P.barrier()

            ctxB2 = ExitStack()
            with ctxB2:
                ctxB2p = ExitStack()
                bcB_ps = TBuf(ctxB2p.enter_context(
                    nc.psum_tensor("bcB_ps", [128, seq], F32)))
                lg_ps = TBuf(ctxB2p.enter_context(
                    nc.psum_tensor("lg_ps", [8, seq], F32)))
                for s4 in range(nsl):
                    P.emit("pe", lambda t, s4=s4: t.matmul(
                        bcB_ps.h[:, s4 * sl_w:(s4 + 1) * sl_w],
                        onesrow_f.h[:, :],
                        rowpool.h[0:1, seq + s4 * sl_w:seq + (s4 + 1) * sl_w],
                        start=True, stop=True),
                        reads=[onesrow_f, rowpool], writes=[bcB_ps])
                P.emit("act", lambda sc: sc.activation(
                    Br.h[:, :], bcB_ps.h[:, :], AF.Copy),
                    reads=[bcB_ps], writes=[Br])
                P.emit("dve", lambda v: v.tensor_mul(
                    rowpool.h[0:1, 0:seq], rowpool.h[0:1, 0:seq],
                    rowpool.h[0:1, seq:2 * seq]),
                    reads=[rowpool], writes=[rowpool])
                for s4 in range(nsl):
                    P.emit("pe", lambda t, s4=s4: t.matmul(
                        bcB_ps.h[:, s4 * sl_w:(s4 + 1) * sl_w],
                        onesrow_f.h[:, :],
                        rowpool.h[0:1, s4 * sl_w:(s4 + 1) * sl_w],
                        start=True, stop=True),
                        reads=[onesrow_f, rowpool], writes=[bcB_ps])
                P.emit("act", lambda sc: sc.activation(
                    Bm.h[:, :], bcB_ps.h[:, :], AF.Copy),
                    reads=[bcB_ps], writes=[Bm])
                for c in range(ne):
                    tn = tnb[c % 2]
                    P.emit("dve", lambda v, c=c, tn=tn: v.tensor_mul(
                        tn.h[:, :], Z[c].h[:, :], Br.h[:, :]),
                        reads=[Z[c], Br], writes=[tn])
                    P.emit("dve", lambda v, c=c, tn=tn: v.tensor_sub(
                        Z[c].h[:, :], tn.h[:, :], Bm.h[:, :]),
                        reads=[tn, Bm], writes=[Z[c]])
                    P.emit("sp", lambda s, c=c: s.dma_start(
                        y1_scr.ap()[c], Z[c].h[:, :]),
                        reads=[Z[c]], chan="y1_out")
                    for s4 in range(nsl):
                        P.emit("pe", lambda t, c=c, s4=s4: t.matmul(
                            lg_ps.h[0:8, s4 * sl_w:(s4 + 1) * sl_w],
                            GW.h[:, c * 8:(c + 1) * 8],
                            Z[c].h[:, s4 * sl_w:(s4 + 1) * sl_w],
                            start=(c == 0), stop=(c == ne - 1)),
                            reads=[GW, Z[c]], writes=[lg_ps])
                lg_sb = TBuf(ctxB2.enter_context(
                    nc.sbuf_tensor("lg_sb", [8, seq], BF16)))
                P.emit("act", lambda sc: sc.activation(
                    lg_sb.h[:, :], lg_ps.h[:, :], AF.Copy),
                    reads=[lg_ps], writes=[lg_sb])
                nt = seq // 128
                ltm = TBuf(ctxB2.enter_context(
                    nc.sbuf_tensor("ltm", [128, nt * 8], BF16)))
                prob = TBuf(ctxB2.enter_context(
                    nc.sbuf_tensor("prob", [128, nt * 8], F32)))
                mx = TBuf(ctxB2.enter_context(
                    nc.sbuf_tensor("mx", [128, nt], F32)))
                den = TBuf(ctxB2.enter_context(
                    nc.sbuf_tensor("den", [128, nt], F32)))
                rtm = TBuf(ctxB2.enter_context(
                    nc.sbuf_tensor("rtm", [128, nt * 8], BF16)))
                P.barrier()
                ctxB2p.close()
                ctxB3 = ExitStack()
                with ctxB3:
                    lt_ps = TBuf(ctxB3.enter_context(
                        nc.psum_tensor("lt_ps", [128, nt * 8], BF16)))
                    rt_ps = TBuf(ctxB3.enter_context(
                        nc.psum_tensor("rt_ps", [8, seq], BF16)))
                    for tt in range(nt):
                        P.emit("pe", lambda t, tt=tt: t.transpose(
                            lt_ps.h[:, tt * 8:(tt + 1) * 8],
                            lg_sb.h[0:8, tt * 128:(tt + 1) * 128],
                            ident_bf.h[0:8, 0:8]),
                            reads=[lg_sb, ident_bf], writes=[lt_ps])
                    P.emit("dve", lambda v: v.tensor_copy(
                        ltm.h[:, :], lt_ps.h[:, :]),
                        reads=[lt_ps], writes=[ltm])
                    P.emit("dve", lambda v: v.reduce_max(
                        mx.h[:, :],
                        ltm.h[:, :].rearrange("p (g x) -> p g x", x=8),
                        axis=AX.X),
                        reads=[ltm], writes=[mx])
                    P.emit("dve", lambda v: v.tensor_scalar_mul(
                        mx.h[:, :], mx.h[:, :], -1.0),
                        reads=[mx], writes=[mx])
                    for tt in range(nt):
                        P.emit("act", lambda sc, tt=tt: sc.activation(
                            prob.h[:, tt * 8:(tt + 1) * 8],
                            ltm.h[:, tt * 8:(tt + 1) * 8], AF.Exp,
                            bias=mx.h[:, tt:tt + 1],
                            accum_out=den.h[:, tt:tt + 1]),
                            reads=[ltm, mx], writes=[prob, den])
                    P.emit("dve", lambda v: v.reciprocal(
                        den.h[:, :], den.h[:, :]),
                        reads=[den], writes=[den])
                    for tt in range(nt):
                        P.emit("dve", lambda v, tt=tt: v.tensor_scalar_mul(
                            rtm.h[:, tt * 8:(tt + 1) * 8],
                            prob.h[:, tt * 8:(tt + 1) * 8],
                            den.h[:, tt:tt + 1]),
                            reads=[prob, den], writes=[rtm])
                    for tt in range(nt):
                        P.emit("pe", lambda t, tt=tt: t.transpose(
                            rt_ps.h[0:8, tt * 128:(tt + 1) * 128],
                            rtm.h[:, tt * 8:(tt + 1) * 8],
                            ident_bf.h[:, :]),
                            reads=[rtm, ident_bf], writes=[rt_ps])
                    P.emit("dve", lambda v: v.tensor_copy(
                        routing_t.h[:, :], rt_ps.h[:, :]),
                        reads=[rt_ps], writes=[routing_t])
                    if debug:
                        P.emit("sp", lambda s: s.dma_start(
                            dbg_rt.ap(), routing_t.h[:, :]),
                            reads=[routing_t], chan="dbg")
                        P.emit("sp", lambda s: s.dma_start(
                            dbg_y1.ap(), y1_scr.ap()), chan="dbg")
                        P.emit("sp", lambda s: s.dma_start(
                            dbg_c.ap(), c_scr.ap()), chan="dbg")
                    P.barrier()

        ctxC = ExitStack()
        with ctxC:
            EC = ctxC.enter_context
            ICS = TBuf(EC(nc.sbuf_tensor("ICS", [8, dm], BF16)))
            OCS = TBuf(EC(nc.sbuf_tensor("OCS", [8, dm], BF16)))
            P.emit("pool", lambda g: g.dma_start(ICS.h[:, :], ics.ap()),
                   writes=[ICS], chan="mics")
            P.emit("pool", lambda g: g.dma_start(OCS.h[:, :], ocs.ap()),
                   writes=[OCS], chan="mocs")
            xs = [[TBuf(EC(nc.sbuf_tensor(f"xs{p}_{e}", [128, st_w], BF16)))
                   for e in range(ne)] for p in range(2)]
            z2 = [TBuf(EC(nc.sbuf_tensor(f"z2{d}", [128, st_w], BF16)))
                  for d in range(ne)]
            WS = [TBuf(EC(nc.sbuf_tensor(f"WS{i}", [128, dm], BF16)))
                  for i in range(3)]
            y1a = [TBuf(EC(nc.sbuf_tensor(f"y1a{i}", [128, st_w], BF16)))
                   for i in range(4)]
            y1r = [TBuf(EC(nc.sbuf_tensor(f"y1r{i}", [128, st_w], BF16)))
                   for i in range(2)]
            os_sb = [TBuf(EC(nc.sbuf_tensor(f"os{i}", [128, st_w], BF16)))
                     for i in range(2)]
            tf = [TBuf(EC(nc.sbuf_tensor(f"tf{i}", [128, st_w], BF16)))
                  for i in range(2)]
            fo = [TBuf(EC(nc.sbuf_tensor(f"fo{i}", [128, st_w], BF16)))
                  for i in range(2)]
            sqc = [TBuf(EC(nc.sbuf_tensor(f"sqc{i}", [128, st_w], BF16)))
                   for i in range(3)]
            tn2 = [TBuf(EC(nc.sbuf_tensor(f"tn2{i}", [128, st_w], BF16)))
                   for i in range(2)]
            y2s = [TBuf(EC(nc.sbuf_tensor(f"y2s{i}", [128, st_w], F32)))
                   for i in range(4)]

            out_ps = [TBuf(EC(nc.psum_tensor(f"out_ps{i}", [128, st_w], F32)))
                      for i in range(2)]
            is_ps = [TBuf(EC(nc.psum_tensor(f"is_ps{i}", [128, st_w], F32)))
                     for i in range(2)]
            os_ps = TBuf(EC(nc.psum_tensor("os_ps", [128, st_w], F32)))
            l2s1 = TBuf(EC(nc.psum_tensor("l2s1", [1, st_w], F32)))
            l2s2 = TBuf(EC(nc.psum_tensor("l2s2", [1, st_w], F32)))
            bc2_ps = TBuf(EC(nc.psum_tensor("bc2_ps", [128, st_w], F32)))

            def emit_xs(sti):
                s0x = sti * st_w
                par = sti % 2
                for e in range(ne):
                    ya = y1a[e % 4]
                    P.emit("sp", lambda s, e=e, ya=ya, s0x=s0x: s.dma_start(
                        ya.h[:, :], y1_scr.ap()[e][:, s0x:s0x + st_w]),
                        writes=[ya], chan=f"y1a{e % 4}")
                    ip = is_ps[e % 2]
                    P.emit("pe", lambda t, e=e, ip=ip, s0x=s0x: t.matmul(
                        ip.h[:, :], ICS.h[:, e * 128:(e + 1) * 128],
                        routing_t.h[:, s0x:s0x + st_w], start=True, stop=True),
                        reads=[ICS, routing_t], writes=[ip])
                    P.emit("dve", lambda v, e=e, ya=ya, ip=ip, par=par:
                           v.tensor_mul(
                               xs[par][e].h[:, :], ya.h[:, :], ip.h[:, :]),
                           reads=[ya, ip], writes=[xs[par][e]])

            def emit_norm(stp, d, tail=False):
                s0p = stp * st_w
                eng = "dve" if (tail and d % 2 == 1) else "pool"
                tn = tn2[d % 2]
                ys = y2s[d % 4]
                if eng == "pool":
                    P.emit("pool", lambda g, d=d, tn=tn: g.tensor_mul(
                        tn.h[:, :], z2[d].h[:, :], Br.h[:, 0:st_w]),
                        reads=[z2[d], Br], writes=[tn])
                    P.emit("pool", lambda g, tn=tn, ys=ys: g.tensor_sub(
                        ys.h[:, :], tn.h[:, :], Bm.h[:, 0:st_w]),
                        reads=[tn, Bm], writes=[ys])
                else:
                    P.emit("dve", lambda v, d=d, tn=tn: v.tensor_mul(
                        tn.h[:, :], z2[d].h[:, :], Br.h[:, 0:st_w]),
                        reads=[z2[d], Br], writes=[tn])
                    P.emit("dve", lambda v, tn=tn, ys=ys: v.tensor_sub(
                        ys.h[:, :], tn.h[:, :], Bm.h[:, 0:st_w]),
                        reads=[tn, Bm], writes=[ys])
                P.emit("pool", lambda g, d=d, ys=ys, s0p=s0p: g.dma_start(
                    out.ap()[d * 128:(d + 1) * 128, s0p:s0p + st_w],
                    ys.h[:, :]),
                    reads=[ys], chan="out")

            emit_xs(0)

            def emit_stats(d, zd, sq):
                P.emit("pe", lambda t, zd=zd, d=d: t.matmul(
                    l2s1.h[0:1, :], ones_bf.h[:, :], zd.h[:, :],
                    start=(d == 0), stop=(d == ne - 1)),
                    reads=[ones_bf, zd], writes=[l2s1])
                P.emit("pe", lambda t, sq=sq, d=d: t.matmul(
                    l2s2.h[0:1, :], ones_bf.h[:, :], sq.h[:, :],
                    start=(d == 0), stop=(d == ne - 1)),
                    reads=[ones_bf, sq], writes=[l2s2])

            pending_epilogue = []
            for sti in range(nst):
                s0 = sti * st_w
                xcur = xs[sti % 2]
                pending_stats = []
                for d in range(ne):
                    wsb = WS[d % 3]
                    P.emit("sp", lambda s, d=d, wsb=wsb: s.dma_start(
                        wsb.h[:, :], ws_scr.ap()[d]),
                        writes=[wsb], chan=f"ws_ffn{d % 3}")
                    P.emit("pe", lambda t, d=d, s0=s0: t.matmul(
                        os_ps.h[:, :], OCS.h[:, d * 128:(d + 1) * 128],
                        routing_t.h[:, s0:s0 + st_w], start=True, stop=True),
                        reads=[OCS, routing_t], writes=[os_ps])
                    ob = os_sb[d % 2]
                    P.emit("act", lambda sc, ob=ob: sc.activation(
                        ob.h[:, :], os_ps.h[:, :], AF.Copy),
                        reads=[os_ps], writes=[ob])
                    op = out_ps[d % 2]
                    for e in range(ne):
                        P.emit("pe", lambda t, e=e, wsb=wsb, op=op, xcur=xcur:
                               t.matmul(
                                   op.h[:, :],
                                   wsb.h[:, e * 128:(e + 1) * 128],
                                   xcur[e].h[:, :],
                                   start=(e == 0), stop=(e == ne - 1)),
                               reads=[wsb, xcur[e]], writes=[op])
                    if pending_stats:
                        emit_stats(*pending_stats.pop())
                    yr = y1r[d % 2]
                    P.emit("sp", lambda s, d=d, yr=yr, s0=s0: s.dma_start(
                        yr.h[:, :], y1_scr.ap()[d][:, s0:s0 + st_w]),
                        writes=[yr], chan=f"y1r{d % 2}")
                    fb = fo[d % 2]
                    P.emit("act", lambda sc, fb=fb, op=op: sc.activation(
                        fb.h[:, :], op.h[:, :], AF.Copy),
                        reads=[op], writes=[fb])
                    tb = tf[d % 2]
                    P.emit("dve", lambda v, tb=tb, fb=fb, ob=ob: v.tensor_mul(
                        tb.h[:, :], fb.h[:, :], ob.h[:, :]),
                        reads=[fb, ob], writes=[tb])
                    if d == 0 and pending_epilogue:
                        pending_epilogue.pop()()
                    P.emit("dve", lambda v, d=d, tb=tb, yr=yr:
                           v.scalar_tensor_tensor(
                               z2[d].h[:, :], tb.h[:, :], BFN.h[:, d:d + 1],
                               yr.h[:, :], ALU.add, ALU.add),
                           reads=[tb, BFN, yr], writes=[z2[d]])
                    sq = sqc[d % 3]
                    P.emit("act", lambda sc, d=d, sq=sq: sc.activation(
                        sq.h[:, :], z2[d].h[:, :], AF.Square),
                        reads=[z2[d]], writes=[sq])
                    pending_stats.append((d, z2[d], sq))
                    if d == 8 and sti + 1 < nst:
                        emit_xs(sti + 1)
                if pending_stats:
                    emit_stats(*pending_stats.pop())

                def emit_epilogue(sti=sti):
                    rA2 = rowpool.h[0:1, 0:st_w]
                    rB2 = rowpool.h[0:1, seq:seq + st_w]
                    P.emit("dve", lambda v, a=rA2: v.tensor_scalar_mul(
                        a, l2s1.h[0:1, :], 1.0 / dm),
                        reads=[l2s1], writes=[rowpool])
                    P.emit("dve", lambda v, a=rA2, b=rB2: v.tensor_mul(b, a, a),
                           reads=[rowpool], writes=[rowpool])
                    P.emit("dve", lambda v, b=rB2: v.tensor_scalar_add(
                        b, b, -EPS),
                        reads=[rowpool], writes=[rowpool])
                    P.emit("dve", lambda v, b=rB2: v.scalar_tensor_tensor(
                        b, l2s2.h[0:1, :], 1.0 / dm, b, ALU.mult, ALU.subtract),
                        reads=[l2s2, rowpool], writes=[rowpool])
                    P.emit("act", lambda sc, b=rB2: sc.activation(b, b, AF.Ln),
                           reads=[rowpool], writes=[rowpool])
                    P.emit("act", lambda sc, b=rB2: sc.activation(
                        b, b, AF.Exp, scale=-0.5),
                        reads=[rowpool], writes=[rowpool])
                    P.emit("pe", lambda t, b=rB2: t.matmul(
                        bc2_ps.h[:, :], onesrow_f.h[:, :], b,
                        start=True, stop=True),
                        reads=[onesrow_f, rowpool], writes=[bc2_ps])
                    P.emit("act", lambda sc: sc.activation(
                        Br.h[:, 0:st_w], bc2_ps.h[:, :], AF.Copy),
                        reads=[bc2_ps], writes=[Br])
                    P.emit("dve", lambda v, a=rA2, b=rB2: v.tensor_mul(
                        a, a, b),
                        reads=[rowpool], writes=[rowpool])
                    P.emit("pe", lambda t, a=rA2: t.matmul(
                        bc2_ps.h[:, :], onesrow_f.h[:, :], a,
                        start=True, stop=True),
                        reads=[onesrow_f, rowpool], writes=[bc2_ps])
                    P.emit("act", lambda sc: sc.activation(
                        Bm.h[:, 0:st_w], bc2_ps.h[:, :], AF.Copy),
                        reads=[bc2_ps], writes=[Bm])
                    for d in range(ne):
                        emit_norm(sti, d, tail=(sti == nst - 1))

                if sti == nst - 1:
                    emit_epilogue()
                else:
                    pending_epilogue.append(emit_epilogue)

            P.barrier()
        P.replay()
    return nc


_CACHE = {}


def _get_program():
    if "nc" not in _CACHE:
        _CACHE["nc"] = build_program()
    return _CACHE["nc"]


def kernel(L0_attn_hidden_states, L0_attn_wq, L0_attn_bq, L0_attn_wk,
           L0_attn_bk, L0_attn_wv, L0_attn_bv, L0_attn_wl, L0_attn_bl,
           L0_ffn_weight, L0_ffn_bias, L0_ffn_gate_w, L0_ffn_in_channel_scale,
           L0_ffn_out_channel_scale):
    x = np.asarray(L0_attn_hidden_states, dtype=np.float32)

    def tile_w(w):
        return np.ascontiguousarray(
            w.T.reshape(NH, 128, NH, 128).transpose(2, 1, 0, 3)).astype(
                np.float32, copy=False)

    wqh = tile_w(np.asarray(L0_attn_wq, np.float32))
    wkh = tile_w(np.asarray(L0_attn_wk, np.float32))
    wvh = tile_w(np.asarray(L0_attn_wv, np.float32))
    wsh = tile_w(np.asarray(L0_ffn_weight, np.float32))
    gwh = np.ascontiguousarray(
        np.asarray(L0_ffn_gate_w, np.float32).T.reshape(NH, 128, 8)
        .transpose(1, 0, 2)).reshape(128, NH * 8)
    bqh = np.ascontiguousarray(np.asarray(L0_attn_bq, np.float32).reshape(NH, 128).T)
    bkh = np.ascontiguousarray(np.asarray(L0_attn_bk, np.float32).reshape(NH, 128).T)
    bvh = np.ascontiguousarray(np.asarray(L0_attn_bv, np.float32).reshape(NH, 128).T)
    bfh = np.ascontiguousarray(np.asarray(L0_ffn_bias, np.float32).reshape(NH, 128).T)
    icsa = np.ascontiguousarray(np.asarray(L0_ffn_in_channel_scale, np.float32))
    ocsa = np.ascontiguousarray(np.asarray(L0_ffn_out_channel_scale, np.float32))

    nc = _get_program()
    in_maps = []
    for c in range(N_CORES):
        in_maps.append({
            "xt": np.ascontiguousarray(x[c].T),
            "wqh": wqh, "wkh": wkh, "wvh": wvh, "wsh": wsh, "gwh": gwh,
            "bqh": bqh, "bkh": bkh, "bvh": bvh, "bfh": bfh,
            "ics": icsa, "ocs": ocsa,
        })
    res = run_bass_kernel_spmd(nc, in_maps, core_ids=list(range(N_CORES)))
    outs = [np.ascontiguousarray(r["out"].T) for r in res.results]
    return np.stack(outs, axis=0).astype(np.float32, copy=False)
